# version 01
# baseline (speedup 1.0000x reference)
import numpy as np
import ml_dtypes
from contextlib import ExitStack

import concourse.bass as bass
import concourse.mybir as mybir
from concourse.bass_utils import run_bass_kernel_spmd
from concourse.ap import AP as RawAP

F32 = mybir.dt.float32
BF16 = mybir.dt.bfloat16
I32 = mybir.dt.int32
AF = mybir.ActivationFunctionType
ALU = mybir.AluOpType
AX = mybir.AxisListType
ds = bass.ds

NCORE = 8
S, D, T = 8192, 2048, 1024
NH, HD = 16, 128
FF = 8192
NJ = 16
NPAIR = 64
NSLOT = 17
C_THR = 0.5009765625
SM_SCALE = float(HD) ** -0.5
RMS_EPS = float(np.float32(np.finfo(np.float32).eps))
LN_EPS = 1e-5
BF = ml_dtypes.bfloat16

STAGE = "full"
DEBUG = False


def core_rows(c):
    return list(range(512 * c, 512 * c + 512)) + list(range(7680 - 512 * c, 8192 - 512 * c))


def slot_table(c):
    def sup(s):
        return (s, 0) if s <= 7 else (15 - s, 512)
    slots = [(0, c), (1, 15 - c)]
    slots += [(0, s) for s in range(c)]
    slots += [(1, s) for s in range(15 - c)]
    assert len(slots) == NSLOT
    rows = []
    for qb, s in slots:
        r, o = sup(s)
        koff = r * 2048 * 1024 + o
        voff = r * 1024 * 2048 + o * 2048
        rows.append([koff, voff, 512 * qb, qb, 0, 0, 0, 0])
    wt = np.zeros((128, 2 * NSLOT), np.float32)
    for t, (qb, s) in enumerate(slots):
        wt[:, qb * NSLOT + t] = 1.0
    return np.asarray(rows, np.int32).reshape(1, -1), wt


def host_constants():
    cbn = np.zeros((128, 640), dtype=BF)
    cbn[:, 0:128] = np.eye(128, dtype=BF)
    cbn[:, 128] = 1.0
    cbn[:, 129:257] = 1.0
    cfn = np.zeros((128, 160), np.float32)
    cfn[:, 0] = SM_SCALE
    cfn[:, 16:144] = 1.0
    q = np.arange(512)[None, :]
    k = np.arange(128)[:, None]
    cmn = np.zeros((128, 4 * 512), dtype=BF)
    for j in range(4):
        cmn[:, j * 512:(j + 1) * 512] = (q >= 128 * j + k).astype(BF)
    return cbn, cfn, cmn


class Sem:
    def __init__(self, es, nc, name):
        self.h = es.enter_context(nc.semaphore(name))
        self.name = name
        self.val = 0


class Prog:
    ENGINES = ("sync", "gpsimd", "vector", "scalar", "tensor")

    def __init__(self, nc, es):
        self.nc = nc
        self.es = es
        self.ops = {e: [] for e in self.ENGINES}
        self.sems = {}
        self.label = "init"

    def sem(self, name):
        if name not in self.sems:
            self.sems[name] = Sem(self.es, self.nc, name)
        return self.sems[name]

    def op(self, eng, fn=None, wait=(), inc=(), ghost_inc=()):
        wait = [(s, v) for (s, v) in wait if v > 0]
        inc2 = []
        for (s, n) in inc:
            s.val += n
            inc2.append((s, n, True))
        for (s, n) in ghost_inc:
            s.val += n
            inc2.append((s, n, False))
        assert fn is not None or not inc2
        self.ops[eng].append((fn, wait, inc2, self.label))

    def dma(self, eng, out, in_, sem, wait=(), **kw):
        self.op(eng, lambda e, out=out, in_=in_, kw=kw: e.dma_start(out=out, in_=in_, **kw),
                wait=wait, inc=[(sem, 16)])
        return sem.val

    def barrier_vals(self):
        return [(s, s.val) for s in self.sems.values()]

    def hard_barrier(self):
        bv = self.barrier_vals()
        for e in self.ENGINES:
            self.op(e, None, wait=bv)

    def check_schedule(self):
        idx = {e: 0 for e in self.ENGINES}
        vals = {s.name: 0 for s in self.sems.values()}
        total = sum(len(v) for v in self.ops.values())
        done = 0
        while done < total:
            progressed = False
            for e in self.ENGINES:
                while idx[e] < len(self.ops[e]):
                    fn, wait, incs, _lb = self.ops[e][idx[e]]
                    if any(vals[s.name] < v for (s, v) in wait):
                        break
                    for (s, n, _att) in incs:
                        vals[s.name] += n
                    idx[e] += 1
                    done += 1
                    progressed = True
            if not progressed:
                detail = []
                for e in self.ENGINES:
                    if idx[e] < len(self.ops[e]):
                        _, wait, _, lb = self.ops[e][idx[e]]
                        w = [(s.name, v, vals[s.name]) for (s, v) in wait
                             if vals[s.name] < v]
                        detail.append(f"{e}[op {idx[e]}/{len(self.ops[e])} @{lb}] blocked on {w}")
                        for (s, v) in wait:
                            if vals[s.name] < v:
                                acc = dict(vals)
                                for e2 in self.ENGINES:
                                    run = vals[s.name]
                                    for j in range(idx[e2], len(self.ops[e2])):
                                        _, _, incs2, lb2 = self.ops[e2][j]
                                        for (s2, n2) in incs2:
                                            if s2.name == s.name:
                                                run += n2
                                        if run >= v:
                                            detail.append(
                                                f"   {s.name}->{v} would come from "
                                                f"{e2}[{j} @{lb2}]")
                                            break
                                    else:
                                        continue
                                    break
                raise RuntimeError("DEADLOCK:\n" + "\n".join(detail))

    def replay(self, block):
        def mk(engname):
            def emit(eng):
                for fn, wait, incs, _lb in self.ops[engname]:
                    for s, v in wait:
                        eng.wait_ge(s.h, v)
                    if fn is not None:
                        ins = fn(eng)
                        for s, n, att in incs:
                            if att:
                                ins = ins.then_inc(s.h, n)
            return emit
        block.sync(mk("sync"))
        block.gpsimd(mk("gpsimd"))
        block.vector(mk("vector"))
        block.scalar(mk("scalar"))
        block.tensor(mk("tensor"))


def psum_bank(es, nc, name, shape, dtype):
    esz = 2 if dtype == BF16 else 4
    per = shape[-1] * esz
    assert per <= 2048
    pad_last = 2048 // esz
    full = list(shape[:-1]) + [pad_last]
    if len(shape) == 2 and shape[0] < 128:
        full[0] = 128
    t = es.enter_context(nc.psum_tensor(name, full, dtype))
    ap = t[:]
    sl = tuple([slice(0, shape[i]) for i in range(len(shape) - 1)]
               + [slice(0, shape[-1])])
    return ap[sl]


def build_program():
    nc = bass.Bass("TRN2", target_bir_lowering=False, detect_race_conditions=False)
    g = lambda n, s, d, k="ExternalInput": nc.dram_tensor(n, s, d, kind=k).ap()

    xin = g("xin", [T, D], F32)
    wgu_s = g("wgu_s", [2048, 2048], F32)
    wp_s = g("wp_s", [2048, 1024], F32)
    wa_s = g("wa_s", [4, 256, 2048], F32)
    bqk_tm_i = g("bqk_tm_i", [128, 32], F32)
    bv_bc_i = g("bv_bc_i", [128, D], F32)
    bo_bc_i = g("bo_bc_i", [128, D], F32)
    cbi = g("cbi", [128, 640], BF16)
    cfi = g("cfi", [128, 160], F32)
    cmi = g("cmi", [128, 4 * 512], BF16)
    tabi = g("tabi", [1, NSLOT * 8], I32)
    wti = g("wti", [128, 2 * NSLOT], F32)
    wtvi = g("wtvi", [NSLOT, 2], F32)
    out = g("out", [T, D], F32, "ExternalOutput")
    if DEBUG:
        dbg_yfm = g("dbg_yfm", [128, 16 * 512], BF16, "ExternalOutput")
        dbg_hfm = g("dbg_hfm", [128, 64 * 512], BF16, "ExternalOutput")
        dbg_stat = g("dbg_stat", [128, 64], F32, "ExternalOutput")
        dbg_hinv = g("dbg_hinv", [128, 4], F32, "ExternalOutput")
        dbg_negm = g("dbg_negm", [128, NH], F32, "ExternalOutput")
        dbg_mrow = g("dbg_mrow", [1, 512], F32, "ExternalOutput")
        dbg_su = g("dbg_su", [NSLOT, NH, 512], F32, "ExternalOutput")
        dbg_av = g("dbg_av", [2, 128, NH * 512], F32, "ExternalOutput")
        dbg_qn = g("dbg_qn", [1, 64], F32, "ExternalOutput")
        dbg_pm = g("dbg_pm", [4, 128, 512], BF16, "ExternalOutput")
        dbg_ks = g("dbg_ks", [128, 16 * 512], BF16, "ExternalOutput")
        dbg_vs = g("dbg_vs", [128, 4 * 2048], BF16, "ExternalOutput")
        dbg_qs = g("dbg_qs", [128, 16 * 512], BF16, "ExternalOutput")
        dbg_avt = g("dbg_avt", [128, 1024], F32, "ExternalOutput")
        dbg_sps = g("dbg_sps", [128, 512], F32, "ExternalOutput")
        dbg_sut = g("dbg_sut", [1, 1024], F32, "ExternalOutput")
        dbg_q = g("dbg_q", [2048, 1024], BF16, "ExternalOutput")
        dbg_y2 = g("dbg_y2", [128, 16 * 1024], BF16, "ExternalOutput")
        dbg_qst = g("dbg_qst", [128, 64], F32, "ExternalOutput")
        dbg_yb = g("dbg_yb", [128, 2048], BF16, "ExternalOutput")
        dbg_xh = g("dbg_xh", [128, 2048], F32, "ExternalOutput")
        dbg_wts = g("dbg_wts", [128, 2 * NSLOT], F32, "ExternalOutput")
        dbg_tab = g("dbg_tab", [1, NSLOT * 8], I32, "ExternalOutput")
        dbg_avb = g("dbg_avb", [128, NH * 512], BF16, "ExternalOutput")
        dbg_srow = g("dbg_srow", [1, NH * 512], F32, "ExternalOutput")
        dbg_k = g("dbg_k", [2, 2048, 1024], BF16, "ExternalOutput")
        dbg_v = g("dbg_v", [8, 128, 2048], BF16, "ExternalOutput")

    idr = lambda n, s, d: nc.dram_tensor(n, s, d).ap()
    sdr = lambda n, s, d: nc.dram_tensor(n, s, d, addr_space="Shared").ap()
    wgu_in = idr("wgu_in", [2048, 2048], BF16)
    wp_in = idr("wp_in", [1024, 2048], BF16)
    wa_in = idr("wa_in", [4, 2048, 256], BF16)
    wgu_all = sdr("wgu_all", [8, 2048, 2048], BF16)
    wp_all = sdr("wp_all", [8, 1024, 2048], BF16)
    wa_all = sdr("wa_all", [8, 4, 2048, 256], BF16)
    k_in = idr("k_in", [2048, 1024], BF16)
    v_in = idr("v_in", [1024, 2048], BF16)
    kn_in = idr("kn_in", [1, 16], F32)
    k_all = sdr("k_all", [8, 2048, 1024], BF16)
    v_all = sdr("v_all", [8, 1024, 2048], BF16)
    kn_all = sdr("kn_all", [8, 1, 16], F32)
    xdram = idr("xdram", [T, D], F32)
    qdram = idr("qdram", [2048, 1024], BF16)
    sudram = idr("sudram", [NSLOT, NH, 512], F32)
    podram = idr("podram", [T, D], BF16)

    es = ExitStack()
    with es:
        P = Prog(nc, es)
        sb = lambda n, s, d: es.enter_context(nc.sbuf_tensor(n, s, d))[:]

        LD = P.sem("LD")
        WG = P.sem("WG")
        XS = P.sem("XS")
        PP = P.sem("PP")
        CC = P.sem("CC")
        AC = P.sem("AC")
        KV = P.sem("KV")
        PE = P.sem("PE")
        DV = P.sem("DV")
        SC = P.sem("SC")
        FE = P.sem("FE")

        cbs = sb("cbs", [128, 640], BF16)
        cfs = sb("cfs", [128, 160], F32)
        tabs = sb("tabs", [1, NSLOT * 8], I32)
        wts = sb("wts", [128, 2 * NSLOT], F32)
        wtv = sb("wtv", [NSLOT, 2], F32)
        bqk_tm = sb("bqk_tm", [128, 32], F32)
        stat = sb("stat", [128, 64], F32)
        fsc = sb("fsc", [128, 16], F32)
        fsb = sb("fsb", [128, 16], BF16)

        def dram_fence(eng, dram_ap, sem):
            p, f = dram_ap.shape[0], dram_ap.shape[1]
            scr = fsb if dram_ap.tensor.dtype == BF16 else fsc
            P.dma(eng, scr[0:p, 0:f], dram_ap, sem)
            return sem.val
        row = sb("row", [1, 2048], F32)
        negm = sb("negm", [128, NH], F32)

        nc.const_aps.register(F32, SM_SCALE, cfs[:, 0:1])
        ident = cbs[:, 0:128]
        ones_col_b = cbs[:, 128:129]
        ones_row_f = cfs[0:1, 16:144]

        for src, dst in [(cbi, cbs), (cfi, cfs), (tabi, tabs),
                         (bqk_tm_i, bqk_tm), (wti, wts), (wtvi, wtv)]:
            P.dma("sync", dst, src, LD)
        ldv = LD.val

        P.label = "preproc"
        with ExitStack() as pes:
            psb = lambda n, s, d: pes.enter_context(nc.sbuf_tensor(n, s, d))[:]
            pps = lambda n, s, d: psum_bank(pes, nc, n, s, d)
            wfp = [psb(f"ppwfp{i}", [128, 2048], F32) for i in range(2)]
            wbb = [psb(f"ppwbb{i}", [128, 2048], BF16) for i in range(2)]
            wab = psb("ppwab", [128, 2048], BF16)
            wsg = psb("ppwsg", [128, 2048], BF16)
            gsc = psb("ppgsc", [128, 32], F32)
            scb = psb("ppscb", [128, 32], BF16)
            thr = psb("ppthr", [128, 32], F32)
            msk = psb("ppmsk", [128, 2048], BF16)
            wht = psb("ppwht", [128, 2048], BF16)
            dd_ = psb("ppdd", [128, 2048], BF16)
            fwd = [psb(f"ppfwd{i}", [128, 2048], BF16) for i in range(2)]
            asmb = [psb(f"ppasm{i}", [128, 2048], BF16) for i in range(2)]
            tp_ps = [pps(f"pptps{i}", [128, 128], BF16) for i in range(2)]

            wfp_free = [0, 0]
            fwd_free = [0, 0]
            asm_free = [0, 0]
            tp_free = [0, 0]

            def quant_tile(par, C, quant, wg_ld):
                G = C // 64
                P.op("vector", lambda e, p=par, C=C: e.tensor_copy(
                    out=wbb[p][:, :C], in_=wfp[p][:, :C]),
                    wait=[(WG, wg_ld)], inc=[(FE, 1)])
                wfp_free[par] = FE.val
                e0 = FE.val
                if not quant:
                    P.op("vector", lambda e, p=par, C=C: e.tensor_copy(
                        out=fwd[p][:, :C], in_=wbb[p][:, :C]),
                        wait=[(FE, e0), (PE, fwd_free[par])], inc=[(FE, 1)])
                    return
                P.op("scalar", lambda e, p=par, C=C: e.activation(
                    out=wab[:, :C], in_=wbb[p][:, :C], func=AF.Abs),
                    wait=[(FE, e0), (FE, FE.val)])
                P.op("scalar", lambda e, p=par, C=C: e.sign(
                    out=wsg[:, :C], in_=wbb[p][:, :C]), inc=[(SC, 1)])
                scv = SC.val
                P.op("vector", lambda e, p=par, C=C, G=G: e.tensor_reduce(
                    out=gsc[:, :G], in_=wbb[p][:, :C].rearrange("p (g k) -> p g k", k=64),
                    axis=AX.X, op=ALU.add, apply_absolute_value=True),
                    wait=[(FE, e0)], inc=[(FE, 1)])
                P.op("vector", lambda e, G=G: e.tensor_scalar(
                    out=scb[:, :G], in0=gsc[:, :G], scalar1=1.0 / 64, scalar2=1e-8,
                    op0=ALU.mult, op1=ALU.max), wait=[(FE, FE.val)], inc=[(FE, 1)])
                P.op("vector", lambda e, G=G: e.tensor_scalar(
                    out=thr[:, :G], in0=scb[:, :G], scalar1=C_THR, scalar2=None,
                    op0=ALU.mult), wait=[(FE, FE.val)], inc=[(FE, 1)])
                P.op("vector", lambda e, C=C, G=G: e.tensor_tensor(
                    out=msk[:, :C].rearrange("p (g k) -> p g k", k=64),
                    in0=wab[:, :C].rearrange("p (g k) -> p g k", k=64),
                    in1=thr[:, :G].unsqueeze(2).broadcast_to([128, G, 64]),
                    op=ALU.is_gt), wait=[(FE, FE.val), (SC, scv)], inc=[(FE, 1)])
                P.op("vector", lambda e, C=C: e.tensor_tensor(
                    out=wht[:, :C], in0=msk[:, :C], in1=wsg[:, :C], op=ALU.mult),
                    wait=[(FE, FE.val)], inc=[(FE, 1)])
                P.op("vector", lambda e, C=C, G=G: e.tensor_tensor(
                    out=wht[:, :C].rearrange("p (g k) -> p g k", k=64),
                    in0=wht[:, :C].rearrange("p (g k) -> p g k", k=64),
                    in1=scb[:, :G].unsqueeze(2).broadcast_to([128, G, 64]),
                    op=ALU.mult), wait=[(FE, FE.val)], inc=[(FE, 1)])
                P.op("vector", lambda e, p=par, C=C: e.tensor_tensor(
                    out=dd_[:, :C], in0=wht[:, :C], in1=wbb[p][:, :C], op=ALU.subtract),
                    wait=[(FE, FE.val)], inc=[(FE, 1)])
                P.op("vector", lambda e, p=par, C=C: e.tensor_tensor(
                    out=fwd[p][:, :C], in0=wbb[p][:, :C], in1=dd_[:, :C], op=ALU.add),
                    wait=[(FE, FE.val), (PE, fwd_free[par])], inc=[(FE, 1)])

            def transpose_out(par, C, dst):
                fv = FE.val
                for jc in range(C // 128):
                    tp = jc % 2
                    P.op("tensor", lambda e, p=par, jc=jc, tp=tp: e.transpose(
                        tp_ps[tp][:], fwd[p][:, jc * 128:(jc + 1) * 128], ident),
                        wait=[(FE, fv), (LD, ldv), (DV, tp_free[tp]),
                              (PP, asm_free[par])],
                        inc=[(PE, 1)])
                    pev = PE.val
                    P.op("vector", lambda e, p=par, jc=jc, tp=tp: e.tensor_copy(
                        out=asmb[p][:, jc * 128:(jc + 1) * 128], in_=tp_ps[tp][:]),
                        wait=[(PE, pev)], inc=[(DV, 1)])
                    tp_free[tp] = DV.val
                fwd_free[par] = PE.val
                P.dma("gpsimd", dst, asmb[par][:, :C], PP, wait=[(DV, DV.val)])
                asm_free[par] = PP.val

            pp_mark = {}
            ti = 0
            for i in range(16):
                par = ti % 2
                P.dma("sync", wfp[par], wgu_s[i * 128:(i + 1) * 128, :], WG,
                      wait=[(FE, wfp_free[par])])
                quant_tile(par, 2048, True, WG.val)
                col = i * 128 if i < 8 else 1024 + (i - 8) * 128
                transpose_out(par, 2048,
                              wgu_in[:, col:col + 128].rearrange("(jc p) h -> p jc h", p=128))
                ti += 1
            pp_mark["gu"] = PP.val
            for i in range(16):
                par = ti % 2
                P.dma("sync", wfp[par][:, 0:1024], wp_s[i * 128:(i + 1) * 128, :], WG,
                      wait=[(FE, wfp_free[par])])
                quant_tile(par, 1024, True, WG.val)
                transpose_out(par, 1024,
                              wp_in[:, i * 128:(i + 1) * 128].rearrange(
                                  "(hc p) d2 -> p hc d2", p=128))
                ti += 1
            pp_mark["wp"] = PP.val
            for w in range(4):
                for i in range(2):
                    par = ti % 2
                    P.dma("sync", wfp[par], wa_s[w, i * 128:(i + 1) * 128, :], WG,
                          wait=[(FE, wfp_free[par])])
                    quant_tile(par, 2048, False, WG.val)
                    transpose_out(par, 2048,
                                  wa_in[w, :, i * 128:(i + 1) * 128].rearrange(
                                      "(jc p) i2 -> p jc i2", p=128))
                    ti += 1
            pp_mark["wa"] = PP.val

            def ag(inp, outp, ppv):
                P.op("gpsimd", lambda e, i=inp, o=outp: e.collective_compute(
                    "AllGather", ALU.bypass, replica_groups=[list(range(8))],
                    ins=[i.opt()], outs=[o.opt()]),
                    wait=[(PP, ppv)], inc=[(CC, 1)])
                return CC.val
            cc_gu = ag(wgu_in, wgu_all, pp_mark["gu"])
            ag(wp_in, wp_all, pp_mark["wp"])
            cc_w = ag(wa_in.rearrange("w a b -> (w a) b"),
                      wa_all.rearrange("r w a b -> r (w a) b"), pp_mark["wa"])

        yb_free = [0]

        def ln_half(xh, ydst, sqs, yb, yps, yps_free, extra=()):
            xs_x = XS.val
            for t in range(4):
                P.op("vector", lambda e, xh=xh, t=t: e.tensor_reduce(
                    out=stat[:, t:t + 1], in_=xh[:, t * 2048:(t + 1) * 2048],
                    axis=AX.X, op=ALU.add), wait=[(XS, xs_x), (FE, FE.val)],
                    inc=[(FE, 1)])
                P.op("scalar", lambda e, xh=xh, sqs=sqs, t=t: e.activation(
                    out=sqs, in_=xh[:, t * 2048:(t + 1) * 2048], func=AF.Square,
                    accum_out=stat[:, 4 + t:5 + t]),
                    wait=[(XS, xs_x)] + list(extra), inc=[(SC, 1)])
            sc0 = SC.val

            def v(fn, wait=()):
                P.op("vector", fn, wait=list(wait) + [(FE, FE.val)], inc=[(FE, 1)])
            v(lambda e: e.tensor_scalar(out=stat[:, 8:12], in0=stat[:, 0:4],
                                        scalar1=1.0 / D, scalar2=None, op0=ALU.mult),
              wait=[(SC, sc0)])
            v(lambda e: e.tensor_scalar(out=stat[:, 12:16], in0=stat[:, 4:8],
                                        scalar1=1.0 / D, scalar2=None, op0=ALU.mult))
            v(lambda e: e.tensor_tensor(out=stat[:, 16:20], in0=stat[:, 8:12],
                                        in1=stat[:, 8:12], op=ALU.mult))
            v(lambda e: e.tensor_tensor(out=stat[:, 20:24], in0=stat[:, 12:16],
                                        in1=stat[:, 16:20], op=ALU.subtract))
            v(lambda e: e.tensor_scalar(out=stat[:, 24:28], in0=stat[:, 20:24],
                                        scalar1=LN_EPS, scalar2=None, op0=ALU.add))
            v(lambda e: e.reciprocal(out=stat[:, 28:32], in_=stat[:, 24:28]))
            P.op("scalar", lambda e: e.activation(out=stat[:, 32:36], in_=stat[:, 28:32],
                                                  func=AF.Sqrt),
                 wait=[(FE, FE.val)], inc=[(FE, 1)])
            v(lambda e: e.tensor_tensor(out=stat[:, 36:40], in0=stat[:, 32:36],
                                        in1=stat[:, 32:36], op=ALU.mult))
            v(lambda e: e.tensor_tensor(out=stat[:, 40:44], in0=stat[:, 36:40],
                                        in1=stat[:, 24:28], op=ALU.mult))
            v(lambda e: e.tensor_scalar(out=stat[:, 44:48], in0=stat[:, 40:44],
                                        scalar1=-0.5, scalar2=1.5,
                                        op0=ALU.mult, op1=ALU.add))
            v(lambda e: e.tensor_tensor(out=stat[:, 48:52], in0=stat[:, 32:36],
                                        in1=stat[:, 44:48], op=ALU.mult))
            v(lambda e: e.scalar_tensor_tensor(out=stat[:, 52:56], in0=stat[:, 8:12],
                                               scalar=-1.0, in1=stat[:, 48:52],
                                               op0=ALU.mult, op1=ALU.mult))
            fe_ln = FE.val
            for t in range(4):
                P.op("scalar", lambda e, xh=xh, yb=yb, t=t: e.activation(
                    out=yb, in_=xh[:, t * 2048:(t + 1) * 2048], func=AF.Identity,
                    bias=stat[:, 52 + t:53 + t], scale=stat[:, 48 + t:49 + t]),
                    wait=[(FE, fe_ln), (PE, yb_free[0])], inc=[(SC, 1)])
                sc_y = SC.val
                for jc in range(16):
                    tp = jc % 2
                    P.op("tensor", lambda e, yb=yb, yps=yps, jc=jc, tp=tp: e.transpose(
                        yps[tp][:], yb[:, jc * 128:(jc + 1) * 128], ident),
                        wait=[(SC, sc_y), (LD, ldv), (DV, yps_free[tp])],
                        inc=[(PE, 1)])
                    pev = PE.val
                    P.op("vector", lambda e, yps=yps, t=t, jc=jc, tp=tp,
                         ydst=ydst: e.tensor_copy(out=ydst(t, jc), in_=yps[tp][:]),
                        wait=[(PE, pev)], inc=[(DV, 1)])
                    yps_free[tp] = DV.val
                yb_free[0] = PE.val
            return DV.val

        def ffn_pass(tag, src_dram, dst_dram, proj_dram=None):
            P.label = tag
            with ExitStack() as fes:
                fsb = lambda n, s, d: fes.enter_context(nc.sbuf_tensor(tag + n, s, d))[:]
                xh = fsb("xh", [128, 4 * 2048], F32)
                x1h = fsb("x1h", [128, 4 * 2048], F32)
                yb = fsb("yb", [128, 2048], BF16)
                yfm = fsb("yfm", [128, 16 * 512], BF16)
                hfm = fsb("hfm", [128, 64 * 512], BF16)
                sg = [fsb(f"sg{i}", [128, 512], BF16) for i in range(2)]
                h2 = [fsb(f"h2{i}", [128, 512], BF16) for i in range(2)]
                wgu = [fsb(f"wgu{i}", [128, 16 * 256], BF16) for i in range(2)]
                wpb = [fsb(f"wpb{i}", [128, 8 * 512], BF16) for i in range(2)]
                sumr = fsb("sumr", [1, 512], F32)
                invr = fsb("invr", [1, 512], F32)
                hinv = fsb("hinv", [128, 4], F32)
                if proj_dram is not None:
                    pb2 = fsb("pb2", [128, 2048], BF16)

                for half in range(2):
                    hx = f"{half}"
                    P.label = f"{tag}h{half}"
                    bar = P.barrier_vals()
                    for t in range(4):
                        P.dma("scalar", xh[:, t * 2048:(t + 1) * 2048],
                              src_dram[(half * 4 + t) * 128:(half * 4 + t + 1) * 128, :],
                              XS, wait=bar if t == 0 else [])
                    extra = []
                    if proj_dram is not None:
                        fe_pr = FE.val
                        for t in range(4):
                            P.dma("scalar", pb2,
                                  proj_dram[(half * 4 + t) * 128:
                                            (half * 4 + t + 1) * 128, :],
                                  XS, wait=[(FE, fe_pr)])
                            xs_p = XS.val
                            P.op("vector", lambda e, t=t: e.tensor_tensor(
                                out=xh[:, t * 2048:(t + 1) * 2048],
                                in0=xh[:, t * 2048:(t + 1) * 2048], in1=pb2,
                                op=ALU.add),
                                wait=[(XS, xs_p), (FE, FE.val)], inc=[(FE, 1)])
                            fe_pr = FE.val
                        extra = [(FE, fe_pr)]
                    with ExitStack() as lnes:
                        yps = [psum_bank(lnes, nc, f"{tag}yps{hx}{i}",
                                         [128, 128], BF16) for i in range(2)]
                        yps_free = [0, 0]
                        dv_y = ln_half(xh, lambda t, jc: yfm[:, jc * 512 + t * 128:
                                                            jc * 512 + (t + 1) * 128],
                                       x1h[:, 0:2048], yb, yps, yps_free,
                                       extra=extra)
                    with ExitStack() as gues:
                        gps = [psum_bank(gues, nc, f"{tag}gps{hx}{i}",
                                         [128, 512], F32) for i in range(2)]
                        ups = [psum_bank(gues, nc, f"{tag}ups{hx}{i}",
                                         [128, 512], F32) for i in range(2)]
                        rps = psum_bank(gues, nc, f"{tag}rps{hx}", [1, 512], F32)
                        gps_free = [0, 0]
                        ups_free = [0, 0]
                        wgu_free = [0, 0]
                        for p in range(NPAIR):
                            par = p % 2
                            blk, loc = p // 8, (p % 8) * 128
                            P.dma("sync", wgu[par].rearrange(
                                "p (jc x) -> p jc x", x=256)[:, :, 0:128],
                                wgu_all[blk, :, loc:loc + 128].rearrange(
                                    "(jc p) h -> p jc h", p=128), WG,
                                wait=[(CC, cc_gu), (PE, wgu_free[par])])
                            P.dma("sync", wgu[par].rearrange(
                                "p (jc x) -> p jc x", x=256)[:, :, 128:256],
                                wgu_all[blk, :, 1024 + loc:1024 + loc + 128].rearrange(
                                    "(jc p) h -> p jc h", p=128), WG)
                            wg_p = WG.val
                            for jc in range(NJ):
                                P.op("tensor", lambda e, par=par, jc=jc: e.matmul(
                                    gps[par], wgu[par][:, jc * 256:jc * 256 + 128],
                                    yfm[:, jc * 512:(jc + 1) * 512],
                                    start=(jc == 0), stop=(jc == NJ - 1),
                                    skip_group_check=True),
                                    wait=([(WG, wg_p), (DV, dv_y), (SC, gps_free[par])]
                                          if jc == 0 else []))
                            for jc in range(NJ):
                                P.op("tensor", lambda e, par=par, jc=jc: e.matmul(
                                    ups[par], wgu[par][:, jc * 256 + 128:jc * 256 + 256],
                                    yfm[:, jc * 512:(jc + 1) * 512],
                                    start=(jc == 0), stop=(jc == NJ - 1),
                                    skip_group_check=True),
                                    wait=([(DV, ups_free[par])] if jc == 0 else []),
                                    inc=[(PE, 1)] if jc == NJ - 1 else [])
                            wgu_free[par] = PE.val
                            pe_p = PE.val
                            P.op("scalar", lambda e, par=par: e.activation(
                                out=sg[par], in_=gps[par], func=AF.Silu),
                                wait=[(PE, pe_p)], inc=[(SC, 1)])
                            gps_free[par] = SC.val
                            sc_sg = SC.val
                            P.op("vector", lambda e, par=par, p=p: e.tensor_tensor(
                                out=hfm[:, p * 512:(p + 1) * 512], in0=sg[par],
                                in1=ups[par], op=ALU.mult),
                                wait=[(SC, sc_sg)], inc=[(DV, 1)])
                            ups_free[par] = DV.val
                            dv_h = DV.val
                            P.op("scalar", lambda e, par=par, p=p: e.activation(
                                out=h2[par], in_=hfm[:, p * 512:(p + 1) * 512],
                                func=AF.Square),
                                wait=[(DV, dv_h), (PE, PE.val)], inc=[(SC, 1)])
                            sc_h2 = SC.val
                            P.op("tensor", lambda e, par=par, p=p: e.matmul(
                                rps, ones_col_b, h2[par], start=(p == 0),
                                stop=(p == NPAIR - 1), skip_group_check=True),
                                wait=[(SC, sc_h2)],
                                inc=[(PE, 1)])
                        dv_hall = DV.val
                        pe_rs = PE.val
                        def vr(fn, wait=()):
                            P.op("vector", fn, wait=list(wait) + [(FE, FE.val)],
                                 inc=[(FE, 1)])
                        vr(lambda e: e.tensor_copy(out=sumr, in_=rps),
                           wait=[(PE, pe_rs)])
                    vr2 = lambda fn, wait=(): P.op(
                        "vector", fn, wait=list(wait) + [(FE, FE.val)], inc=[(FE, 1)])
                    A, B, C, Dv = (row[0:1, 0:512], row[0:1, 512:1024],
                                   row[0:1, 1024:1536], row[0:1, 1536:2048])
                    vr2(lambda e: e.tensor_scalar(out=A, in0=sumr,
                                                  scalar1=1.0 / FF, scalar2=RMS_EPS,
                                                  op0=ALU.mult, op1=ALU.add))
                    vr2(lambda e: e.reciprocal(out=B, in_=A))
                    P.op("scalar", lambda e: e.activation(
                        out=C, in_=B, func=AF.Sqrt),
                        wait=[(FE, FE.val)], inc=[(FE, 1)])
                    vr2(lambda e: e.tensor_tensor(out=B, in0=C, in1=C, op=ALU.mult))
                    vr2(lambda e: e.tensor_tensor(out=Dv, in0=B, in1=A, op=ALU.mult))
                    vr2(lambda e: e.tensor_scalar(out=B, in0=Dv,
                                                  scalar1=-0.5, scalar2=1.5,
                                                  op0=ALU.mult, op1=ALU.add))
                    vr2(lambda e: e.tensor_tensor(out=invr, in0=C, in1=B, op=ALU.mult))
                    fe_inv = FE.val
                    with ExitStack() as pjes:
                        pj = [psum_bank(pjes, nc, f"{tag}pj{hx}{i}",
                                        [128, 512], F32) for i in range(4)]
                        tps2 = psum_bank(pjes, nc, f"{tag}tps2{hx}", [128, 4], F32)
                        for t in range(4):
                            P.op("tensor", lambda e, t=t: e.matmul(
                                tps2[:, t:t + 1], invr[0:1, t * 128:(t + 1) * 128],
                                ones_row_f[0:1, 0:1], start=(t == 0), stop=(t == 3),
                                skip_group_check=True),
                                wait=[(FE, fe_inv)] if t == 0 else [],
                                inc=[(PE, 1)] if t == 3 else [])
                        pe_ti = PE.val
                        P.op("vector", lambda e: e.tensor_scalar(
                            out=hinv, in0=tps2, scalar1=0.5, scalar2=None,
                            op0=ALU.mult),
                            wait=[(PE, pe_ti), (FE, FE.val)], inc=[(FE, 1)])
                        fe_hinv = FE.val
                        wpb_free = [0, 0]
                        pj_free = [0, 0, 0, 0]
                        for dt in range(4):
                            for sci in range(8):
                                par = (dt * 8 + sci) % 2
                                P.dma("sync", wpb[par],
                                      wp_all[sci, :, dt * 512:(dt + 1) * 512].rearrange(
                                          "(hc p) d2 -> p hc d2", p=128), WG,
                                      wait=[(CC, cc_w), (PE, wpb_free[par])])
                                wg_pj = WG.val
                                for t in range(4):
                                    for jc in range(8):
                                        pp = sci * 8 + jc
                                        P.op("tensor", lambda e, t=t, par=par, jc=jc,
                                             pp=pp, sci=sci: e.matmul(
                                                 pj[t], hfm[:, pp * 512 + t * 128:
                                                            pp * 512 + (t + 1) * 128],
                                                 wpb[par][:, jc * 512:(jc + 1) * 512],
                                                 start=(sci == 0 and jc == 0),
                                                 stop=(sci == 7 and jc == 7),
                                                 skip_group_check=True),
                                            wait=(([(WG, wg_pj), (DV, dv_hall)]
                                                   if t == 0 else [])
                                                  + [(DV, pj_free[t])]
                                                  if jc == 0 and sci == 0 else
                                                  ([(WG, wg_pj)] if jc == 0 and
                                                   t == 0 else [])),
                                            inc=([(PE, 1)] if (t == 3 and jc == 7)
                                                 else []))
                                wpb_free[par] = PE.val
                            pe_dt = PE.val
                            for t in range(4):
                                P.op("vector", lambda e, dt=dt, t=t:
                                     e.scalar_tensor_tensor(
                                         out=x1h[:, t * 2048 + dt * 512:
                                                 t * 2048 + (dt + 1) * 512],
                                         in0=pj[t], scalar=hinv[:, t:t + 1],
                                         in1=xh[:, t * 2048 + dt * 512:
                                                t * 2048 + (dt + 1) * 512],
                                         op0=ALU.mult, op1=ALU.add),
                                     wait=[(PE, pe_dt), (FE, fe_hinv)], inc=[(DV, 1)])
                                pj_free[t] = DV.val
                    dv_x1 = DV.val
                    for t in range(4):
                        P.dma("gpsimd", dst_dram[(half * 4 + t) * 128:
                                                 (half * 4 + t + 1) * 128, :],
                              x1h[:, t * 2048:(t + 1) * 2048], AC,
                              wait=[(DV, dv_x1)])
                    dram_fence("gpsimd", dst_dram[half * 512:half * 512 + 128, 0:8],
                               AC)
                    if DEBUG and tag == "f1" and half == 0:
                        P.dma("gpsimd", dbg_yfm, yfm, AC, wait=[(DV, dv_x1)])
                        P.dma("gpsimd", dbg_hfm, hfm, AC)
                        P.dma("gpsimd", dbg_stat, stat, AC)
                        P.dma("gpsimd", dbg_hinv, hinv, AC)

        ffn_pass("f1", xin, xdram)

        if STAGE in ("x2", "full"):
            P.label = "qkv"
            with ExitStack() as qes:
                qsb = lambda n, s, d: qes.enter_context(nc.sbuf_tensor(n, s, d))[:]
                qps = lambda n, s, d: psum_bank(qes, nc, n, s, d)
                y2fm = [qsb(f"y2fm{i}", [128, 16 * 512], BF16) for i in range(2)]
                wat = qsb("wat", [128, 16 * 2048], BF16)
                k2t = qsb("k2t", [128, 512], BF16)
                knr = qsb("knr", [1, 64], F32)
                qnr = qsb("qnr", [1, 64], F32)
                bv_bc = qsb("bv_bc", [128, D], F32)
                mrow = qsb("mrow", [1, 512], F32)
                P.dma("sync", bv_bc, bv_bc_i, LD)
                kvps = [qps(f"kvps{i}", [128, 512], F32) for i in range(2)]
                nps = qps("nps", [1, 512], F32)
                mps = qps("mps", [128, NH], F32)
                yps = [qps(f"qyps{i}", [128, 128], BF16) for i in range(2)]
                yps_free = [0, 0]

                lnes2 = ExitStack()
                with lnes2:
                    lsb = lambda n, s, d: lnes2.enter_context(
                        nc.sbuf_tensor(n, s, d))[:]
                    xh = lsb("qxh", [128, 4 * 2048], F32)
                    yb = lsb("qyb", [128, 2048], BF16)
                    qsqs = lsb("qsqs", [128, 2048], F32)
                    for half in range(2):
                        bar = P.barrier_vals()
                        for t in range(4):
                            P.dma("scalar", xh[:, t * 2048:(t + 1) * 2048],
                                  xdram[(half * 4 + t) * 128:
                                        (half * 4 + t + 1) * 128, :],
                                  XS, wait=bar if t == 0 else [])
                        ln_half(xh, lambda t, jc, h=half: y2fm[h][:, jc * 512 +
                                                              t * 128:jc * 512 +
                                                              (t + 1) * 128],
                                qsqs, yb, yps, yps_free)
                        if DEBUG and half == 1:
                            P.dma("gpsimd", dbg_qst, stat, AC,
                                  wait=[(DV, DV.val), (SC, SC.val)])
                            P.dma("gpsimd", dbg_yb, yb, AC)
                            P.dma("gpsimd", dbg_xh, xh[:, 0:2048], AC)
                dv_y2 = DV.val
                kqpan = qsb("kqpan", [128, 32 * 512], BF16)
                vpan = qsb("vpan", [128, 32 * 512], BF16)
                if DEBUG:
                    P.dma("gpsimd", dbg_y2[:, 0:16 * 512], y2fm[0], AC,
                          wait=[(DV, dv_y2)])
                    P.dma("gpsimd", dbg_y2[:, 16 * 512:], y2fm[1], AC)

                def load_w(widx, wv):
                    for r in range(8):
                        P.dma("sync", wat.rearrange("p (jc i) -> p jc i", i=2048)
                              [:, :, 256 * r:256 * (r + 1)],
                              wa_all[r, widx, :, :].rearrange(
                                  "(jc p) i -> p jc i", p=128),
                              WG, wait=wv if r == 0 else [])
                    return WG.val

                kvps_free = [[], []]
                nrm = [0]
                cc_k = [0]

                for widx, kind in [(1, "k"), (2, "v"), (0, "q")]:
                    wg_w = load_w(widx, [(CC, cc_w), (PE, PE.val)])
                    if kind in ("k", "q"):
                        for ic in range(16):
                            for th in range(2):
                                par = (ic * 2 + th) % 2
                                sl = kqpan[:, (ic * 2 + th) * 512:
                                           (ic * 2 + th + 1) * 512]
                                for jc in range(16):
                                    P.op("tensor", lambda e, par=par, ic=ic, th=th,
                                         jc=jc: e.matmul(
                                             kvps[par], wat[:, jc * 2048 + ic * 128:
                                                            jc * 2048 + (ic + 1) * 128],
                                             y2fm[th][:, jc * 512:(jc + 1) * 512],
                                             start=(jc == 0), stop=(jc == 15),
                                             skip_group_check=True),
                                        wait=([(WG, wg_w), (DV, dv_y2)]
                                              + kvps_free[par] if jc == 0 else []),
                                        inc=[(PE, 1)] if jc == 15 else [])
                                pe_kv = PE.val
                                bcol = (16 if kind == "k" else 0) + ic
                                P.op("scalar", lambda e, par=par, bcol=bcol,
                                     sl=sl: e.activation(
                                         out=sl, in_=kvps[par], func=AF.Identity,
                                         bias=bqk_tm[:, bcol:bcol + 1]),
                                    wait=[(PE, pe_kv)]
                                    + ([(CC, cc_k[0])] if kind == "q"
                                       and ic == 0 and th == 0 else []),
                                    inc=[(SC, 1)])
                                kvps_free[par] = [(SC, SC.val)]
                                sc_kv = SC.val
                                kq_dst = k_in if kind == "k" else qdram
                                P.dma("sync", kq_dst[ic * 128:(ic + 1) * 128,
                                                     th * 512:(th + 1) * 512],
                                      sl, AC, wait=[(SC, sc_kv)])
                                P.op("scalar", lambda e, sl=sl: e.activation(
                                    out=k2t, in_=sl, func=AF.Square),
                                    wait=[(SC, sc_kv)], inc=[(SC, 1)])
                                sc_k2 = SC.val
                                P.op("tensor", lambda e: e.matmul(
                                    nps, ones_col_b, k2t, start=True, stop=True,
                                    skip_group_check=True),
                                    wait=[(SC, sc_k2), (FE, nrm[0])], inc=[(PE, 1)])
                                pe_n = PE.val
                                tgt = knr if kind == "k" else qnr
                                P.op("vector", lambda e, t2=tgt, ic=ic, th=th:
                                     e.tensor_reduce(
                                         out=t2[0:1, ic * 2 + th:ic * 2 + th + 1],
                                         in_=nps, axis=AX.X, op=ALU.max),
                                     wait=[(PE, pe_n), (FE, FE.val)], inc=[(FE, 1)])
                                nrm[0] = FE.val
                    else:
                        for t in range(8):
                            hh, tt = t // 4, t % 4
                            for iv in range(4):
                                par = (t * 4 + iv) % 2
                                vsl = vpan[:, (t * 4 + iv) * 512:
                                           (t * 4 + iv + 1) * 512]
                                for jc in range(16):
                                    P.op("tensor", lambda e, par=par, hh=hh, tt=tt,
                                         iv=iv, jc=jc: e.matmul(
                                             kvps[par],
                                             y2fm[hh][:, jc * 512 + tt * 128:
                                                      jc * 512 + (tt + 1) * 128],
                                             wat[:, jc * 2048 + iv * 512:
                                                 jc * 2048 + (iv + 1) * 512],
                                             start=(jc == 0), stop=(jc == 15),
                                             skip_group_check=True),
                                        wait=([(WG, wg_w)] + kvps_free[par]
                                              if jc == 0 else []),
                                        inc=[(PE, 1)] if jc == 15 else [])
                                pe_v = PE.val
                                P.op("vector", lambda e, par=par, iv=iv,
                                     vsl=vsl: e.tensor_tensor(
                                         out=vsl, in0=kvps[par],
                                         in1=bv_bc[:, iv * 512:(iv + 1) * 512],
                                         op=ALU.add),
                                     wait=[(PE, pe_v), (FE, FE.val)],
                                     inc=[(DV, 1)])
                                kvps_free[par] = [(DV, DV.val)]
                                dv_v = DV.val
                                P.dma("sync", v_in[t * 128:(t + 1) * 128,
                                                   iv * 512:(iv + 1) * 512],
                                      vsl, AC, wait=[(DV, dv_v)])
                    if kind == "k":
                        P.op("vector", lambda e: e.tensor_reduce(
                            out=mrow[0:1, 224:240],
                            in_=knr[0:1, 0:32].rearrange("x (h t2) -> x h t2", t2=2),
                            axis=AX.X, op=ALU.max),
                            wait=[(FE, nrm[0]), (FE, FE.val)], inc=[(FE, 1)])
                        fe_kn = FE.val
                        P.dma("gpsimd", kn_in, mrow[0:1, 224:240], AC,
                              wait=[(FE, fe_kn)])
                        dram_fence("sync", k_in[0:128, 0:8], AC)
                        ac_k = AC.val
                        P.op("gpsimd", lambda e: e.collective_compute(
                            "AllGather", ALU.bypass, replica_groups=[list(range(8))],
                            ins=[k_in.opt()], outs=[k_all.opt()]),
                            wait=[(AC, ac_k)], inc=[(CC, 1)])
                        P.op("gpsimd", lambda e: e.collective_compute(
                            "AllGather", ALU.bypass, replica_groups=[list(range(8))],
                            ins=[kn_in.opt()], outs=[kn_all.opt()]),
                            inc=[(CC, 1)])
                        cc_k[0] = CC.val
                    if kind == "v":
                        dram_fence("sync", v_in[0:128, 0:8], AC)
                        ac_v = AC.val
                        P.op("gpsimd", lambda e: e.collective_compute(
                            "AllGather", ALU.bypass, replica_groups=[list(range(8))],
                            ins=[v_in.opt()], outs=[v_all.opt()]),
                            wait=[(AC, ac_v)], inc=[(CC, 1)])
                dram_fence("sync", qdram[0:128, 0:8], AC)
                cc_kv = CC.val
                ac_qkv = AC.val

                def vm(fn, wait=()):
                    P.op("vector", fn, wait=list(wait) + [(FE, FE.val)], inc=[(FE, 1)])
                P.dma("scalar", mrow[0:1, 0:128].rearrange(
                    "x (h r) -> x h r", r=8),
                    kn_all.rearrange("r x h -> x h r"), XS, wait=[(CC, cc_kv)],
                    allow_slow_non_contiguous=True)
                xs_kn = XS.val
                vm(lambda e: e.tensor_reduce(
                    out=mrow[0:1, 128:144],
                    in_=mrow[0:1, 0:128].rearrange("x (h r) -> x h r", r=8),
                    axis=AX.X, op=ALU.max), wait=[(XS, xs_kn)])
                vm(lambda e: e.tensor_reduce(
                    out=mrow[0:1, 144:160],
                    in_=qnr[0:1, 0:32].rearrange("x (h t2) -> x h t2", t2=2),
                    axis=AX.X, op=ALU.max), wait=[(FE, nrm[0])])
                vm(lambda e: e.tensor_tensor(
                    out=mrow[0:1, 160:176], in0=mrow[0:1, 128:144],
                    in1=mrow[0:1, 144:160], op=ALU.mult))
                vm(lambda e: e.tensor_scalar(
                    out=mrow[0:1, 176:192], in0=mrow[0:1, 160:176],
                    scalar1=1.0 / HD, scalar2=None, op0=ALU.mult))
                P.op("scalar", lambda e: e.activation(
                    out=mrow[0:1, 192:208], in_=mrow[0:1, 176:192], func=AF.Sqrt),
                    wait=[(FE, FE.val)], inc=[(FE, 1)])
                vm(lambda e: e.tensor_scalar(
                    out=mrow[0:1, 208:224], in0=mrow[0:1, 192:208],
                    scalar1=-1.0, scalar2=None, op0=ALU.mult))
                fe_m = FE.val
                P.op("tensor", lambda e: e.matmul(
                    mps, ones_row_f, mrow[0:1, 208:224], start=True, stop=True,
                    skip_group_check=True), wait=[(FE, fe_m)], inc=[(PE, 1)])
                pe_m = PE.val
                P.op("vector", lambda e: e.tensor_copy(out=negm, in_=mps),
                     wait=[(PE, pe_m), (FE, FE.val)], inc=[(FE, 1)])
                fe_negm = FE.val
                if DEBUG:
                    P.dma("gpsimd", dbg_negm, negm, AC, wait=[(FE, fe_negm)])
                    P.dma("gpsimd", dbg_mrow, mrow, AC)
                    P.dma("gpsimd", dbg_qn, qnr, AC)
                    P.dma("gpsimd", dbg_q, qdram, AC, wait=[(AC, ac_qkv)])
                    P.dma("gpsimd", dbg_k, k_all[0:2], AC, wait=[(CC, cc_kv)])
                    P.dma("gpsimd", dbg_v, v_all[:, 0:128, :], AC)

            P.label = "attn"
            avsb_es = ExitStack()
            av_sb = [avsb_es.enter_context(
                nc.sbuf_tensor(f"avsb{i}", [128, NH * 512], F32))[:]
                for i in range(2)]
            with ExitStack() as aes:
                asb = lambda n, s, d: aes.enter_context(nc.sbuf_tensor(n, s, d))[:]
                apsf = lambda n, s, d: psum_bank(aes, nc, n, s, d)
                ks = [asb(f"ks{i}", [128, 16 * 512], BF16) for i in range(2)]
                vs = [asb(f"vs{i}", [128, 4 * 2048], BF16) for i in range(2)]
                qs = [asb(f"qs{i}", [128, 16 * 512], BF16) for i in range(2)]
                pb = [asb(f"pb{i}", [128, 512], BF16) for i in range(4)]
                pm = [asb(f"pm{i}", [128, 512], BF16) for i in range(4)]
                fencb = asb("fencb", [128, 8], BF16)
                cms = asb("cms", [128, 4 * 512], BF16)
                P.dma("sync", cms, cmi, LD)
                ldc = LD.val
                sut = [asb(f"sut{i}", [1, 512], F32) for i in range(2)]
                sps = [apsf(f"sps{i}", [128, 512], F32) for i in range(2)]
                avp = [apsf(f"avp{i}", [128, 512], F32) for i in range(2)]
                sup = [apsf(f"sup{i}", [128, 512], F32) for i in range(2)]

                kq_free = [0, 0]
                v_free = [0, 0]
                sps_free = [0, 0]
                avp_free = [0, 0]
                sup_free = [0, 0]
                sut_free = [0, 0]
                pb_free = [0, 0, 0, 0]
                pm_free = [0, 0, 0, 0]

                KVK = P.sem("KVK")
                KVV = P.sem("KVV")
                KVQ = P.sem("KVQ")

                def dyn_load(eng, t, col, buf_ap, ap_dims, sem):
                    sh = sem.h

                    def fn(e, t=t, col=col, buf_ap=buf_ap, ap_dims=ap_dims, sh=sh):
                        with e.register(f"r{eng}{t}") as r:
                            e.reg_load(r, tabs[0:1, t * 8 + col:t * 8 + col + 1])
                            roff = e.snap(r, True, min_val=0,
                                          max_val=8 * 2048 * 1024)
                            src_t = k_all.tensor if col == 0 else (
                                v_all.tensor if col == 1 else qdram.tensor)
                            e.dma_start(out=buf_ap,
                                        in_=RawAP(src_t, roff, ap_dims)
                                        ).then_inc(sh, 16)
                            return e.dma_start(
                                out=fencb,
                                in_=RawAP(src_t, roff, [[ap_dims[0][0], 128],
                                                        [1, 8]]))
                    P.op(eng, fn, wait=[(CC, cc_kv), (AC, ac_qkv),
                                        (PE, kq_free[t % 2] if col != 1
                                         else v_free[t % 2])],
                         inc=[(sem, 16)], ghost_inc=[(sem, 16)])
                    return sem.val

                K_AP = [[1024, 128], [128 * 1024, 16], [1, 512]]
                V_AP = [[2048, 128], [128 * 2048, 4], [1, 2048]]
                Q_AP = [[1024, 128], [128 * 1024, 16], [1, 512]]

                def load_slot(t):
                    par = t % 2
                    a = dyn_load("sync", t, 0,
                                 ks[par].rearrange("p (ic t2) -> p ic t2", t2=512),
                                 K_AP, KVK)
                    b = dyn_load("gpsimd", t, 1,
                                 vs[par].rearrange("p (kc d2) -> p kc d2", d2=2048),
                                 V_AP, KVV)
                    c = dyn_load("scalar", t, 2,
                                 qs[par].rearrange("p (ic t2) -> p ic t2", t2=512),
                                 Q_AP, KVQ)
                    return [(KVK, a), (KVV, b), (KVQ, c)]

                kv_ld = [None] * (NSLOT + 1)
                kv_ld[0] = load_slot(0)
                for t in range(NSLOT):
                    par = t % 2
                    if t + 1 < NSLOT:
                        kv_ld[t + 1] = load_slot(t + 1)
                    diag = t < 2
                    for h in range(NH):
                        hp = h % 2
                        for kc in range(4):
                            sp = kc % 2
                            P.op("tensor", lambda e, par=par, h=h, kc=kc, sp=sp:
                                 e.matmul(
                                     sps[sp], ks[par][:, h * 512 + kc * 128:
                                                      h * 512 + (kc + 1) * 128],
                                     qs[par][:, h * 512:(h + 1) * 512],
                                     start=True, stop=True, skip_group_check=True),
                                 wait=kv_ld[t] + [(SC, sps_free[sp]),
                                                  (FE, fe_negm)],
                                 inc=[(PE, 1)])
                            pe_s = PE.val
                            P.op("scalar", lambda e, sp=sp, kc=kc, h=h: e.activation(
                                out=pb[kc], in_=sps[sp], func=AF.Exp,
                                bias=negm[:, h:h + 1], scale=SM_SCALE),
                                wait=[(PE, pe_s), (PE, pb_free[kc])], inc=[(SC, 1)])
                            sps_free[sp] = SC.val
                            sc_e = SC.val
                            if diag:
                                P.op("vector", lambda e, kc=kc: e.tensor_tensor(
                                    out=pm[kc], in0=pb[kc],
                                    in1=cms[:, kc * 512:(kc + 1) * 512], op=ALU.mult),
                                    wait=[(SC, sc_e), (LD, ldc), (PE, pm_free[kc])],
                                    inc=[(DV, 1)])
                                dv_m = DV.val
                                src2, wv = pm, [(DV, dv_m)]
                            else:
                                src2, wv = pb, [(SC, sc_e)]
                            P.op("tensor", lambda e, par=par, h=h, kc=kc, s2=src2,
                                 hp=hp: e.matmul(
                                     avp[hp], vs[par][:, kc * 2048 + h * 128:
                                                      kc * 2048 + (h + 1) * 128],
                                     s2[kc], start=(kc == 0), stop=(kc == 3),
                                     skip_group_check=True),
                                 wait=wv + ([(DV, avp_free[hp])] + kv_ld[t]
                                            if kc == 0 else []))
                            P.op("tensor", lambda e, kc=kc, s2=src2, hp=hp: e.matmul(
                                sup[hp], cbs[:, 129:257], s2[kc], start=(kc == 0),
                                stop=(kc == 3), skip_group_check=True),
                                wait=([(DV, sup_free[hp])] if kc == 0 else []),
                                inc=[(PE, 1)])
                            pb_free[kc] = PE.val
                            pm_free[kc] = PE.val
                        pe_av = PE.val
                        for qb2 in range(2):
                            if t == 0:
                                P.op("vector", lambda e, hp=hp, qb2=qb2, t=t, h=h:
                                     e.tensor_scalar(
                                         out=av_sb[qb2][:, h * 512:(h + 1) * 512],
                                         in0=avp[hp],
                                         scalar1=wts[:, qb2 * NSLOT + t:
                                                     qb2 * NSLOT + t + 1],
                                         scalar2=None, op0=ALU.mult),
                                     wait=([(PE, pe_av), (LD, ldv)]
                                           if qb2 == 0 else []),
                                     inc=[(DV, 1)])
                            else:
                                P.op("vector", lambda e, hp=hp, qb2=qb2, t=t, h=h:
                                     e.scalar_tensor_tensor(
                                         out=av_sb[qb2][:, h * 512:(h + 1) * 512],
                                         in0=avp[hp],
                                         scalar=wts[:, qb2 * NSLOT + t:
                                                    qb2 * NSLOT + t + 1],
                                         in1=av_sb[qb2][:, h * 512:(h + 1) * 512],
                                         op0=ALU.mult, op1=ALU.add),
                                     wait=([(PE, pe_av), (LD, ldv)]
                                           if qb2 == 0 else []),
                                     inc=[(DV, 1)])
                        avp_free[hp] = DV.val
                        P.op("vector", lambda e, hp=hp: e.tensor_copy(
                            out=sut[hp], in_=sup[hp][0:1, :]),
                            wait=[(AC, sut_free[hp])], inc=[(DV, 1)])
                        sup_free[hp] = DV.val
                        dv_su = DV.val
                        P.dma("gpsimd", sudram[t, h, :].unsqueeze(0), sut[hp], AC,
                              wait=[(DV, dv_su)])
                        dram_fence("gpsimd", sudram[t, h, 0:8].unsqueeze(0), AC)
                        sut_free[hp] = AC.val
                    kq_free[par] = PE.val
                    v_free[par] = PE.val
                if DEBUG:
                    P.dma("gpsimd", dbg_av[0], av_sb[0], AC,
                          wait=[(DV, DV.val)])
                    P.dma("gpsimd", dbg_av[1], av_sb[1], AC)
                    P.dma("gpsimd", dbg_wts, wts, AC)
                    P.dma("gpsimd", dbg_tab, tabs, AC)

            P.label = "oproj"
            dram_fence("gpsimd", sudram[NSLOT - 1, NH - 1, 0:8].unsqueeze(0), AC)
            with ExitStack() as oes:
                osb = lambda n, s, d: oes.enter_context(nc.sbuf_tensor(n, s, d))[:]
                opsf = lambda n, s, d: psum_bank(oes, nc, n, s, d)
                wos = osb("wos", [128, 16 * 2048], BF16)
                sus2 = [osb(f"sus{i}", [NSLOT, 512], F32) for i in range(2)]
                avb = osb("avb", [128, 16 * 512], BF16)
                pout = osb("pout", [128, 8 * 2048], BF16)
                bo_bc = osb("bo_bc", [128, D], F32)
                P.dma("sync", bo_bc, bo_bc_i, LD)
                ops = [opsf(f"ops{i}", [128, 512], F32) for i in range(2)]
                ibp = opsf("ibp", [128, 512], F32)
                sps2 = opsf("sps2", [1, 512], F32)
                jnk = opsf("jnk", [128, 512], F32)

                def pe_pad():
                    P.op("tensor", lambda e: e.matmul(
                        jnk, ident, cbs[:, 0:512], start=True, stop=True,
                        skip_group_check=True), inc=[(PE, 1)])
                    return PE.val

                ac_att = AC.val
                bar_o = P.barrier_vals()
                for r in range(8):
                    P.dma("sync", wos.rearrange("p (jc i) -> p jc i", i=2048)
                          [:, :, 256 * r:256 * (r + 1)],
                          wa_all[r, 3, :, :].rearrange("(jc p) i -> p jc i", p=128),
                          WG, wait=bar_o if r == 0 else [])
                wg_o = WG.val
                if DEBUG:
                    P.dma("gpsimd", dbg_su, sudram, AC, wait=[(AC, ac_att)])
                sus_xs = [0, 0]
                sus_pe = [0, 0]

                def sus_load(pos, h):
                    sl = pos % 2
                    P.dma("scalar", sus2[sl], sudram[:, h, :], XS,
                          wait=[(AC, ac_att), (PE, sus_pe[sl])])
                    sus_xs[sl] = XS.val

                avb_free = 0
                ops_free = [0, 0]
                dbg_gate = []
                for qb in range(2):
                    dv_att = DV.val
                    ibp_free = 0
                    row_free = 0
                    sps2_free = 0
                    hseq = ([0] if qb == 0 else []) + list(range(NH))
                    sus_load(0, hseq[0])
                    for pos, h in enumerate(hseq):
                        sl = pos % 2
                        if pos + 1 < len(hseq):
                            sus_load(pos + 1, hseq[pos + 1])
                        P.op("tensor", lambda e, qb=qb, sl=sl: e.matmul(
                            sps2, wtv[:, qb:qb + 1], sus2[sl],
                            start=True, stop=True, skip_group_check=True),
                            wait=[(XS, sus_xs[sl]), (LD, LD.val),
                                  (FE, sps2_free)],
                            inc=[(PE, 1)])
                        sus_pe[sl] = PE.val
                        pe_su = pe_pad()
                        P.op("vector", lambda e: e.reciprocal(
                            out=row[0:1, 0:512], in_=sps2[0:1, :]),
                            wait=[(PE, pe_su), (FE, FE.val), (PE, row_free)],
                            inc=[(FE, 1)])
                        fe_i = FE.val
                        sps2_free = fe_i
                        P.op("tensor", lambda e: e.matmul(
                            ibp, ones_row_f, row[0:1, 0:512], start=True, stop=True,
                            skip_group_check=True),
                            wait=[(FE, fe_i), (DV, ibp_free)], inc=[(PE, 1)])
                        pe_i = pe_pad()
                        row_free = pe_i
                        P.op("vector", lambda e, qb=qb, h=h: e.tensor_tensor(
                            out=avb[:, h * 512:(h + 1) * 512],
                            in0=av_sb[qb][:, h * 512:(h + 1) * 512], in1=ibp,
                            op=ALU.mult),
                            wait=[(PE, pe_i), (DV, dv_att), (PE, avb_free)]
                            + dbg_gate,
                            inc=[(DV, 1)])
                        ibp_free = DV.val
                    dv_avb = DV.val
                    if DEBUG and qb == 0:
                        P.dma("gpsimd", dbg_avb, avb, AC, wait=[(DV, dv_avb)])
                        dbg_gate = [(AC, AC.val)]
                    for t in range(4):
                        gt = qb * 4 + t
                        psl = pout[:, gt * 2048:(gt + 1) * 2048]
                        for i in range(4):
                            par = i % 2
                            for h in range(NH):
                                P.op("tensor", lambda e, t=t, i=i, h=h, par=par:
                                     e.matmul(
                                         ops[par], avb[:, h * 512 + t * 128:
                                                       h * 512 + (t + 1) * 128],
                                         wos[:, h * 2048 + i * 512:
                                             h * 2048 + (i + 1) * 512],
                                         start=(h == 0), stop=(h == 15),
                                         skip_group_check=True),
                                     wait=([(WG, wg_o), (DV, dv_avb),
                                            (DV, ops_free[par])] if h == 0 else []),
                                     inc=[(PE, 1)] if h == 15 else [])
                            pe_o = PE.val
                            P.op("vector", lambda e, i=i, par=par, psl=psl:
                                 e.scalar_tensor_tensor(
                                     out=psl[:, i * 512:(i + 1) * 512],
                                     in0=ops[par], scalar=1.0,
                                     in1=bo_bc[:, i * 512:(i + 1) * 512],
                                     op0=ALU.mult, op1=ALU.add),
                                 wait=[(PE, pe_o), (LD, LD.val)],
                                 inc=[(DV, 1)])
                            ops_free[par] = DV.val
                        dv_p = DV.val
                        P.dma("sync", podram[gt * 128:(gt + 1) * 128, :], psl, AC,
                              wait=[(DV, dv_p)])
                    avb_free = PE.val
                dram_fence("sync", podram[0:128, 0:8], AC)
            avsb_es.close()

        if STAGE == "full":
            ffn_pass("f2", xdram, out, proj_dram=podram)
        else:
            bar = P.barrier_vals()
            with ExitStack() as x2es:
                x2b = x2es.enter_context(
                    nc.sbuf_tensor("x2asm", [128, 2048], F32))[:]
                p2b = x2es.enter_context(
                    nc.sbuf_tensor("p2asm", [128, 2048], BF16))[:]
                fe_a = FE.val
                for i in range(8):
                    P.dma("scalar", x2b, xdram[i * 128:(i + 1) * 128, :], XS,
                          wait=(bar if i == 0 else []) + [(AC, AC.val)])
                    P.dma("scalar", p2b, podram[i * 128:(i + 1) * 128, :], XS)
                    xs_a = XS.val
                    P.op("vector", lambda e: e.tensor_tensor(
                        out=x2b, in0=x2b, in1=p2b, op=ALU.add),
                        wait=[(XS, xs_a), (FE, fe_a)], inc=[(FE, 1)])
                    fe_a = FE.val
                    P.dma("gpsimd", out[i * 128:(i + 1) * 128, :], x2b, AC,
                          wait=[(FE, fe_a)])
        dram_fence("gpsimd", out[0:128, 0:8], AC)
        P.op("gpsimd", None, wait=P.barrier_vals())

        P.check_schedule()
        blk = es.enter_context(nc.Block())
        P.replay(blk)
    return nc


_CACHE = {}


def kernel(**inputs):
    x = np.asarray(inputs["L0_attn_hidden_states"]).reshape(S, D)
    wgu = np.asarray(inputs["L0_ffn_w_gate_up"])
    wp = np.asarray(inputs["L0_ffn_w_proj"])
    cbn, cfn, cmn = host_constants()
    bq = np.asarray(inputs["L0_attn_bq"]).astype(np.float32)
    bk = np.asarray(inputs["L0_attn_bk"]).astype(np.float32)
    bv = np.asarray(inputs["L0_attn_bv"]).astype(np.float32)
    bo = np.asarray(inputs["L0_attn_bo"]).astype(np.float32)
    bqk_tm = np.concatenate([bq.reshape(16, 128).T, bk.reshape(16, 128).T], 1)
    in_maps = []
    for c in range(NCORE):
        rows = core_rows(c)
        gate = wgu[1024 * c:1024 * (c + 1), :]
        up = wgu[8192 + 1024 * c:8192 + 1024 * (c + 1), :]
        m = {
            "xin": np.ascontiguousarray(x[rows]),
            "wgu_s": np.ascontiguousarray(np.concatenate([gate, up], 0)),
            "wp_s": np.ascontiguousarray(wp[:, 1024 * c:1024 * (c + 1)]),
            "wa_s": np.ascontiguousarray(np.stack(
                [np.asarray(inputs[f"L0_attn_W{w}"])[256 * c:256 * (c + 1), :]
                 for w in "qkvo"])),
            "bqk_tm_i": np.ascontiguousarray(bqk_tm.astype(np.float32)),
            "bv_bc_i": np.ascontiguousarray(np.tile(bv[None, :], (128, 1))),
            "bo_bc_i": np.ascontiguousarray(np.tile(bo[None, :], (128, 1))),
            "cbi": cbn, "cfi": cfn, "cmi": cmn,
            "tabi": slot_table(c)[0], "wti": slot_table(c)[1],
            "wtvi": np.ascontiguousarray(
                slot_table(c)[1][0, :].reshape(2, NSLOT).T.astype(np.float32)),
        }
        in_maps.append(m)
    if "nc" not in _CACHE:
        _CACHE["nc"] = build_program()
    res = run_bass_kernel_spmd(_CACHE["nc"], in_maps, core_ids=list(range(NCORE)))
    full = np.zeros((S, D), np.float32)
    for c in range(NCORE):
        full[core_rows(c)] = res.results[c]["out"]
    return full.reshape(1, S, D)


# version 03
# speedup vs baseline: 1.1573x; 1.1573x over previous
import numpy as np
import ml_dtypes
from contextlib import ExitStack

import concourse.bass as bass
import concourse.mybir as mybir
from concourse.bass_utils import run_bass_kernel_spmd
from concourse.ap import AP as RawAP

F32 = mybir.dt.float32
BF16 = mybir.dt.bfloat16
I32 = mybir.dt.int32
AF = mybir.ActivationFunctionType
ALU = mybir.AluOpType
AX = mybir.AxisListType
ds = bass.ds

NCORE = 8
S, D, T = 8192, 2048, 1024
NH, HD = 16, 128
FF = 8192
NJ = 16
NPAIR = 64
NSLOT = 17
C_THR = 0.5009765625
SM_SCALE = float(HD) ** -0.5
RMS_EPS = float(np.float32(np.finfo(np.float32).eps))
LN_EPS = 1e-5
BF = ml_dtypes.bfloat16

STAGE = "full"
DEBUG = False


def core_rows(c):
    return list(range(512 * c, 512 * c + 512)) + list(range(7680 - 512 * c, 8192 - 512 * c))


def slot_table(c):
    def sup(s):
        return (s, 0) if s <= 7 else (15 - s, 512)
    slots = [(0, c), (1, 15 - c)]
    slots += [(0, s) for s in range(c)]
    slots += [(1, s) for s in range(15 - c)]
    assert len(slots) == NSLOT
    rows = []
    for qb, s in slots:
        r, o = sup(s)
        koff = r * 2048 * 1024 + o
        voff = r * 1024 * 2048 + o * 2048
        rows.append([koff, voff, 512 * qb, qb, 0, 0, 0, 0])
    wt = np.zeros((128, 2 * NSLOT), np.float32)
    for t, (qb, s) in enumerate(slots):
        wt[:, qb * NSLOT + t] = 1.0
    return np.asarray(rows, np.int32).reshape(1, -1), wt


def host_constants():
    cbn = np.zeros((128, 640), dtype=BF)
    cbn[:, 0:128] = np.eye(128, dtype=BF)
    cbn[:, 128] = 1.0
    cbn[:, 129:257] = 1.0
    cfn = np.zeros((128, 160), np.float32)
    cfn[:, 0] = SM_SCALE
    cfn[:, 16:144] = 1.0
    q = np.arange(512)[None, :]
    k = np.arange(128)[:, None]
    cmn = np.zeros((128, 4 * 512), dtype=BF)
    for j in range(4):
        cmn[:, j * 512:(j + 1) * 512] = (q >= 128 * j + k).astype(BF)
    return cbn, cfn, cmn


class Sem:
    def __init__(self, es, nc, name):
        self.h = es.enter_context(nc.semaphore(name))
        self.name = name
        self.val = 0


class Prog:
    ENGINES = ("sync", "gpsimd", "vector", "scalar", "tensor")

    def __init__(self, nc, es):
        self.nc = nc
        self.es = es
        self.ops = {e: [] for e in self.ENGINES}
        self.sems = {}
        self.label = "init"

    def sem(self, name):
        if name not in self.sems:
            self.sems[name] = Sem(self.es, self.nc, name)
        return self.sems[name]

    def op(self, eng, fn=None, wait=(), inc=(), ghost_inc=()):
        wait = [(s, v) for (s, v) in wait if v > 0]
        inc2 = []
        for (s, n) in inc:
            s.val += n
            inc2.append((s, n, True))
        for (s, n) in ghost_inc:
            s.val += n
            inc2.append((s, n, False))
        assert fn is not None or not inc2
        self.ops[eng].append((fn, wait, inc2, self.label))

    def dma(self, eng, out, in_, sem, wait=(), **kw):
        self.op(eng, lambda e, out=out, in_=in_, kw=kw: e.dma_start(out=out, in_=in_, **kw),
                wait=wait, inc=[(sem, 16)])
        return sem.val

    def barrier_vals(self):
        return [(s, s.val) for s in self.sems.values()]

    def hard_barrier(self):
        bv = self.barrier_vals()
        for e in self.ENGINES:
            self.op(e, None, wait=bv)

    def check_schedule(self):
        idx = {e: 0 for e in self.ENGINES}
        vals = {s.name: 0 for s in self.sems.values()}
        total = sum(len(v) for v in self.ops.values())
        done = 0
        while done < total:
            progressed = False
            for e in self.ENGINES:
                while idx[e] < len(self.ops[e]):
                    fn, wait, incs, _lb = self.ops[e][idx[e]]
                    if any(vals[s.name] < v for (s, v) in wait):
                        break
                    for (s, n, _att) in incs:
                        vals[s.name] += n
                    idx[e] += 1
                    done += 1
                    progressed = True
            if not progressed:
                detail = []
                for e in self.ENGINES:
                    if idx[e] < len(self.ops[e]):
                        _, wait, _, lb = self.ops[e][idx[e]]
                        w = [(s.name, v, vals[s.name]) for (s, v) in wait
                             if vals[s.name] < v]
                        detail.append(f"{e}[op {idx[e]}/{len(self.ops[e])} @{lb}] blocked on {w}")
                        for (s, v) in wait:
                            if vals[s.name] < v:
                                acc = dict(vals)
                                for e2 in self.ENGINES:
                                    run = vals[s.name]
                                    for j in range(idx[e2], len(self.ops[e2])):
                                        _, _, incs2, lb2 = self.ops[e2][j]
                                        for (s2, n2) in incs2:
                                            if s2.name == s.name:
                                                run += n2
                                        if run >= v:
                                            detail.append(
                                                f"   {s.name}->{v} would come from "
                                                f"{e2}[{j} @{lb2}]")
                                            break
                                    else:
                                        continue
                                    break
                raise RuntimeError("DEADLOCK:\n" + "\n".join(detail))

    def replay(self, block):
        def mk(engname):
            def emit(eng):
                for fn, wait, incs, _lb in self.ops[engname]:
                    for s, v in wait:
                        eng.wait_ge(s.h, v)
                    if fn is not None:
                        ins = fn(eng)
                        for s, n, att in incs:
                            if att:
                                ins = ins.then_inc(s.h, n)
            return emit
        block.sync(mk("sync"))
        block.gpsimd(mk("gpsimd"))
        block.vector(mk("vector"))
        block.scalar(mk("scalar"))
        block.tensor(mk("tensor"))


def psum_bank(es, nc, name, shape, dtype):
    esz = 2 if dtype == BF16 else 4
    per = shape[-1] * esz
    assert per <= 2048
    pad_last = 2048 // esz
    full = list(shape[:-1]) + [pad_last]
    if len(shape) == 2 and shape[0] < 128:
        full[0] = 128
    t = es.enter_context(nc.psum_tensor(name, full, dtype))
    ap = t[:]
    sl = tuple([slice(0, shape[i]) for i in range(len(shape) - 1)]
               + [slice(0, shape[-1])])
    return ap[sl]


def build_program():
    nc = bass.Bass("TRN2", target_bir_lowering=False, detect_race_conditions=False)
    g = lambda n, s, d, k="ExternalInput": nc.dram_tensor(n, s, d, kind=k).ap()

    xin = g("xin", [T, D], F32)
    wgu_s = g("wgu_s", [2048, 2048], F32)
    wp_s = g("wp_s", [2048, 1024], F32)
    wa_s = g("wa_s", [4, 256, 2048], F32)
    bqk_tm_i = g("bqk_tm_i", [128, 32], F32)
    bv_bc_i = g("bv_bc_i", [128, D], F32)
    bo_bc_i = g("bo_bc_i", [128, D], F32)
    cbi = g("cbi", [128, 640], BF16)
    cfi = g("cfi", [128, 160], F32)
    cmi = g("cmi", [128, 4 * 512], BF16)
    tabi = g("tabi", [1, NSLOT * 8], I32)
    wti = g("wti", [128, 2 * NSLOT], F32)
    wtvi = g("wtvi", [NSLOT, 2], F32)
    out = g("out", [T, D], F32, "ExternalOutput")
    if DEBUG:
        dbg_yfm = g("dbg_yfm", [128, 16 * 512], BF16, "ExternalOutput")
        dbg_hfm = g("dbg_hfm", [128, 64 * 512], BF16, "ExternalOutput")
        dbg_stat = g("dbg_stat", [128, 64], F32, "ExternalOutput")
        dbg_hinv = g("dbg_hinv", [128, 4], F32, "ExternalOutput")
        dbg_negm = g("dbg_negm", [128, NH], F32, "ExternalOutput")
        dbg_mrow = g("dbg_mrow", [1, 512], F32, "ExternalOutput")
        dbg_su = g("dbg_su", [NSLOT, NH, 512], F32, "ExternalOutput")
        dbg_av = g("dbg_av", [2, 128, NH * 512], F32, "ExternalOutput")
        dbg_qn = g("dbg_qn", [1, 64], F32, "ExternalOutput")
        dbg_pm = g("dbg_pm", [4, 128, 512], BF16, "ExternalOutput")
        dbg_ks = g("dbg_ks", [128, 16 * 512], BF16, "ExternalOutput")
        dbg_vs = g("dbg_vs", [128, 4 * 2048], BF16, "ExternalOutput")
        dbg_qs = g("dbg_qs", [128, 16 * 512], BF16, "ExternalOutput")
        dbg_avt = g("dbg_avt", [128, 1024], F32, "ExternalOutput")
        dbg_sps = g("dbg_sps", [128, 512], F32, "ExternalOutput")
        dbg_sut = g("dbg_sut", [1, 1024], F32, "ExternalOutput")
        dbg_q = g("dbg_q", [2048, 1024], BF16, "ExternalOutput")
        dbg_y2 = g("dbg_y2", [128, 16 * 1024], BF16, "ExternalOutput")
        dbg_qst = g("dbg_qst", [128, 64], F32, "ExternalOutput")
        dbg_yb = g("dbg_yb", [128, 2048], BF16, "ExternalOutput")
        dbg_xh = g("dbg_xh", [128, 2048], F32, "ExternalOutput")
        dbg_wts = g("dbg_wts", [128, 2 * NSLOT], F32, "ExternalOutput")
        dbg_tab = g("dbg_tab", [1, NSLOT * 8], I32, "ExternalOutput")
        dbg_avb = g("dbg_avb", [128, NH * 512], BF16, "ExternalOutput")
        dbg_srow = g("dbg_srow", [1, NH * 512], F32, "ExternalOutput")
        dbg_k = g("dbg_k", [2, 2048, 1024], BF16, "ExternalOutput")
        dbg_v = g("dbg_v", [8, 128, 2048], BF16, "ExternalOutput")

    idr = lambda n, s, d: nc.dram_tensor(n, s, d).ap()
    sdr = lambda n, s, d: nc.dram_tensor(n, s, d, addr_space="Shared").ap()
    wgu_in = idr("wgu_in", [2048, 2048], BF16)
    wp_in = idr("wp_in", [1024, 2048], BF16)
    wa_in = idr("wa_in", [4, 2048, 256], BF16)
    wgu_all = sdr("wgu_all", [8, 2048, 2048], BF16)
    wp_all = sdr("wp_all", [8, 1024, 2048], BF16)
    wa_all = sdr("wa_all", [8, 4, 2048, 256], BF16)
    k_in = idr("k_in", [2048, 1024], BF16)
    v_in = idr("v_in", [1024, 2048], BF16)
    kn_in = idr("kn_in", [1, 16], F32)
    k_all = sdr("k_all", [8, 2048, 1024], BF16)
    v_all = sdr("v_all", [8, 1024, 2048], BF16)
    kn_all = sdr("kn_all", [8, 1, 16], F32)
    xdram = idr("xdram", [T, D], F32)
    qdram = idr("qdram", [2048, 1024], BF16)
    sudram = idr("sudram", [NSLOT, NH, 512], F32)
    podram = idr("podram", [T, D], BF16)

    es = ExitStack()
    with es:
        P = Prog(nc, es)
        sb = lambda n, s, d: es.enter_context(nc.sbuf_tensor(n, s, d))[:]

        LD = P.sem("LD")
        WG = P.sem("WG")
        XS = P.sem("XS")
        PP = P.sem("PP")
        CC = P.sem("CC")
        AC = P.sem("AC")
        KV = P.sem("KV")
        PE = P.sem("PE")
        DV = P.sem("DV")
        SC = P.sem("SC")
        FE = P.sem("FE")

        cbs = sb("cbs", [128, 640], BF16)
        cfs = sb("cfs", [128, 160], F32)
        tabs = sb("tabs", [1, NSLOT * 8], I32)
        wts = sb("wts", [128, 2 * NSLOT], F32)
        wtv = sb("wtv", [NSLOT, 2], F32)
        bqk_tm = sb("bqk_tm", [128, 32], F32)
        stat = sb("stat", [128, 64], F32)
        fsc = sb("fsc", [128, 16], F32)
        fsb = sb("fsb", [128, 16], BF16)

        def dram_fence(eng, dram_ap, sem):
            p, f = dram_ap.shape[0], dram_ap.shape[1]
            scr = fsb if dram_ap.tensor.dtype == BF16 else fsc
            P.dma(eng, scr[0:p, 0:f], dram_ap, sem)
            return sem.val
        row = sb("row", [1, 2048], F32)
        negm = sb("negm", [128, NH], F32)

        nc.const_aps.register(F32, SM_SCALE, cfs[:, 0:1])
        ident = cbs[:, 0:128]
        ones_col_b = cbs[:, 128:129]
        ones_row_f = cfs[0:1, 16:144]

        for src, dst in [(cbi, cbs), (cfi, cfs), (tabi, tabs),
                         (bqk_tm_i, bqk_tm), (wti, wts), (wtvi, wtv)]:
            P.dma("sync", dst, src, LD)
        ldv = LD.val

        P.label = "preproc"
        with ExitStack() as pes:
            psb = lambda n, s, d: pes.enter_context(nc.sbuf_tensor(n, s, d))[:]
            pps = lambda n, s, d: psum_bank(pes, nc, n, s, d)
            wfp = [psb(f"ppwfp{i}", [128, 2048], F32) for i in range(2)]
            wbb = [psb(f"ppwbb{i}", [128, 2048], BF16) for i in range(2)]
            wab = psb("ppwab", [128, 2048], BF16)
            wsg = psb("ppwsg", [128, 2048], BF16)
            gsc = psb("ppgsc", [128, 32], F32)
            scb = psb("ppscb", [128, 32], BF16)
            thr = psb("ppthr", [128, 32], F32)
            msk = psb("ppmsk", [128, 2048], BF16)
            wht = psb("ppwht", [128, 2048], BF16)
            dd_ = psb("ppdd", [128, 2048], BF16)
            fwd = [psb(f"ppfwd{i}", [128, 2048], BF16) for i in range(2)]
            asmb = [psb(f"ppasm{i}", [128, 2048], BF16) for i in range(2)]
            tp_ps = [pps(f"pptps{i}", [128, 128], BF16) for i in range(2)]

            wfp_free = [0, 0]
            fwd_free = [0, 0]
            asm_free = [0, 0]
            tp_free = [0, 0]

            def quant_tile(par, C, quant, wg_ld):
                G = C // 64
                P.op("vector", lambda e, p=par, C=C: e.tensor_copy(
                    out=wbb[p][:, :C], in_=wfp[p][:, :C]),
                    wait=[(WG, wg_ld)], inc=[(FE, 1)])
                wfp_free[par] = FE.val
                e0 = FE.val
                if not quant:
                    P.op("vector", lambda e, p=par, C=C: e.tensor_copy(
                        out=fwd[p][:, :C], in_=wbb[p][:, :C]),
                        wait=[(FE, e0), (PE, fwd_free[par])], inc=[(FE, 1)])
                    return
                P.op("scalar", lambda e, p=par, C=C: e.activation(
                    out=wab[:, :C], in_=wbb[p][:, :C], func=AF.Abs),
                    wait=[(FE, e0), (FE, FE.val)])
                P.op("scalar", lambda e, p=par, C=C: e.sign(
                    out=wsg[:, :C], in_=wbb[p][:, :C]), inc=[(SC, 1)])
                scv = SC.val
                P.op("vector", lambda e, p=par, C=C, G=G: e.tensor_reduce(
                    out=gsc[:, :G], in_=wbb[p][:, :C].rearrange("p (g k) -> p g k", k=64),
                    axis=AX.X, op=ALU.add, apply_absolute_value=True),
                    wait=[(FE, e0)], inc=[(FE, 1)])
                P.op("vector", lambda e, G=G: e.tensor_scalar(
                    out=scb[:, :G], in0=gsc[:, :G], scalar1=1.0 / 64, scalar2=1e-8,
                    op0=ALU.mult, op1=ALU.max), wait=[(FE, FE.val)], inc=[(FE, 1)])
                P.op("vector", lambda e, G=G: e.tensor_scalar(
                    out=thr[:, :G], in0=scb[:, :G], scalar1=C_THR, scalar2=None,
                    op0=ALU.mult), wait=[(FE, FE.val)], inc=[(FE, 1)])
                P.op("vector", lambda e, C=C, G=G: e.tensor_tensor(
                    out=msk[:, :C].rearrange("p (g k) -> p g k", k=64),
                    in0=wab[:, :C].rearrange("p (g k) -> p g k", k=64),
                    in1=thr[:, :G].unsqueeze(2).broadcast_to([128, G, 64]),
                    op=ALU.is_gt), wait=[(FE, FE.val), (SC, scv)], inc=[(FE, 1)])
                P.op("vector", lambda e, C=C: e.tensor_tensor(
                    out=wht[:, :C], in0=msk[:, :C], in1=wsg[:, :C], op=ALU.mult),
                    wait=[(FE, FE.val)], inc=[(FE, 1)])
                P.op("vector", lambda e, C=C, G=G: e.tensor_tensor(
                    out=wht[:, :C].rearrange("p (g k) -> p g k", k=64),
                    in0=wht[:, :C].rearrange("p (g k) -> p g k", k=64),
                    in1=scb[:, :G].unsqueeze(2).broadcast_to([128, G, 64]),
                    op=ALU.mult), wait=[(FE, FE.val)], inc=[(FE, 1)])
                P.op("vector", lambda e, p=par, C=C: e.tensor_tensor(
                    out=dd_[:, :C], in0=wht[:, :C], in1=wbb[p][:, :C], op=ALU.subtract),
                    wait=[(FE, FE.val)], inc=[(FE, 1)])
                P.op("vector", lambda e, p=par, C=C: e.tensor_tensor(
                    out=fwd[p][:, :C], in0=wbb[p][:, :C], in1=dd_[:, :C], op=ALU.add),
                    wait=[(FE, FE.val), (PE, fwd_free[par])], inc=[(FE, 1)])

            def transpose_out(par, C, dst):
                fv = FE.val
                for jc in range(C // 128):
                    tp = jc % 2
                    P.op("tensor", lambda e, p=par, jc=jc, tp=tp: e.transpose(
                        tp_ps[tp][:], fwd[p][:, jc * 128:(jc + 1) * 128], ident),
                        wait=[(FE, fv), (LD, ldv), (DV, tp_free[tp]),
                              (PP, asm_free[par])],
                        inc=[(PE, 1)])
                    pev = PE.val
                    P.op("vector", lambda e, p=par, jc=jc, tp=tp: e.tensor_copy(
                        out=asmb[p][:, jc * 128:(jc + 1) * 128], in_=tp_ps[tp][:]),
                        wait=[(PE, pev)], inc=[(DV, 1)])
                    tp_free[tp] = DV.val
                fwd_free[par] = PE.val
                P.dma("gpsimd", dst, asmb[par][:, :C], PP, wait=[(DV, DV.val)])
                asm_free[par] = PP.val

            pp_mark = {}
            ti = 0
            for i in range(16):
                par = ti % 2
                P.dma("sync", wfp[par], wgu_s[i * 128:(i + 1) * 128, :], WG,
                      wait=[(FE, wfp_free[par])])
                quant_tile(par, 2048, True, WG.val)
                col = i * 128 if i < 8 else 1024 + (i - 8) * 128
                transpose_out(par, 2048,
                              wgu_in[:, col:col + 128].rearrange("(jc p) h -> p jc h", p=128))
                ti += 1
            pp_mark["gu"] = PP.val
            for i in range(16):
                par = ti % 2
                P.dma("sync", wfp[par][:, 0:1024], wp_s[i * 128:(i + 1) * 128, :], WG,
                      wait=[(FE, wfp_free[par])])
                quant_tile(par, 1024, True, WG.val)
                transpose_out(par, 1024,
                              wp_in[:, i * 128:(i + 1) * 128].rearrange(
                                  "(hc p) d2 -> p hc d2", p=128))
                ti += 1
            pp_mark["wp"] = PP.val
            for w in range(4):
                for i in range(2):
                    par = ti % 2
                    P.dma("sync", wfp[par], wa_s[w, i * 128:(i + 1) * 128, :], WG,
                          wait=[(FE, wfp_free[par])])
                    quant_tile(par, 2048, False, WG.val)
                    transpose_out(par, 2048,
                                  wa_in[w, :, i * 128:(i + 1) * 128].rearrange(
                                      "(jc p) i2 -> p jc i2", p=128))
                    ti += 1
            pp_mark["wa"] = PP.val

            def ag(inp, outp, ppv):
                P.op("gpsimd", lambda e, i=inp, o=outp: e.collective_compute(
                    "AllGather", ALU.bypass, replica_groups=[list(range(8))],
                    ins=[i.opt()], outs=[o.opt()]),
                    wait=[(PP, ppv)], inc=[(CC, 1)])
                return CC.val
            cc_gu = ag(wgu_in, wgu_all, pp_mark["gu"])
            ag(wp_in, wp_all, pp_mark["wp"])
            cc_w = ag(wa_in.rearrange("w a b -> (w a) b"),
                      wa_all.rearrange("r w a b -> r (w a) b"), pp_mark["wa"])

        yb_free = [0]

        def ln_half(xh, ydst, sqs, yb, yps, yps_free, extra=()):
            xs_x = XS.val
            for t in range(4):
                P.op("vector", lambda e, xh=xh, t=t: e.tensor_reduce(
                    out=stat[:, t:t + 1], in_=xh[:, t * 2048:(t + 1) * 2048],
                    axis=AX.X, op=ALU.add), wait=[(XS, xs_x), (FE, FE.val)],
                    inc=[(FE, 1)])
                P.op("scalar", lambda e, xh=xh, sqs=sqs, t=t: e.activation(
                    out=sqs, in_=xh[:, t * 2048:(t + 1) * 2048], func=AF.Square,
                    accum_out=stat[:, 4 + t:5 + t]),
                    wait=[(XS, xs_x)] + list(extra), inc=[(SC, 1)])
            sc0 = SC.val

            def v(fn, wait=()):
                P.op("vector", fn, wait=list(wait) + [(FE, FE.val)], inc=[(FE, 1)])
            v(lambda e: e.tensor_scalar(out=stat[:, 8:12], in0=stat[:, 0:4],
                                        scalar1=1.0 / D, scalar2=None, op0=ALU.mult),
              wait=[(SC, sc0)])
            v(lambda e: e.tensor_scalar(out=stat[:, 12:16], in0=stat[:, 4:8],
                                        scalar1=1.0 / D, scalar2=None, op0=ALU.mult))
            v(lambda e: e.tensor_tensor(out=stat[:, 16:20], in0=stat[:, 8:12],
                                        in1=stat[:, 8:12], op=ALU.mult))
            v(lambda e: e.tensor_tensor(out=stat[:, 20:24], in0=stat[:, 12:16],
                                        in1=stat[:, 16:20], op=ALU.subtract))
            v(lambda e: e.tensor_scalar(out=stat[:, 24:28], in0=stat[:, 20:24],
                                        scalar1=LN_EPS, scalar2=None, op0=ALU.add))
            v(lambda e: e.reciprocal(out=stat[:, 28:32], in_=stat[:, 24:28]))
            P.op("scalar", lambda e: e.activation(out=stat[:, 32:36], in_=stat[:, 28:32],
                                                  func=AF.Sqrt),
                 wait=[(FE, FE.val)], inc=[(FE, 1)])
            v(lambda e: e.tensor_tensor(out=stat[:, 36:40], in0=stat[:, 32:36],
                                        in1=stat[:, 32:36], op=ALU.mult))
            v(lambda e: e.tensor_tensor(out=stat[:, 40:44], in0=stat[:, 36:40],
                                        in1=stat[:, 24:28], op=ALU.mult))
            v(lambda e: e.tensor_scalar(out=stat[:, 44:48], in0=stat[:, 40:44],
                                        scalar1=-0.5, scalar2=1.5,
                                        op0=ALU.mult, op1=ALU.add))
            v(lambda e: e.tensor_tensor(out=stat[:, 48:52], in0=stat[:, 32:36],
                                        in1=stat[:, 44:48], op=ALU.mult))
            v(lambda e: e.scalar_tensor_tensor(out=stat[:, 52:56], in0=stat[:, 8:12],
                                               scalar=-1.0, in1=stat[:, 48:52],
                                               op0=ALU.mult, op1=ALU.mult))
            fe_ln = FE.val
            for t in range(4):
                P.op("scalar", lambda e, xh=xh, yb=yb, t=t: e.activation(
                    out=yb, in_=xh[:, t * 2048:(t + 1) * 2048], func=AF.Identity,
                    bias=stat[:, 52 + t:53 + t], scale=stat[:, 48 + t:49 + t]),
                    wait=[(FE, fe_ln), (PE, yb_free[0])], inc=[(SC, 1)])
                sc_y = SC.val
                for jc in range(16):
                    tp = jc % 2
                    P.op("tensor", lambda e, yb=yb, yps=yps, jc=jc, tp=tp: e.transpose(
                        yps[tp][:], yb[:, jc * 128:(jc + 1) * 128], ident),
                        wait=[(SC, sc_y), (LD, ldv), (DV, yps_free[tp])],
                        inc=[(PE, 1)])
                    pev = PE.val
                    P.op("vector", lambda e, yps=yps, t=t, jc=jc, tp=tp,
                         ydst=ydst: e.tensor_copy(out=ydst(t, jc), in_=yps[tp][:]),
                        wait=[(PE, pev)], inc=[(DV, 1)])
                    yps_free[tp] = DV.val
                yb_free[0] = PE.val
            return DV.val

        def ffn_pass(tag, src_dram, dst_dram, proj_dram=None):
            P.label = tag
            with ExitStack() as fes:
                fsb = lambda n, s, d: fes.enter_context(nc.sbuf_tensor(tag + n, s, d))[:]
                xh = fsb("xh", [128, 4 * 2048], F32)
                x1h = fsb("x1h", [128, 4 * 2048], F32)
                yb = fsb("yb", [128, 2048], BF16)
                yfm = fsb("yfm", [128, 16 * 512], BF16)
                hfm = fsb("hfm", [128, 64 * 512], BF16)
                sg = [fsb(f"sg{i}", [128, 512], BF16) for i in range(2)]
                h2 = [fsb(f"h2{i}", [128, 512], BF16) for i in range(2)]
                wgu = [fsb(f"wgu{i}", [128, 16 * 256], BF16) for i in range(2)]
                wpb = [fsb(f"wpb{i}", [128, 8 * 512], BF16) for i in range(2)]
                sumr = fsb("sumr", [1, 512], F32)
                invr = fsb("invr", [1, 512], F32)
                hinv = fsb("hinv", [128, 4], F32)
                if proj_dram is not None:
                    pb2 = fsb("pb2", [128, 2048], BF16)

                for half in range(2):
                    hx = f"{half}"
                    P.label = f"{tag}h{half}"
                    bar = P.barrier_vals()
                    for t in range(4):
                        P.dma("scalar", xh[:, t * 2048:(t + 1) * 2048],
                              src_dram[(half * 4 + t) * 128:(half * 4 + t + 1) * 128, :],
                              XS, wait=bar if t == 0 else [])
                    extra = []
                    if proj_dram is not None:
                        fe_pr = FE.val
                        for t in range(4):
                            P.dma("scalar", pb2,
                                  proj_dram[(half * 4 + t) * 128:
                                            (half * 4 + t + 1) * 128, :],
                                  XS, wait=[(FE, fe_pr)])
                            xs_p = XS.val
                            P.op("vector", lambda e, t=t: e.tensor_tensor(
                                out=xh[:, t * 2048:(t + 1) * 2048],
                                in0=xh[:, t * 2048:(t + 1) * 2048], in1=pb2,
                                op=ALU.add),
                                wait=[(XS, xs_p), (FE, FE.val)], inc=[(FE, 1)])
                            fe_pr = FE.val
                        extra = [(FE, fe_pr)]
                    with ExitStack() as lnes:
                        yps = [psum_bank(lnes, nc, f"{tag}yps{hx}{i}",
                                         [128, 128], BF16) for i in range(2)]
                        yps_free = [0, 0]
                        dv_y = ln_half(xh, lambda t, jc: yfm[:, jc * 512 + t * 128:
                                                            jc * 512 + (t + 1) * 128],
                                       x1h[:, 0:2048], yb, yps, yps_free,
                                       extra=extra)
                    with ExitStack() as gues:
                        gps = [psum_bank(gues, nc, f"{tag}gps{hx}{i}",
                                         [128, 512], F32) for i in range(2)]
                        ups = [psum_bank(gues, nc, f"{tag}ups{hx}{i}",
                                         [128, 512], F32) for i in range(2)]
                        rps = psum_bank(gues, nc, f"{tag}rps{hx}", [1, 512], F32)
                        gps_free = [0, 0]
                        ups_free = [0, 0]
                        wgu_free = [0, 0]
                        for p in range(NPAIR):
                            par = p % 2
                            blk, loc = p // 8, (p % 8) * 128
                            P.dma("sync", wgu[par].rearrange(
                                "p (jc x) -> p jc x", x=256)[:, :, 0:128],
                                wgu_all[blk, :, loc:loc + 128].rearrange(
                                    "(jc p) h -> p jc h", p=128), WG,
                                wait=[(CC, cc_gu), (PE, wgu_free[par])])
                            P.dma("sync", wgu[par].rearrange(
                                "p (jc x) -> p jc x", x=256)[:, :, 128:256],
                                wgu_all[blk, :, 1024 + loc:1024 + loc + 128].rearrange(
                                    "(jc p) h -> p jc h", p=128), WG)
                            wg_p = WG.val
                            for jc in range(NJ):
                                P.op("tensor", lambda e, par=par, jc=jc: e.matmul(
                                    gps[par], wgu[par][:, jc * 256:jc * 256 + 128],
                                    yfm[:, jc * 512:(jc + 1) * 512],
                                    start=(jc == 0), stop=(jc == NJ - 1),
                                    skip_group_check=True),
                                    wait=([(WG, wg_p), (DV, dv_y), (SC, gps_free[par])]
                                          if jc == 0 else []))
                            for jc in range(NJ):
                                P.op("tensor", lambda e, par=par, jc=jc: e.matmul(
                                    ups[par], wgu[par][:, jc * 256 + 128:jc * 256 + 256],
                                    yfm[:, jc * 512:(jc + 1) * 512],
                                    start=(jc == 0), stop=(jc == NJ - 1),
                                    skip_group_check=True),
                                    wait=([(DV, ups_free[par])] if jc == 0 else []),
                                    inc=[(PE, 1)] if jc == NJ - 1 else [])
                            wgu_free[par] = PE.val
                            pe_p = PE.val
                            P.op("scalar", lambda e, par=par: e.activation(
                                out=sg[par], in_=gps[par], func=AF.Silu),
                                wait=[(PE, pe_p)], inc=[(SC, 1)])
                            gps_free[par] = SC.val
                            sc_sg = SC.val
                            P.op("vector", lambda e, par=par, p=p: e.tensor_tensor(
                                out=hfm[:, p * 512:(p + 1) * 512], in0=sg[par],
                                in1=ups[par], op=ALU.mult),
                                wait=[(SC, sc_sg)], inc=[(DV, 1)])
                            ups_free[par] = DV.val
                            dv_h = DV.val
                            P.op("scalar", lambda e, par=par, p=p: e.activation(
                                out=h2[par], in_=hfm[:, p * 512:(p + 1) * 512],
                                func=AF.Square),
                                wait=[(DV, dv_h), (PE, PE.val)], inc=[(SC, 1)])
                            sc_h2 = SC.val
                            P.op("tensor", lambda e, par=par, p=p: e.matmul(
                                rps, ones_col_b, h2[par], start=(p == 0),
                                stop=(p == NPAIR - 1), skip_group_check=True),
                                wait=[(SC, sc_h2)],
                                inc=[(PE, 1)])
                        dv_hall = DV.val
                        pe_rs = PE.val
                        def vr(fn, wait=()):
                            P.op("vector", fn, wait=list(wait) + [(FE, FE.val)],
                                 inc=[(FE, 1)])
                        vr(lambda e: e.tensor_copy(out=sumr, in_=rps),
                           wait=[(PE, pe_rs)])
                    vr2 = lambda fn, wait=(): P.op(
                        "vector", fn, wait=list(wait) + [(FE, FE.val)], inc=[(FE, 1)])
                    A, B, C, Dv = (row[0:1, 0:512], row[0:1, 512:1024],
                                   row[0:1, 1024:1536], row[0:1, 1536:2048])
                    vr2(lambda e: e.tensor_scalar(out=A, in0=sumr,
                                                  scalar1=1.0 / FF, scalar2=RMS_EPS,
                                                  op0=ALU.mult, op1=ALU.add))
                    vr2(lambda e: e.reciprocal(out=B, in_=A))
                    P.op("scalar", lambda e: e.activation(
                        out=C, in_=B, func=AF.Sqrt),
                        wait=[(FE, FE.val)], inc=[(FE, 1)])
                    vr2(lambda e: e.tensor_tensor(out=B, in0=C, in1=C, op=ALU.mult))
                    vr2(lambda e: e.tensor_tensor(out=Dv, in0=B, in1=A, op=ALU.mult))
                    vr2(lambda e: e.tensor_scalar(out=B, in0=Dv,
                                                  scalar1=-0.5, scalar2=1.5,
                                                  op0=ALU.mult, op1=ALU.add))
                    vr2(lambda e: e.tensor_tensor(out=invr, in0=C, in1=B, op=ALU.mult))
                    fe_inv = FE.val
                    with ExitStack() as pjes:
                        pj = [psum_bank(pjes, nc, f"{tag}pj{hx}{i}",
                                        [128, 512], F32) for i in range(4)]
                        tps2 = psum_bank(pjes, nc, f"{tag}tps2{hx}", [128, 4], F32)
                        for t in range(4):
                            P.op("tensor", lambda e, t=t: e.matmul(
                                tps2[:, t:t + 1], invr[0:1, t * 128:(t + 1) * 128],
                                ones_row_f[0:1, 0:1], start=(t == 0), stop=(t == 3),
                                skip_group_check=True),
                                wait=[(FE, fe_inv)] if t == 0 else [],
                                inc=[(PE, 1)] if t == 3 else [])
                        pe_ti = PE.val
                        P.op("vector", lambda e: e.tensor_scalar(
                            out=hinv, in0=tps2, scalar1=0.5, scalar2=None,
                            op0=ALU.mult),
                            wait=[(PE, pe_ti), (FE, FE.val)], inc=[(FE, 1)])
                        fe_hinv = FE.val
                        wpb_free = [0, 0]
                        pj_free = [0, 0, 0, 0]
                        for dt in range(4):
                            for sci in range(8):
                                par = (dt * 8 + sci) % 2
                                P.dma("sync", wpb[par],
                                      wp_all[sci, :, dt * 512:(dt + 1) * 512].rearrange(
                                          "(hc p) d2 -> p hc d2", p=128), WG,
                                      wait=[(CC, cc_w), (PE, wpb_free[par])])
                                wg_pj = WG.val
                                for t in range(4):
                                    for jc in range(8):
                                        pp = sci * 8 + jc
                                        P.op("tensor", lambda e, t=t, par=par, jc=jc,
                                             pp=pp, sci=sci: e.matmul(
                                                 pj[t], hfm[:, pp * 512 + t * 128:
                                                            pp * 512 + (t + 1) * 128],
                                                 wpb[par][:, jc * 512:(jc + 1) * 512],
                                                 start=(sci == 0 and jc == 0),
                                                 stop=(sci == 7 and jc == 7),
                                                 skip_group_check=True),
                                            wait=(([(WG, wg_pj), (DV, dv_hall)]
                                                   if t == 0 else [])
                                                  + [(DV, pj_free[t])]
                                                  if jc == 0 and sci == 0 else
                                                  ([(WG, wg_pj)] if jc == 0 and
                                                   t == 0 else [])),
                                            inc=([(PE, 1)] if (t == 3 and jc == 7)
                                                 else []))
                                wpb_free[par] = PE.val
                            pe_dt = PE.val
                            for t in range(4):
                                P.op("vector", lambda e, dt=dt, t=t:
                                     e.scalar_tensor_tensor(
                                         out=x1h[:, t * 2048 + dt * 512:
                                                 t * 2048 + (dt + 1) * 512],
                                         in0=pj[t], scalar=hinv[:, t:t + 1],
                                         in1=xh[:, t * 2048 + dt * 512:
                                                t * 2048 + (dt + 1) * 512],
                                         op0=ALU.mult, op1=ALU.add),
                                     wait=[(PE, pe_dt), (FE, fe_hinv)], inc=[(DV, 1)])
                                pj_free[t] = DV.val
                    dv_x1 = DV.val
                    for t in range(4):
                        P.dma("gpsimd", dst_dram[(half * 4 + t) * 128:
                                                 (half * 4 + t + 1) * 128, :],
                              x1h[:, t * 2048:(t + 1) * 2048], AC,
                              wait=[(DV, dv_x1)])
                    dram_fence("gpsimd", dst_dram[half * 512:half * 512 + 128, 0:8],
                               AC)
                    if DEBUG and tag == "f1" and half == 0:
                        P.dma("gpsimd", dbg_yfm, yfm, AC, wait=[(DV, dv_x1)])
                        P.dma("gpsimd", dbg_hfm, hfm, AC)
                        P.dma("gpsimd", dbg_stat, stat, AC)
                        P.dma("gpsimd", dbg_hinv, hinv, AC)

        ffn_pass("f1", xin, xdram)

        if STAGE in ("x2", "full"):
            P.label = "qkv"
            with ExitStack() as qes:
                qsb = lambda n, s, d: qes.enter_context(nc.sbuf_tensor(n, s, d))[:]
                qps = lambda n, s, d: psum_bank(qes, nc, n, s, d)
                y2fm = [qsb(f"y2fm{i}", [128, 16 * 512], BF16) for i in range(2)]
                wat = qsb("wat", [128, 16 * 2048], BF16)
                k2t = qsb("k2t", [128, 512], BF16)
                knr = qsb("knr", [1, 64], F32)
                qnr = qsb("qnr", [1, 64], F32)
                bv_bc = qsb("bv_bc", [128, D], F32)
                mrow = qsb("mrow", [1, 512], F32)
                P.dma("sync", bv_bc, bv_bc_i, LD)
                kvps = [qps(f"kvps{i}", [128, 512], F32) for i in range(2)]
                nps = qps("nps", [1, 512], F32)
                mps = qps("mps", [128, NH], F32)
                yps = [qps(f"qyps{i}", [128, 128], BF16) for i in range(2)]
                yps_free = [0, 0]

                lnes2 = ExitStack()
                with lnes2:
                    lsb = lambda n, s, d: lnes2.enter_context(
                        nc.sbuf_tensor(n, s, d))[:]
                    xh = lsb("qxh", [128, 4 * 2048], F32)
                    yb = lsb("qyb", [128, 2048], BF16)
                    qsqs = lsb("qsqs", [128, 2048], F32)
                    for half in range(2):
                        bar = P.barrier_vals()
                        for t in range(4):
                            P.dma("scalar", xh[:, t * 2048:(t + 1) * 2048],
                                  xdram[(half * 4 + t) * 128:
                                        (half * 4 + t + 1) * 128, :],
                                  XS, wait=bar if t == 0 else [])
                        ln_half(xh, lambda t, jc, h=half: y2fm[h][:, jc * 512 +
                                                              t * 128:jc * 512 +
                                                              (t + 1) * 128],
                                qsqs, yb, yps, yps_free)
                        if DEBUG and half == 1:
                            P.dma("gpsimd", dbg_qst, stat, AC,
                                  wait=[(DV, DV.val), (SC, SC.val)])
                            P.dma("gpsimd", dbg_yb, yb, AC)
                            P.dma("gpsimd", dbg_xh, xh[:, 0:2048], AC)
                dv_y2 = DV.val
                kqpan = qsb("kqpan", [128, 32 * 512], BF16)
                vpan = qsb("vpan", [128, 32 * 512], BF16)
                if DEBUG:
                    P.dma("gpsimd", dbg_y2[:, 0:16 * 512], y2fm[0], AC,
                          wait=[(DV, dv_y2)])
                    P.dma("gpsimd", dbg_y2[:, 16 * 512:], y2fm[1], AC)

                def load_w(widx, wv):
                    for r in range(8):
                        P.dma("sync", wat.rearrange("p (jc i) -> p jc i", i=2048)
                              [:, :, 256 * r:256 * (r + 1)],
                              wa_all[r, widx, :, :].rearrange(
                                  "(jc p) i -> p jc i", p=128),
                              WG, wait=wv if r == 0 else [])
                    return WG.val

                kvps_free = [[], []]
                nrm = [0]
                cc_k = [0]

                for widx, kind in [(1, "k"), (2, "v"), (0, "q")]:
                    wg_w = load_w(widx, [(CC, cc_w), (PE, PE.val)])
                    if kind in ("k", "q"):
                        for ic in range(16):
                            for th in range(2):
                                par = (ic * 2 + th) % 2
                                sl = kqpan[:, (ic * 2 + th) * 512:
                                           (ic * 2 + th + 1) * 512]
                                for jc in range(16):
                                    P.op("tensor", lambda e, par=par, ic=ic, th=th,
                                         jc=jc: e.matmul(
                                             kvps[par], wat[:, jc * 2048 + ic * 128:
                                                            jc * 2048 + (ic + 1) * 128],
                                             y2fm[th][:, jc * 512:(jc + 1) * 512],
                                             start=(jc == 0), stop=(jc == 15),
                                             skip_group_check=True),
                                        wait=([(WG, wg_w), (DV, dv_y2)]
                                              + kvps_free[par] if jc == 0 else []),
                                        inc=[(PE, 1)] if jc == 15 else [])
                                pe_kv = PE.val
                                bcol = (16 if kind == "k" else 0) + ic
                                P.op("scalar", lambda e, par=par, bcol=bcol,
                                     sl=sl: e.activation(
                                         out=sl, in_=kvps[par], func=AF.Identity,
                                         bias=bqk_tm[:, bcol:bcol + 1]),
                                    wait=[(PE, pe_kv)]
                                    + ([(CC, cc_k[0])] if kind == "q"
                                       and ic == 0 and th == 0 else []),
                                    inc=[(SC, 1)])
                                kvps_free[par] = [(SC, SC.val)]
                                sc_kv = SC.val
                                kq_dst = k_in if kind == "k" else qdram
                                P.dma("sync", kq_dst[ic * 128:(ic + 1) * 128,
                                                     th * 512:(th + 1) * 512],
                                      sl, AC, wait=[(SC, sc_kv)])
                                P.op("scalar", lambda e, sl=sl: e.activation(
                                    out=k2t, in_=sl, func=AF.Square),
                                    wait=[(SC, sc_kv)], inc=[(SC, 1)])
                                sc_k2 = SC.val
                                P.op("tensor", lambda e: e.matmul(
                                    nps, ones_col_b, k2t, start=True, stop=True,
                                    skip_group_check=True),
                                    wait=[(SC, sc_k2), (FE, nrm[0])], inc=[(PE, 1)])
                                pe_n = PE.val
                                tgt = knr if kind == "k" else qnr
                                P.op("vector", lambda e, t2=tgt, ic=ic, th=th:
                                     e.tensor_reduce(
                                         out=t2[0:1, ic * 2 + th:ic * 2 + th + 1],
                                         in_=nps, axis=AX.X, op=ALU.max),
                                     wait=[(PE, pe_n), (FE, FE.val)], inc=[(FE, 1)])
                                nrm[0] = FE.val
                    else:
                        for t in range(8):
                            hh, tt = t // 4, t % 4
                            for iv in range(4):
                                par = (t * 4 + iv) % 2
                                vsl = vpan[:, (t * 4 + iv) * 512:
                                           (t * 4 + iv + 1) * 512]
                                for jc in range(16):
                                    P.op("tensor", lambda e, par=par, hh=hh, tt=tt,
                                         iv=iv, jc=jc: e.matmul(
                                             kvps[par],
                                             y2fm[hh][:, jc * 512 + tt * 128:
                                                      jc * 512 + (tt + 1) * 128],
                                             wat[:, jc * 2048 + iv * 512:
                                                 jc * 2048 + (iv + 1) * 512],
                                             start=(jc == 0), stop=(jc == 15),
                                             skip_group_check=True),
                                        wait=([(WG, wg_w)] + kvps_free[par]
                                              if jc == 0 else []),
                                        inc=[(PE, 1)] if jc == 15 else [])
                                pe_v = PE.val
                                P.op("vector", lambda e, par=par, iv=iv,
                                     vsl=vsl: e.tensor_tensor(
                                         out=vsl, in0=kvps[par],
                                         in1=bv_bc[:, iv * 512:(iv + 1) * 512],
                                         op=ALU.add),
                                     wait=[(PE, pe_v), (FE, FE.val)],
                                     inc=[(DV, 1)])
                                kvps_free[par] = [(DV, DV.val)]
                                dv_v = DV.val
                                P.dma("sync", v_in[t * 128:(t + 1) * 128,
                                                   iv * 512:(iv + 1) * 512],
                                      vsl, AC, wait=[(DV, dv_v)])
                    if kind == "k":
                        P.op("vector", lambda e: e.tensor_reduce(
                            out=mrow[0:1, 224:240],
                            in_=knr[0:1, 0:32].rearrange("x (h t2) -> x h t2", t2=2),
                            axis=AX.X, op=ALU.max),
                            wait=[(FE, nrm[0]), (FE, FE.val)], inc=[(FE, 1)])
                        fe_kn = FE.val
                        P.dma("gpsimd", kn_in, mrow[0:1, 224:240], AC,
                              wait=[(FE, fe_kn)])
                        dram_fence("sync", k_in[0:128, 0:8], AC)
                        ac_k = AC.val
                        P.op("gpsimd", lambda e: e.collective_compute(
                            "AllGather", ALU.bypass, replica_groups=[list(range(8))],
                            ins=[k_in.opt()], outs=[k_all.opt()]),
                            wait=[(AC, ac_k)], inc=[(CC, 1)])
                        P.op("gpsimd", lambda e: e.collective_compute(
                            "AllGather", ALU.bypass, replica_groups=[list(range(8))],
                            ins=[kn_in.opt()], outs=[kn_all.opt()]),
                            inc=[(CC, 1)])
                        cc_k[0] = CC.val
                    if kind == "v":
                        dram_fence("sync", v_in[0:128, 0:8], AC)
                        ac_v = AC.val
                        P.op("gpsimd", lambda e: e.collective_compute(
                            "AllGather", ALU.bypass, replica_groups=[list(range(8))],
                            ins=[v_in.opt()], outs=[v_all.opt()]),
                            wait=[(AC, ac_v)], inc=[(CC, 1)])
                dram_fence("sync", qdram[0:128, 0:8], AC)
                cc_kv = CC.val
                ac_qkv = AC.val

                def vm(fn, wait=()):
                    P.op("vector", fn, wait=list(wait) + [(FE, FE.val)], inc=[(FE, 1)])
                P.dma("scalar", mrow[0:1, 0:128].rearrange(
                    "x (h r) -> x h r", r=8),
                    kn_all.rearrange("r x h -> x h r"), XS, wait=[(CC, cc_kv)],
                    allow_slow_non_contiguous=True)
                xs_kn = XS.val
                vm(lambda e: e.tensor_reduce(
                    out=mrow[0:1, 128:144],
                    in_=mrow[0:1, 0:128].rearrange("x (h r) -> x h r", r=8),
                    axis=AX.X, op=ALU.max), wait=[(XS, xs_kn)])
                vm(lambda e: e.tensor_reduce(
                    out=mrow[0:1, 144:160],
                    in_=qnr[0:1, 0:32].rearrange("x (h t2) -> x h t2", t2=2),
                    axis=AX.X, op=ALU.max), wait=[(FE, nrm[0])])
                vm(lambda e: e.tensor_tensor(
                    out=mrow[0:1, 160:176], in0=mrow[0:1, 128:144],
                    in1=mrow[0:1, 144:160], op=ALU.mult))
                vm(lambda e: e.tensor_scalar(
                    out=mrow[0:1, 176:192], in0=mrow[0:1, 160:176],
                    scalar1=1.0 / HD, scalar2=None, op0=ALU.mult))
                P.op("scalar", lambda e: e.activation(
                    out=mrow[0:1, 192:208], in_=mrow[0:1, 176:192], func=AF.Sqrt),
                    wait=[(FE, FE.val)], inc=[(FE, 1)])
                vm(lambda e: e.tensor_scalar(
                    out=mrow[0:1, 208:224], in0=mrow[0:1, 192:208],
                    scalar1=-1.0, scalar2=None, op0=ALU.mult))
                fe_m = FE.val
                P.op("tensor", lambda e: e.matmul(
                    mps, ones_row_f, mrow[0:1, 208:224], start=True, stop=True,
                    skip_group_check=True), wait=[(FE, fe_m)], inc=[(PE, 1)])
                pe_m = PE.val
                P.op("vector", lambda e: e.tensor_copy(out=negm, in_=mps),
                     wait=[(PE, pe_m), (FE, FE.val)], inc=[(FE, 1)])
                fe_negm = FE.val
                if DEBUG:
                    P.dma("gpsimd", dbg_negm, negm, AC, wait=[(FE, fe_negm)])
                    P.dma("gpsimd", dbg_mrow, mrow, AC)
                    P.dma("gpsimd", dbg_qn, qnr, AC)
                    P.dma("gpsimd", dbg_q, qdram, AC, wait=[(AC, ac_qkv)])
                    P.dma("gpsimd", dbg_k, k_all[0:2], AC, wait=[(CC, cc_kv)])
                    P.dma("gpsimd", dbg_v, v_all[:, 0:128, :], AC)

            P.label = "attn"
            avsb_es = ExitStack()
            av_sb = [avsb_es.enter_context(
                nc.sbuf_tensor(f"avsb{i}", [128, NH * 512], F32))[:]
                for i in range(2)]
            with ExitStack() as aes:
                asb = lambda n, s, d: aes.enter_context(nc.sbuf_tensor(n, s, d))[:]
                apsf = lambda n, s, d: psum_bank(aes, nc, n, s, d)
                ks = [asb(f"ks{i}", [128, 16 * 512], BF16) for i in range(2)]
                vs = [asb(f"vs{i}", [128, 4 * 2048], BF16) for i in range(2)]
                qs = [asb(f"qs{i}", [128, 16 * 512], BF16) for i in range(2)]
                pb = [asb(f"pb{i}", [128, 512], BF16) for i in range(4)]
                pm = [asb(f"pm{i}", [128, 512], BF16) for i in range(4)]
                fencb = asb("fencb", [128, 8], BF16)
                cms = asb("cms", [128, 4 * 512], BF16)
                P.dma("sync", cms, cmi, LD)
                ldc = LD.val
                sut = [asb(f"sut{i}", [1, 512], F32) for i in range(2)]
                sps = [apsf(f"sps{i}", [128, 512], F32) for i in range(2)]
                avp = [apsf(f"avp{i}", [128, 512], F32) for i in range(2)]
                sup = [apsf(f"sup{i}", [128, 512], F32) for i in range(2)]

                kq_free = [0, 0]
                v_free = [0, 0]
                sps_free = [0, 0]
                avp_free = [0, 0]
                sup_free = [0, 0]
                sut_free = [0, 0]
                pb_free = [0, 0, 0, 0]
                pm_free = [0, 0, 0, 0]

                KVK = P.sem("KVK")
                KVV = P.sem("KVV")
                KVQ = P.sem("KVQ")

                def dyn_load(eng, t, col, buf_ap, ap_dims, sem):
                    sh = sem.h

                    def fn(e, t=t, col=col, buf_ap=buf_ap, ap_dims=ap_dims, sh=sh):
                        with e.register(f"r{eng}{t}") as r:
                            e.reg_load(r, tabs[0:1, t * 8 + col:t * 8 + col + 1])
                            roff = e.snap(r, True, min_val=0,
                                          max_val=8 * 2048 * 1024)
                            src_t = k_all.tensor if col == 0 else (
                                v_all.tensor if col == 1 else qdram.tensor)
                            e.dma_start(out=buf_ap,
                                        in_=RawAP(src_t, roff, ap_dims)
                                        ).then_inc(sh, 16)
                            return e.dma_start(
                                out=fencb,
                                in_=RawAP(src_t, roff, [[ap_dims[0][0], 128],
                                                        [1, 8]]))
                    P.op(eng, fn, wait=[(CC, cc_kv), (AC, ac_qkv),
                                        (PE, kq_free[t % 2] if col != 1
                                         else v_free[t % 2])],
                         inc=[(sem, 16)], ghost_inc=[(sem, 16)])
                    return sem.val

                K_AP = [[1024, 128], [128 * 1024, 16], [1, 512]]
                V_AP = [[2048, 128], [128 * 2048, 4], [1, 2048]]
                Q_AP = [[1024, 128], [128 * 1024, 16], [1, 512]]

                def load_slot(t):
                    par = t % 2
                    a = dyn_load("sync", t, 0,
                                 ks[par].rearrange("p (ic t2) -> p ic t2", t2=512),
                                 K_AP, KVK)
                    b = dyn_load("gpsimd", t, 1,
                                 vs[par].rearrange("p (kc d2) -> p kc d2", d2=2048),
                                 V_AP, KVV)
                    c = dyn_load("scalar", t, 2,
                                 qs[par].rearrange("p (ic t2) -> p ic t2", t2=512),
                                 Q_AP, KVQ)
                    return [(KVK, a), (KVV, b), (KVQ, c)]

                kv_ld = [None] * (NSLOT + 1)
                kv_ld[0] = load_slot(0)
                for t in range(NSLOT):
                    par = t % 2
                    if t + 1 < NSLOT:
                        kv_ld[t + 1] = load_slot(t + 1)
                    diag = t < 2
                    for h in range(NH):
                        hp = h % 2

                        def a_score(kc, par=par, h=h):
                            sp = kc % 2
                            P.op("tensor", lambda e, par=par, h=h, kc=kc, sp=sp:
                                 e.matmul(
                                     sps[sp], ks[par][:, h * 512 + kc * 128:
                                                      h * 512 + (kc + 1) * 128],
                                     qs[par][:, h * 512:(h + 1) * 512],
                                     start=True, stop=True,
                                     skip_group_check=True),
                                 wait=kv_ld[t] + [(SC, sps_free[sp]),
                                                  (FE, fe_negm)],
                                 inc=[(PE, 1)])
                            return PE.val

                        def a_exp(kc, pe_s, h=h):
                            sp = kc % 2
                            P.op("scalar", lambda e, sp=sp, kc=kc, h=h:
                                 e.activation(
                                     out=pb[kc], in_=sps[sp], func=AF.Exp,
                                     bias=negm[:, h:h + 1], scale=SM_SCALE),
                                 wait=[(PE, pe_s), (PE, pb_free[kc])],
                                 inc=[(SC, 1)])
                            sps_free[sp] = SC.val
                            sc_e = SC.val
                            if diag:
                                P.op("vector", lambda e, kc=kc: e.tensor_tensor(
                                    out=pm[kc], in0=pb[kc],
                                    in1=cms[:, kc * 512:(kc + 1) * 512],
                                    op=ALU.mult),
                                    wait=[(SC, sc_e), (LD, ldc),
                                          (PE, pm_free[kc])],
                                    inc=[(DV, 1)])
                                return pm, [(DV, DV.val)]
                            return pb, [(SC, sc_e)]

                        def a_avsu(kc, src2, wv, par=par, h=h, hp=hp):
                            P.op("tensor", lambda e, par=par, h=h, kc=kc,
                                 s2=src2, hp=hp: e.matmul(
                                     avp[hp], vs[par][:, kc * 2048 + h * 128:
                                                      kc * 2048 + (h + 1) * 128],
                                     s2[kc], start=(kc == 0), stop=(kc == 3),
                                     skip_group_check=True),
                                 wait=wv + ([(DV, avp_free[hp])] + kv_ld[t]
                                            if kc == 0 else []))
                            P.op("tensor", lambda e, kc=kc, s2=src2, hp=hp:
                                 e.matmul(
                                     sup[hp], cbs[:, 129:257], s2[kc],
                                     start=(kc == 0), stop=(kc == 3),
                                     skip_group_check=True),
                                 wait=([(DV, sup_free[hp])] if kc == 0 else []),
                                 inc=[(PE, 1)])
                            pb_free[kc] = PE.val
                            pm_free[kc] = PE.val

                        pe_s = [None] * 4
                        pe_s[0] = a_score(0)
                        pe_s[1] = a_score(1)
                        for kc in range(4):
                            src2, wv = a_exp(kc, pe_s[kc])
                            if kc + 2 < 4:
                                pe_s[kc + 2] = a_score(kc + 2)
                            a_avsu(kc, src2, wv)
                        pe_av = PE.val
                        for qb2 in range(2):
                            if t == 0:
                                P.op("vector", lambda e, hp=hp, qb2=qb2, t=t, h=h:
                                     e.tensor_scalar(
                                         out=av_sb[qb2][:, h * 512:(h + 1) * 512],
                                         in0=avp[hp],
                                         scalar1=wts[:, qb2 * NSLOT + t:
                                                     qb2 * NSLOT + t + 1],
                                         scalar2=None, op0=ALU.mult),
                                     wait=([(PE, pe_av), (LD, ldv)]
                                           if qb2 == 0 else []),
                                     inc=[(DV, 1)])
                            else:
                                P.op("vector", lambda e, hp=hp, qb2=qb2, t=t, h=h:
                                     e.scalar_tensor_tensor(
                                         out=av_sb[qb2][:, h * 512:(h + 1) * 512],
                                         in0=avp[hp],
                                         scalar=wts[:, qb2 * NSLOT + t:
                                                    qb2 * NSLOT + t + 1],
                                         in1=av_sb[qb2][:, h * 512:(h + 1) * 512],
                                         op0=ALU.mult, op1=ALU.add),
                                     wait=([(PE, pe_av), (LD, ldv)]
                                           if qb2 == 0 else []),
                                     inc=[(DV, 1)])
                        avp_free[hp] = DV.val
                        P.op("vector", lambda e, hp=hp: e.tensor_copy(
                            out=sut[hp], in_=sup[hp][0:1, :]),
                            wait=[(AC, sut_free[hp])], inc=[(DV, 1)])
                        sup_free[hp] = DV.val
                        dv_su = DV.val
                        P.dma("gpsimd", sudram[t, h, :].unsqueeze(0), sut[hp], AC,
                              wait=[(DV, dv_su)])
                        dram_fence("gpsimd", sudram[t, h, 0:8].unsqueeze(0), AC)
                        sut_free[hp] = AC.val
                    kq_free[par] = PE.val
                    v_free[par] = PE.val
                if DEBUG:
                    P.dma("gpsimd", dbg_av[0], av_sb[0], AC,
                          wait=[(DV, DV.val)])
                    P.dma("gpsimd", dbg_av[1], av_sb[1], AC)
                    P.dma("gpsimd", dbg_wts, wts, AC)
                    P.dma("gpsimd", dbg_tab, tabs, AC)

            P.label = "oproj"
            dram_fence("gpsimd", sudram[NSLOT - 1, NH - 1, 0:8].unsqueeze(0), AC)
            with ExitStack() as oes:
                osb = lambda n, s, d: oes.enter_context(nc.sbuf_tensor(n, s, d))[:]
                opsf = lambda n, s, d: psum_bank(oes, nc, n, s, d)
                wos = osb("wos", [128, 16 * 2048], BF16)
                sus2 = [osb(f"sus{i}", [NSLOT, 512], F32) for i in range(2)]
                avb = osb("avb", [128, 16 * 512], BF16)
                pout = osb("pout", [128, 8 * 2048], BF16)
                bo_bc = osb("bo_bc", [128, D], F32)
                P.dma("sync", bo_bc, bo_bc_i, LD)
                ops = [opsf(f"ops{i}", [128, 512], F32) for i in range(2)]
                ibp = opsf("ibp", [128, 512], F32)
                sps2 = opsf("sps2", [1, 512], F32)
                jnk = opsf("jnk", [128, 512], F32)

                def pe_pad():
                    P.op("tensor", lambda e: e.matmul(
                        jnk, ident, cbs[:, 0:512], start=True, stop=True,
                        skip_group_check=True), inc=[(PE, 1)])
                    return PE.val

                ac_att = AC.val
                bar_o = P.barrier_vals()
                for r in range(8):
                    P.dma("sync", wos.rearrange("p (jc i) -> p jc i", i=2048)
                          [:, :, 256 * r:256 * (r + 1)],
                          wa_all[r, 3, :, :].rearrange("(jc p) i -> p jc i", p=128),
                          WG, wait=bar_o if r == 0 else [])
                wg_o = WG.val
                if DEBUG:
                    P.dma("gpsimd", dbg_su, sudram, AC, wait=[(AC, ac_att)])
                sus_xs = [0, 0]
                sus_pe = [0, 0]

                def sus_load(pos, h):
                    sl = pos % 2
                    P.dma("scalar", sus2[sl], sudram[:, h, :], XS,
                          wait=[(AC, ac_att), (PE, sus_pe[sl])])
                    sus_xs[sl] = XS.val

                avb_free = 0
                ops_free = [0, 0]
                dbg_gate = []
                for qb in range(2):
                    dv_att = DV.val
                    ibp_free = 0
                    row_free = 0
                    sps2_free = 0
                    hseq = ([0] if qb == 0 else []) + list(range(NH))
                    sus_load(0, hseq[0])
                    for pos, h in enumerate(hseq):
                        sl = pos % 2
                        if pos + 1 < len(hseq):
                            sus_load(pos + 1, hseq[pos + 1])
                        P.op("tensor", lambda e, qb=qb, sl=sl: e.matmul(
                            sps2, wtv[:, qb:qb + 1], sus2[sl],
                            start=True, stop=True, skip_group_check=True),
                            wait=[(XS, sus_xs[sl]), (LD, LD.val),
                                  (FE, sps2_free)],
                            inc=[(PE, 1)])
                        sus_pe[sl] = PE.val
                        pe_su = pe_pad()
                        P.op("vector", lambda e: e.reciprocal(
                            out=row[0:1, 0:512], in_=sps2[0:1, :]),
                            wait=[(PE, pe_su), (FE, FE.val), (PE, row_free)],
                            inc=[(FE, 1)])
                        fe_i = FE.val
                        sps2_free = fe_i
                        P.op("tensor", lambda e: e.matmul(
                            ibp, ones_row_f, row[0:1, 0:512], start=True, stop=True,
                            skip_group_check=True),
                            wait=[(FE, fe_i), (DV, ibp_free)], inc=[(PE, 1)])
                        pe_i = pe_pad()
                        row_free = pe_i
                        P.op("vector", lambda e, qb=qb, h=h: e.tensor_tensor(
                            out=avb[:, h * 512:(h + 1) * 512],
                            in0=av_sb[qb][:, h * 512:(h + 1) * 512], in1=ibp,
                            op=ALU.mult),
                            wait=[(PE, pe_i), (DV, dv_att), (PE, avb_free)]
                            + dbg_gate,
                            inc=[(DV, 1)])
                        ibp_free = DV.val
                    dv_avb = DV.val
                    if DEBUG and qb == 0:
                        P.dma("gpsimd", dbg_avb, avb, AC, wait=[(DV, dv_avb)])
                        dbg_gate = [(AC, AC.val)]
                    for t in range(4):
                        gt = qb * 4 + t
                        psl = pout[:, gt * 2048:(gt + 1) * 2048]
                        for i in range(4):
                            par = i % 2
                            for h in range(NH):
                                P.op("tensor", lambda e, t=t, i=i, h=h, par=par:
                                     e.matmul(
                                         ops[par], avb[:, h * 512 + t * 128:
                                                       h * 512 + (t + 1) * 128],
                                         wos[:, h * 2048 + i * 512:
                                             h * 2048 + (i + 1) * 512],
                                         start=(h == 0), stop=(h == 15),
                                         skip_group_check=True),
                                     wait=([(WG, wg_o), (DV, dv_avb),
                                            (DV, ops_free[par])] if h == 0 else []),
                                     inc=[(PE, 1)] if h == 15 else [])
                            pe_o = PE.val
                            P.op("vector", lambda e, i=i, par=par, psl=psl:
                                 e.scalar_tensor_tensor(
                                     out=psl[:, i * 512:(i + 1) * 512],
                                     in0=ops[par], scalar=1.0,
                                     in1=bo_bc[:, i * 512:(i + 1) * 512],
                                     op0=ALU.mult, op1=ALU.add),
                                 wait=[(PE, pe_o), (LD, LD.val)],
                                 inc=[(DV, 1)])
                            ops_free[par] = DV.val
                        dv_p = DV.val
                        P.dma("sync", podram[gt * 128:(gt + 1) * 128, :], psl, AC,
                              wait=[(DV, dv_p)])
                    avb_free = PE.val
                dram_fence("sync", podram[0:128, 0:8], AC)
            avsb_es.close()

        if STAGE == "full":
            ffn_pass("f2", xdram, out, proj_dram=podram)
        else:
            bar = P.barrier_vals()
            with ExitStack() as x2es:
                x2b = x2es.enter_context(
                    nc.sbuf_tensor("x2asm", [128, 2048], F32))[:]
                p2b = x2es.enter_context(
                    nc.sbuf_tensor("p2asm", [128, 2048], BF16))[:]
                fe_a = FE.val
                for i in range(8):
                    P.dma("scalar", x2b, xdram[i * 128:(i + 1) * 128, :], XS,
                          wait=(bar if i == 0 else []) + [(AC, AC.val)])
                    P.dma("scalar", p2b, podram[i * 128:(i + 1) * 128, :], XS)
                    xs_a = XS.val
                    P.op("vector", lambda e: e.tensor_tensor(
                        out=x2b, in0=x2b, in1=p2b, op=ALU.add),
                        wait=[(XS, xs_a), (FE, fe_a)], inc=[(FE, 1)])
                    fe_a = FE.val
                    P.dma("gpsimd", out[i * 128:(i + 1) * 128, :], x2b, AC,
                          wait=[(FE, fe_a)])
        dram_fence("gpsimd", out[0:128, 0:8], AC)
        P.op("gpsimd", None, wait=P.barrier_vals())

        P.check_schedule()
        blk = es.enter_context(nc.Block())
        P.replay(blk)
    return nc


_CACHE = {}


def kernel(**inputs):
    x = np.asarray(inputs["L0_attn_hidden_states"]).reshape(S, D)
    wgu = np.asarray(inputs["L0_ffn_w_gate_up"])
    wp = np.asarray(inputs["L0_ffn_w_proj"])
    cbn, cfn, cmn = host_constants()
    bq = np.asarray(inputs["L0_attn_bq"]).astype(np.float32)
    bk = np.asarray(inputs["L0_attn_bk"]).astype(np.float32)
    bv = np.asarray(inputs["L0_attn_bv"]).astype(np.float32)
    bo = np.asarray(inputs["L0_attn_bo"]).astype(np.float32)
    bqk_tm = np.concatenate([bq.reshape(16, 128).T, bk.reshape(16, 128).T], 1)
    in_maps = []
    for c in range(NCORE):
        rows = core_rows(c)
        gate = wgu[1024 * c:1024 * (c + 1), :]
        up = wgu[8192 + 1024 * c:8192 + 1024 * (c + 1), :]
        m = {
            "xin": np.ascontiguousarray(x[rows]),
            "wgu_s": np.ascontiguousarray(np.concatenate([gate, up], 0)),
            "wp_s": np.ascontiguousarray(wp[:, 1024 * c:1024 * (c + 1)]),
            "wa_s": np.ascontiguousarray(np.stack(
                [np.asarray(inputs[f"L0_attn_W{w}"])[256 * c:256 * (c + 1), :]
                 for w in "qkvo"])),
            "bqk_tm_i": np.ascontiguousarray(bqk_tm.astype(np.float32)),
            "bv_bc_i": np.ascontiguousarray(np.tile(bv[None, :], (128, 1))),
            "bo_bc_i": np.ascontiguousarray(np.tile(bo[None, :], (128, 1))),
            "cbi": cbn, "cfi": cfn, "cmi": cmn,
            "tabi": slot_table(c)[0], "wti": slot_table(c)[1],
            "wtvi": np.ascontiguousarray(
                slot_table(c)[1][0, :].reshape(2, NSLOT).T.astype(np.float32)),
        }
        in_maps.append(m)
    if "nc" not in _CACHE:
        _CACHE["nc"] = build_program()
    res = run_bass_kernel_spmd(_CACHE["nc"], in_maps, core_ids=list(range(NCORE)))
    full = np.zeros((S, D), np.float32)
    for c in range(NCORE):
        full[core_rows(c)] = res.results[c]["out"]
    return full.reshape(1, S, D)


# version 04
# speedup vs baseline: 1.1937x; 1.0314x over previous
import numpy as np
import ml_dtypes
from contextlib import ExitStack

import concourse.bass as bass
import concourse.mybir as mybir
from concourse.bass_utils import run_bass_kernel_spmd
from concourse.ap import AP as RawAP

F32 = mybir.dt.float32
BF16 = mybir.dt.bfloat16
I32 = mybir.dt.int32
AF = mybir.ActivationFunctionType
ALU = mybir.AluOpType
AX = mybir.AxisListType
ds = bass.ds

NCORE = 8
S, D, T = 8192, 2048, 1024
NH, HD = 16, 128
FF = 8192
NJ = 16
NPAIR = 64
NSLOT = 17
C_THR = 0.5009765625
SM_SCALE = float(HD) ** -0.5
RMS_EPS = float(np.float32(np.finfo(np.float32).eps))
LN_EPS = 1e-5
BF = ml_dtypes.bfloat16

STAGE = "full"
DEBUG = False


def core_rows(c):
    return list(range(512 * c, 512 * c + 512)) + list(range(7680 - 512 * c, 8192 - 512 * c))


def slot_table(c):
    def sup(s):
        return (s, 0) if s <= 7 else (15 - s, 512)
    slots = [(0, c), (1, 15 - c)]
    slots += [(0, s) for s in range(c)]
    slots += [(1, s) for s in range(15 - c)]
    assert len(slots) == NSLOT
    rows = []
    for qb, s in slots:
        r, o = sup(s)
        koff = r * 2048 * 1024 + o
        voff = r * 1024 * 2048 + o * 2048
        rows.append([koff, voff, 512 * qb, qb, 0, 0, 0, 0])
    wt = np.zeros((128, 2 * NSLOT), np.float32)
    for t, (qb, s) in enumerate(slots):
        wt[:, qb * NSLOT + t] = 1.0
    return np.asarray(rows, np.int32).reshape(1, -1), wt


def host_constants():
    cbn = np.zeros((128, 640), dtype=BF)
    cbn[:, 0:128] = np.eye(128, dtype=BF)
    cbn[:, 128] = 1.0
    cbn[:, 129:257] = 1.0
    cfn = np.zeros((128, 160), np.float32)
    cfn[:, 0] = SM_SCALE
    cfn[:, 16:144] = 1.0
    q = np.arange(512)[None, :]
    k = np.arange(128)[:, None]
    cmn = np.zeros((128, 4 * 512), dtype=BF)
    for j in range(4):
        cmn[:, j * 512:(j + 1) * 512] = (q >= 128 * j + k).astype(BF)
    return cbn, cfn, cmn


class Sem:
    def __init__(self, es, nc, name):
        self.h = es.enter_context(nc.semaphore(name))
        self.name = name
        self.val = 0


class Prog:
    ENGINES = ("sync", "gpsimd", "vector", "scalar", "tensor")

    def __init__(self, nc, es):
        self.nc = nc
        self.es = es
        self.ops = {e: [] for e in self.ENGINES}
        self.sems = {}
        self.label = "init"

    def sem(self, name):
        if name not in self.sems:
            self.sems[name] = Sem(self.es, self.nc, name)
        return self.sems[name]

    def op(self, eng, fn=None, wait=(), inc=(), ghost_inc=()):
        wait = [(s, v) for (s, v) in wait if v > 0]
        inc2 = []
        for (s, n) in inc:
            s.val += n
            inc2.append((s, n, True))
        for (s, n) in ghost_inc:
            s.val += n
            inc2.append((s, n, False))
        assert fn is not None or not inc2
        self.ops[eng].append((fn, wait, inc2, self.label))

    def dma(self, eng, out, in_, sem, wait=(), **kw):
        self.op(eng, lambda e, out=out, in_=in_, kw=kw: e.dma_start(out=out, in_=in_, **kw),
                wait=wait, inc=[(sem, 16)])
        return sem.val

    def barrier_vals(self):
        return [(s, s.val) for s in self.sems.values()]

    def hard_barrier(self):
        bv = self.barrier_vals()
        for e in self.ENGINES:
            self.op(e, None, wait=bv)

    def check_schedule(self):
        idx = {e: 0 for e in self.ENGINES}
        vals = {s.name: 0 for s in self.sems.values()}
        total = sum(len(v) for v in self.ops.values())
        done = 0
        while done < total:
            progressed = False
            for e in self.ENGINES:
                while idx[e] < len(self.ops[e]):
                    fn, wait, incs, _lb = self.ops[e][idx[e]]
                    if any(vals[s.name] < v for (s, v) in wait):
                        break
                    for (s, n, _att) in incs:
                        vals[s.name] += n
                    idx[e] += 1
                    done += 1
                    progressed = True
            if not progressed:
                detail = []
                for e in self.ENGINES:
                    if idx[e] < len(self.ops[e]):
                        _, wait, _, lb = self.ops[e][idx[e]]
                        w = [(s.name, v, vals[s.name]) for (s, v) in wait
                             if vals[s.name] < v]
                        detail.append(f"{e}[op {idx[e]}/{len(self.ops[e])} @{lb}] blocked on {w}")
                        for (s, v) in wait:
                            if vals[s.name] < v:
                                acc = dict(vals)
                                for e2 in self.ENGINES:
                                    run = vals[s.name]
                                    for j in range(idx[e2], len(self.ops[e2])):
                                        _, _, incs2, lb2 = self.ops[e2][j]
                                        for (s2, n2) in incs2:
                                            if s2.name == s.name:
                                                run += n2
                                        if run >= v:
                                            detail.append(
                                                f"   {s.name}->{v} would come from "
                                                f"{e2}[{j} @{lb2}]")
                                            break
                                    else:
                                        continue
                                    break
                raise RuntimeError("DEADLOCK:\n" + "\n".join(detail))

    def replay(self, block):
        def mk(engname):
            def emit(eng):
                for fn, wait, incs, _lb in self.ops[engname]:
                    for s, v in wait:
                        eng.wait_ge(s.h, v)
                    if fn is not None:
                        ins = fn(eng)
                        for s, n, att in incs:
                            if att:
                                ins = ins.then_inc(s.h, n)
            return emit
        block.sync(mk("sync"))
        block.gpsimd(mk("gpsimd"))
        block.vector(mk("vector"))
        block.scalar(mk("scalar"))
        block.tensor(mk("tensor"))


def psum_bank(es, nc, name, shape, dtype):
    esz = 2 if dtype == BF16 else 4
    per = shape[-1] * esz
    assert per <= 2048
    pad_last = 2048 // esz
    full = list(shape[:-1]) + [pad_last]
    if len(shape) == 2 and shape[0] < 128:
        full[0] = 128
    t = es.enter_context(nc.psum_tensor(name, full, dtype))
    ap = t[:]
    sl = tuple([slice(0, shape[i]) for i in range(len(shape) - 1)]
               + [slice(0, shape[-1])])
    return ap[sl]


def build_program():
    nc = bass.Bass("TRN2", target_bir_lowering=False, detect_race_conditions=False)
    g = lambda n, s, d, k="ExternalInput": nc.dram_tensor(n, s, d, kind=k).ap()

    xin = g("xin", [T, D], F32)
    wgu_s = g("wgu_s", [2048, 2048], F32)
    wp_s = g("wp_s", [2048, 1024], F32)
    wa_s = g("wa_s", [4, 256, 2048], F32)
    bqk_tm_i = g("bqk_tm_i", [128, 32], F32)
    bv_bc_i = g("bv_bc_i", [128, D], F32)
    bo_bc_i = g("bo_bc_i", [128, D], F32)
    cbi = g("cbi", [128, 640], BF16)
    cfi = g("cfi", [128, 160], F32)
    cmi = g("cmi", [128, 4 * 512], BF16)
    tabi = g("tabi", [1, NSLOT * 8], I32)
    wti = g("wti", [128, 2 * NSLOT], F32)
    wtvi = g("wtvi", [NSLOT, 2], F32)
    out = g("out", [T, D], F32, "ExternalOutput")
    if DEBUG:
        dbg_yfm = g("dbg_yfm", [128, 16 * 512], BF16, "ExternalOutput")
        dbg_hfm = g("dbg_hfm", [128, 64 * 512], BF16, "ExternalOutput")
        dbg_stat = g("dbg_stat", [128, 64], F32, "ExternalOutput")
        dbg_hinv = g("dbg_hinv", [128, 4], F32, "ExternalOutput")
        dbg_negm = g("dbg_negm", [128, NH], F32, "ExternalOutput")
        dbg_mrow = g("dbg_mrow", [1, 512], F32, "ExternalOutput")
        dbg_su = g("dbg_su", [NSLOT, NH, 512], F32, "ExternalOutput")
        dbg_av = g("dbg_av", [2, 128, NH * 512], F32, "ExternalOutput")
        dbg_qn = g("dbg_qn", [1, 64], F32, "ExternalOutput")
        dbg_pm = g("dbg_pm", [4, 128, 512], BF16, "ExternalOutput")
        dbg_ks = g("dbg_ks", [128, 16 * 512], BF16, "ExternalOutput")
        dbg_vs = g("dbg_vs", [128, 4 * 2048], BF16, "ExternalOutput")
        dbg_qs = g("dbg_qs", [128, 16 * 512], BF16, "ExternalOutput")
        dbg_avt = g("dbg_avt", [128, 1024], F32, "ExternalOutput")
        dbg_sps = g("dbg_sps", [128, 512], F32, "ExternalOutput")
        dbg_sut = g("dbg_sut", [1, 1024], F32, "ExternalOutput")
        dbg_q = g("dbg_q", [2048, 1024], BF16, "ExternalOutput")
        dbg_y2 = g("dbg_y2", [128, 16 * 1024], BF16, "ExternalOutput")
        dbg_qst = g("dbg_qst", [128, 64], F32, "ExternalOutput")
        dbg_yb = g("dbg_yb", [128, 2048], BF16, "ExternalOutput")
        dbg_xh = g("dbg_xh", [128, 2048], F32, "ExternalOutput")
        dbg_wts = g("dbg_wts", [128, 2 * NSLOT], F32, "ExternalOutput")
        dbg_tab = g("dbg_tab", [1, NSLOT * 8], I32, "ExternalOutput")
        dbg_avb = g("dbg_avb", [128, NH * 512], BF16, "ExternalOutput")
        dbg_srow = g("dbg_srow", [1, NH * 512], F32, "ExternalOutput")
        dbg_k = g("dbg_k", [2, 2048, 1024], BF16, "ExternalOutput")
        dbg_v = g("dbg_v", [8, 128, 2048], BF16, "ExternalOutput")

    idr = lambda n, s, d: nc.dram_tensor(n, s, d).ap()
    sdr = lambda n, s, d: nc.dram_tensor(n, s, d, addr_space="Shared").ap()
    wgu_in = idr("wgu_in", [2048, 2048], BF16)
    wp_in = idr("wp_in", [1024, 2048], BF16)
    wa_in = idr("wa_in", [4, 2048, 256], BF16)
    wgu_all = sdr("wgu_all", [8, 2048, 2048], BF16)
    wp_all = sdr("wp_all", [8, 1024, 2048], BF16)
    wa_all = sdr("wa_all", [8, 4, 2048, 256], BF16)
    k_in = idr("k_in", [2048, 1024], BF16)
    v_in = idr("v_in", [1024, 2048], BF16)
    kn_in = idr("kn_in", [1, 16], F32)
    k_all = sdr("k_all", [8, 2048, 1024], BF16)
    v_all = sdr("v_all", [8, 1024, 2048], BF16)
    kn_all = sdr("kn_all", [8, 1, 16], F32)
    xdram = idr("xdram", [T, D], F32)
    qdram = idr("qdram", [2048, 1024], BF16)
    sudram = idr("sudram", [NSLOT, NH, 512], F32)
    podram = idr("podram", [T, D], BF16)

    es = ExitStack()
    with es:
        P = Prog(nc, es)
        sb = lambda n, s, d: es.enter_context(nc.sbuf_tensor(n, s, d))[:]

        LD = P.sem("LD")
        WG = P.sem("WG")
        XS = P.sem("XS")
        PP = P.sem("PP")
        CC = P.sem("CC")
        AC = P.sem("AC")
        KV = P.sem("KV")
        PE = P.sem("PE")
        DV = P.sem("DV")
        SC = P.sem("SC")
        FE = P.sem("FE")

        cbs = sb("cbs", [128, 640], BF16)
        cfs = sb("cfs", [128, 160], F32)
        tabs = sb("tabs", [1, NSLOT * 8], I32)
        wts = sb("wts", [128, 2 * NSLOT], F32)
        wtv = sb("wtv", [NSLOT, 2], F32)
        bqk_tm = sb("bqk_tm", [128, 32], F32)
        stat = sb("stat", [128, 64], F32)
        fsc = sb("fsc", [128, 16], F32)
        fsb = sb("fsb", [128, 16], BF16)

        def dram_fence(eng, dram_ap, sem):
            p, f = dram_ap.shape[0], dram_ap.shape[1]
            scr = fsb if dram_ap.tensor.dtype == BF16 else fsc
            P.dma(eng, scr[0:p, 0:f], dram_ap, sem)
            return sem.val
        row = sb("row", [1, 2048], F32)
        negm = sb("negm", [128, NH], F32)

        nc.const_aps.register(F32, SM_SCALE, cfs[:, 0:1])
        ident = cbs[:, 0:128]
        ones_col_b = cbs[:, 128:129]
        ones_row_f = cfs[0:1, 16:144]

        for src, dst in [(cbi, cbs), (cfi, cfs), (tabi, tabs),
                         (bqk_tm_i, bqk_tm), (wti, wts), (wtvi, wtv)]:
            P.dma("sync", dst, src, LD)
        ldv = LD.val

        P.label = "preproc"
        with ExitStack() as pes:
            psb = lambda n, s, d: pes.enter_context(nc.sbuf_tensor(n, s, d))[:]
            pps = lambda n, s, d: psum_bank(pes, nc, n, s, d)
            wfp = [psb(f"ppwfp{i}", [128, 2048], F32) for i in range(2)]
            wbb = [psb(f"ppwbb{i}", [128, 2048], BF16) for i in range(2)]
            wab = psb("ppwab", [128, 2048], BF16)
            wsg = psb("ppwsg", [128, 2048], BF16)
            gsc = psb("ppgsc", [128, 32], F32)
            scb = psb("ppscb", [128, 32], BF16)
            thr = psb("ppthr", [128, 32], F32)
            msk = psb("ppmsk", [128, 2048], BF16)
            wht = psb("ppwht", [128, 2048], BF16)
            dd_ = psb("ppdd", [128, 2048], BF16)
            fwd = [psb(f"ppfwd{i}", [128, 2048], BF16) for i in range(2)]
            asmb = [psb(f"ppasm{i}", [128, 2048], BF16) for i in range(2)]
            tp_ps = [pps(f"pptps{i}", [128, 128], BF16) for i in range(2)]

            wfp_free = [0, 0]
            fwd_free = [0, 0]
            asm_free = [0, 0]
            tp_free = [0, 0]

            def quant_tile(par, C, quant, wg_ld):
                G = C // 64
                P.op("vector", lambda e, p=par, C=C: e.tensor_copy(
                    out=wbb[p][:, :C], in_=wfp[p][:, :C]),
                    wait=[(WG, wg_ld)], inc=[(FE, 1)])
                wfp_free[par] = FE.val
                e0 = FE.val
                if not quant:
                    P.op("vector", lambda e, p=par, C=C: e.tensor_copy(
                        out=fwd[p][:, :C], in_=wbb[p][:, :C]),
                        wait=[(FE, e0), (PE, fwd_free[par])], inc=[(FE, 1)])
                    return
                P.op("scalar", lambda e, p=par, C=C: e.activation(
                    out=wab[:, :C], in_=wbb[p][:, :C], func=AF.Abs),
                    wait=[(FE, e0), (FE, FE.val)])
                P.op("scalar", lambda e, p=par, C=C: e.sign(
                    out=wsg[:, :C], in_=wbb[p][:, :C]), inc=[(SC, 1)])
                scv = SC.val
                P.op("vector", lambda e, p=par, C=C, G=G: e.tensor_reduce(
                    out=gsc[:, :G], in_=wbb[p][:, :C].rearrange("p (g k) -> p g k", k=64),
                    axis=AX.X, op=ALU.add, apply_absolute_value=True),
                    wait=[(FE, e0)], inc=[(FE, 1)])
                P.op("vector", lambda e, G=G: e.tensor_scalar(
                    out=scb[:, :G], in0=gsc[:, :G], scalar1=1.0 / 64, scalar2=1e-8,
                    op0=ALU.mult, op1=ALU.max), wait=[(FE, FE.val)], inc=[(FE, 1)])
                P.op("vector", lambda e, G=G: e.tensor_scalar(
                    out=thr[:, :G], in0=scb[:, :G], scalar1=C_THR, scalar2=None,
                    op0=ALU.mult), wait=[(FE, FE.val)], inc=[(FE, 1)])
                P.op("vector", lambda e, C=C, G=G: e.tensor_tensor(
                    out=msk[:, :C].rearrange("p (g k) -> p g k", k=64),
                    in0=wab[:, :C].rearrange("p (g k) -> p g k", k=64),
                    in1=thr[:, :G].unsqueeze(2).broadcast_to([128, G, 64]),
                    op=ALU.is_gt), wait=[(FE, FE.val), (SC, scv)], inc=[(FE, 1)])
                P.op("vector", lambda e, C=C: e.tensor_tensor(
                    out=wht[:, :C], in0=msk[:, :C], in1=wsg[:, :C], op=ALU.mult),
                    wait=[(FE, FE.val)], inc=[(FE, 1)])
                P.op("vector", lambda e, C=C, G=G: e.tensor_tensor(
                    out=wht[:, :C].rearrange("p (g k) -> p g k", k=64),
                    in0=wht[:, :C].rearrange("p (g k) -> p g k", k=64),
                    in1=scb[:, :G].unsqueeze(2).broadcast_to([128, G, 64]),
                    op=ALU.mult), wait=[(FE, FE.val)], inc=[(FE, 1)])
                P.op("vector", lambda e, p=par, C=C: e.tensor_tensor(
                    out=dd_[:, :C], in0=wht[:, :C], in1=wbb[p][:, :C], op=ALU.subtract),
                    wait=[(FE, FE.val)], inc=[(FE, 1)])
                P.op("vector", lambda e, p=par, C=C: e.tensor_tensor(
                    out=fwd[p][:, :C], in0=wbb[p][:, :C], in1=dd_[:, :C], op=ALU.add),
                    wait=[(FE, FE.val), (PE, fwd_free[par])], inc=[(FE, 1)])

            def transpose_out(par, C, dst):
                fv = FE.val
                for jc in range(C // 128):
                    tp = jc % 2
                    P.op("tensor", lambda e, p=par, jc=jc, tp=tp: e.transpose(
                        tp_ps[tp][:], fwd[p][:, jc * 128:(jc + 1) * 128], ident),
                        wait=[(FE, fv), (LD, ldv), (DV, tp_free[tp]),
                              (PP, asm_free[par])],
                        inc=[(PE, 1)])
                    pev = PE.val
                    P.op("vector", lambda e, p=par, jc=jc, tp=tp: e.tensor_copy(
                        out=asmb[p][:, jc * 128:(jc + 1) * 128], in_=tp_ps[tp][:]),
                        wait=[(PE, pev)], inc=[(DV, 1)])
                    tp_free[tp] = DV.val
                fwd_free[par] = PE.val
                P.dma("gpsimd", dst, asmb[par][:, :C], PP, wait=[(DV, DV.val)])
                asm_free[par] = PP.val

            pp_mark = {}
            ti = 0
            for i in range(16):
                par = ti % 2
                P.dma("sync", wfp[par], wgu_s[i * 128:(i + 1) * 128, :], WG,
                      wait=[(FE, wfp_free[par])])
                quant_tile(par, 2048, True, WG.val)
                col = i * 128 if i < 8 else 1024 + (i - 8) * 128
                transpose_out(par, 2048,
                              wgu_in[:, col:col + 128].rearrange("(jc p) h -> p jc h", p=128))
                ti += 1
            pp_mark["gu"] = PP.val
            for i in range(16):
                par = ti % 2
                P.dma("sync", wfp[par][:, 0:1024], wp_s[i * 128:(i + 1) * 128, :], WG,
                      wait=[(FE, wfp_free[par])])
                quant_tile(par, 1024, True, WG.val)
                transpose_out(par, 1024,
                              wp_in[:, i * 128:(i + 1) * 128].rearrange(
                                  "(hc p) d2 -> p hc d2", p=128))
                ti += 1
            pp_mark["wp"] = PP.val
            for w in range(4):
                for i in range(2):
                    par = ti % 2
                    P.dma("sync", wfp[par], wa_s[w, i * 128:(i + 1) * 128, :], WG,
                          wait=[(FE, wfp_free[par])])
                    quant_tile(par, 2048, False, WG.val)
                    transpose_out(par, 2048,
                                  wa_in[w, :, i * 128:(i + 1) * 128].rearrange(
                                      "(jc p) i2 -> p jc i2", p=128))
                    ti += 1
            pp_mark["wa"] = PP.val

            def ag(inp, outp, ppv):
                P.op("gpsimd", lambda e, i=inp, o=outp: e.collective_compute(
                    "AllGather", ALU.bypass, replica_groups=[list(range(8))],
                    ins=[i.opt()], outs=[o.opt()]),
                    wait=[(PP, ppv)], inc=[(CC, 1)])
                return CC.val
            cc_gu = ag(wgu_in, wgu_all, pp_mark["gu"])
            ag(wp_in, wp_all, pp_mark["wp"])
            cc_w = ag(wa_in.rearrange("w a b -> (w a) b"),
                      wa_all.rearrange("r w a b -> r (w a) b"), pp_mark["wa"])

        yb_free = [0]

        def ln_half(xh, ydst, sqs, yb, yps, yps_free, extra=()):
            xs_x = XS.val
            for t in range(4):
                P.op("vector", lambda e, xh=xh, t=t: e.tensor_reduce(
                    out=stat[:, t:t + 1], in_=xh[:, t * 2048:(t + 1) * 2048],
                    axis=AX.X, op=ALU.add), wait=[(XS, xs_x), (FE, FE.val)],
                    inc=[(FE, 1)])
                P.op("scalar", lambda e, xh=xh, sqs=sqs, t=t: e.activation(
                    out=sqs, in_=xh[:, t * 2048:(t + 1) * 2048], func=AF.Square,
                    accum_out=stat[:, 4 + t:5 + t]),
                    wait=[(XS, xs_x)] + list(extra), inc=[(SC, 1)])
            sc0 = SC.val

            def v(fn, wait=()):
                P.op("vector", fn, wait=list(wait) + [(FE, FE.val)], inc=[(FE, 1)])
            v(lambda e: e.tensor_scalar(out=stat[:, 8:12], in0=stat[:, 0:4],
                                        scalar1=1.0 / D, scalar2=None, op0=ALU.mult),
              wait=[(SC, sc0)])
            v(lambda e: e.tensor_scalar(out=stat[:, 12:16], in0=stat[:, 4:8],
                                        scalar1=1.0 / D, scalar2=None, op0=ALU.mult))
            v(lambda e: e.tensor_tensor(out=stat[:, 16:20], in0=stat[:, 8:12],
                                        in1=stat[:, 8:12], op=ALU.mult))
            v(lambda e: e.tensor_tensor(out=stat[:, 20:24], in0=stat[:, 12:16],
                                        in1=stat[:, 16:20], op=ALU.subtract))
            v(lambda e: e.tensor_scalar(out=stat[:, 24:28], in0=stat[:, 20:24],
                                        scalar1=LN_EPS, scalar2=None, op0=ALU.add))
            v(lambda e: e.reciprocal(out=stat[:, 28:32], in_=stat[:, 24:28]))
            P.op("scalar", lambda e: e.activation(out=stat[:, 32:36], in_=stat[:, 28:32],
                                                  func=AF.Sqrt),
                 wait=[(FE, FE.val)], inc=[(FE, 1)])
            v(lambda e: e.tensor_tensor(out=stat[:, 36:40], in0=stat[:, 32:36],
                                        in1=stat[:, 32:36], op=ALU.mult))
            v(lambda e: e.tensor_tensor(out=stat[:, 40:44], in0=stat[:, 36:40],
                                        in1=stat[:, 24:28], op=ALU.mult))
            v(lambda e: e.tensor_scalar(out=stat[:, 44:48], in0=stat[:, 40:44],
                                        scalar1=-0.5, scalar2=1.5,
                                        op0=ALU.mult, op1=ALU.add))
            v(lambda e: e.tensor_tensor(out=stat[:, 48:52], in0=stat[:, 32:36],
                                        in1=stat[:, 44:48], op=ALU.mult))
            v(lambda e: e.scalar_tensor_tensor(out=stat[:, 52:56], in0=stat[:, 8:12],
                                               scalar=-1.0, in1=stat[:, 48:52],
                                               op0=ALU.mult, op1=ALU.mult))
            fe_ln = FE.val
            for t in range(4):
                P.op("scalar", lambda e, xh=xh, yb=yb, t=t: e.activation(
                    out=yb, in_=xh[:, t * 2048:(t + 1) * 2048], func=AF.Identity,
                    bias=stat[:, 52 + t:53 + t], scale=stat[:, 48 + t:49 + t]),
                    wait=[(FE, fe_ln), (PE, yb_free[0])], inc=[(SC, 1)])
                sc_y = SC.val
                for jc in range(16):
                    tp = jc % 2
                    P.op("tensor", lambda e, yb=yb, yps=yps, jc=jc, tp=tp: e.transpose(
                        yps[tp][:], yb[:, jc * 128:(jc + 1) * 128], ident),
                        wait=[(SC, sc_y), (LD, ldv), (DV, yps_free[tp])],
                        inc=[(PE, 1)])
                    pev = PE.val
                    P.op("vector", lambda e, yps=yps, t=t, jc=jc, tp=tp,
                         ydst=ydst: e.tensor_copy(out=ydst(t, jc), in_=yps[tp][:]),
                        wait=[(PE, pev)], inc=[(DV, 1)])
                    yps_free[tp] = DV.val
                yb_free[0] = PE.val
            return DV.val

        def ffn_pass(tag, src_dram, dst_dram, proj_dram=None):
            P.label = tag
            with ExitStack() as fes:
                fsb = lambda n, s, d: fes.enter_context(nc.sbuf_tensor(tag + n, s, d))[:]
                xh = fsb("xh", [128, 4 * 2048], F32)
                x1h = fsb("x1h", [128, 4 * 2048], F32)
                yb = fsb("yb", [128, 2048], BF16)
                yfm = fsb("yfm", [128, 16 * 512], BF16)
                hfm = fsb("hfm", [128, 64 * 512], BF16)
                sg = [fsb(f"sg{i}", [128, 512], BF16) for i in range(2)]
                h2 = [fsb(f"h2{i}", [128, 512], BF16) for i in range(2)]
                wgu = [fsb(f"wgu{i}", [128, 16 * 256], BF16) for i in range(2)]
                wpb = [fsb(f"wpb{i}", [128, 8 * 512], BF16) for i in range(2)]
                sumr = fsb("sumr", [1, 512], F32)
                invr = fsb("invr", [1, 512], F32)
                hinv = fsb("hinv", [128, 4], F32)
                if proj_dram is not None:
                    pb2 = fsb("pb2", [128, 2048], BF16)

                for half in range(2):
                    hx = f"{half}"
                    P.label = f"{tag}h{half}"
                    bar = P.barrier_vals()
                    for t in range(4):
                        P.dma("scalar", xh[:, t * 2048:(t + 1) * 2048],
                              src_dram[(half * 4 + t) * 128:(half * 4 + t + 1) * 128, :],
                              XS, wait=bar if t == 0 else [])
                    extra = []
                    if proj_dram is not None:
                        fe_pr = FE.val
                        for t in range(4):
                            P.dma("scalar", pb2,
                                  proj_dram[(half * 4 + t) * 128:
                                            (half * 4 + t + 1) * 128, :],
                                  XS, wait=[(FE, fe_pr)])
                            xs_p = XS.val
                            P.op("vector", lambda e, t=t: e.tensor_tensor(
                                out=xh[:, t * 2048:(t + 1) * 2048],
                                in0=xh[:, t * 2048:(t + 1) * 2048], in1=pb2,
                                op=ALU.add),
                                wait=[(XS, xs_p), (FE, FE.val)], inc=[(FE, 1)])
                            fe_pr = FE.val
                        extra = [(FE, fe_pr)]
                    with ExitStack() as lnes:
                        yps = [psum_bank(lnes, nc, f"{tag}yps{hx}{i}",
                                         [128, 128], BF16) for i in range(2)]
                        yps_free = [0, 0]
                        dv_y = ln_half(xh, lambda t, jc: yfm[:, jc * 512 + t * 128:
                                                            jc * 512 + (t + 1) * 128],
                                       x1h[:, 0:2048], yb, yps, yps_free,
                                       extra=extra)
                    with ExitStack() as gues:
                        gps = [psum_bank(gues, nc, f"{tag}gps{hx}{i}",
                                         [128, 512], F32) for i in range(2)]
                        ups = [psum_bank(gues, nc, f"{tag}ups{hx}{i}",
                                         [128, 512], F32) for i in range(2)]
                        rps = psum_bank(gues, nc, f"{tag}rps{hx}", [1, 512], F32)
                        gps_free = [0, 0]
                        ups_free = [0, 0]
                        wgu_free = [0, 0]
                        for p in range(NPAIR):
                            par = p % 2
                            blk, loc = p // 8, (p % 8) * 128
                            P.dma("sync", wgu[par].rearrange(
                                "p (jc x) -> p jc x", x=256)[:, :, 0:128],
                                wgu_all[blk, :, loc:loc + 128].rearrange(
                                    "(jc p) h -> p jc h", p=128), WG,
                                wait=[(CC, cc_gu), (PE, wgu_free[par])])
                            P.dma("sync", wgu[par].rearrange(
                                "p (jc x) -> p jc x", x=256)[:, :, 128:256],
                                wgu_all[blk, :, 1024 + loc:1024 + loc + 128].rearrange(
                                    "(jc p) h -> p jc h", p=128), WG)
                            wg_p = WG.val
                            for jc in range(NJ):
                                P.op("tensor", lambda e, par=par, jc=jc: e.matmul(
                                    gps[par], wgu[par][:, jc * 256:jc * 256 + 128],
                                    yfm[:, jc * 512:(jc + 1) * 512],
                                    start=(jc == 0), stop=(jc == NJ - 1),
                                    skip_group_check=True),
                                    wait=([(WG, wg_p), (DV, dv_y), (SC, gps_free[par])]
                                          if jc == 0 else []))
                            for jc in range(NJ):
                                P.op("tensor", lambda e, par=par, jc=jc: e.matmul(
                                    ups[par], wgu[par][:, jc * 256 + 128:jc * 256 + 256],
                                    yfm[:, jc * 512:(jc + 1) * 512],
                                    start=(jc == 0), stop=(jc == NJ - 1),
                                    skip_group_check=True),
                                    wait=([(DV, ups_free[par])] if jc == 0 else []),
                                    inc=[(PE, 1)] if jc == NJ - 1 else [])
                            wgu_free[par] = PE.val
                            pe_p = PE.val
                            P.op("scalar", lambda e, par=par: e.activation(
                                out=sg[par], in_=gps[par], func=AF.Silu),
                                wait=[(PE, pe_p)], inc=[(SC, 1)])
                            gps_free[par] = SC.val
                            sc_sg = SC.val
                            P.op("vector", lambda e, par=par, p=p: e.tensor_tensor(
                                out=hfm[:, p * 512:(p + 1) * 512], in0=sg[par],
                                in1=ups[par], op=ALU.mult),
                                wait=[(SC, sc_sg)], inc=[(DV, 1)])
                            ups_free[par] = DV.val
                            dv_h = DV.val
                            P.op("scalar", lambda e, par=par, p=p: e.activation(
                                out=h2[par], in_=hfm[:, p * 512:(p + 1) * 512],
                                func=AF.Square),
                                wait=[(DV, dv_h), (PE, PE.val)], inc=[(SC, 1)])
                            sc_h2 = SC.val
                            P.op("tensor", lambda e, par=par, p=p: e.matmul(
                                rps, ones_col_b, h2[par], start=(p == 0),
                                stop=(p == NPAIR - 1), skip_group_check=True),
                                wait=[(SC, sc_h2)],
                                inc=[(PE, 1)])
                        dv_hall = DV.val
                        pe_rs = PE.val
                        def vr(fn, wait=()):
                            P.op("vector", fn, wait=list(wait) + [(FE, FE.val)],
                                 inc=[(FE, 1)])
                        vr(lambda e: e.tensor_copy(out=sumr, in_=rps),
                           wait=[(PE, pe_rs)])
                    vr2 = lambda fn, wait=(): P.op(
                        "vector", fn, wait=list(wait) + [(FE, FE.val)], inc=[(FE, 1)])
                    A, B, C, Dv = (row[0:1, 0:512], row[0:1, 512:1024],
                                   row[0:1, 1024:1536], row[0:1, 1536:2048])
                    vr2(lambda e: e.tensor_scalar(out=A, in0=sumr,
                                                  scalar1=1.0 / FF, scalar2=RMS_EPS,
                                                  op0=ALU.mult, op1=ALU.add))
                    vr2(lambda e: e.reciprocal(out=B, in_=A))
                    P.op("scalar", lambda e: e.activation(
                        out=C, in_=B, func=AF.Sqrt),
                        wait=[(FE, FE.val)], inc=[(FE, 1)])
                    vr2(lambda e: e.tensor_tensor(out=B, in0=C, in1=C, op=ALU.mult))
                    vr2(lambda e: e.tensor_tensor(out=Dv, in0=B, in1=A, op=ALU.mult))
                    vr2(lambda e: e.tensor_scalar(out=B, in0=Dv,
                                                  scalar1=-0.5, scalar2=1.5,
                                                  op0=ALU.mult, op1=ALU.add))
                    vr2(lambda e: e.tensor_tensor(out=invr, in0=C, in1=B, op=ALU.mult))
                    fe_inv = FE.val
                    with ExitStack() as pjes:
                        pj = [psum_bank(pjes, nc, f"{tag}pj{hx}{i}",
                                        [128, 512], F32) for i in range(4)]
                        tps2 = psum_bank(pjes, nc, f"{tag}tps2{hx}", [128, 4], F32)
                        for t in range(4):
                            P.op("tensor", lambda e, t=t: e.matmul(
                                tps2[:, t:t + 1], invr[0:1, t * 128:(t + 1) * 128],
                                ones_row_f[0:1, 0:1], start=(t == 0), stop=(t == 3),
                                skip_group_check=True),
                                wait=[(FE, fe_inv)] if t == 0 else [],
                                inc=[(PE, 1)] if t == 3 else [])
                        pe_ti = PE.val
                        P.op("vector", lambda e: e.tensor_scalar(
                            out=hinv, in0=tps2, scalar1=0.5, scalar2=None,
                            op0=ALU.mult),
                            wait=[(PE, pe_ti), (FE, FE.val)], inc=[(FE, 1)])
                        fe_hinv = FE.val
                        wpb_free = [0, 0]
                        pj_free = [0, 0, 0, 0]
                        for dt in range(4):
                            for sci in range(8):
                                par = (dt * 8 + sci) % 2
                                P.dma("sync", wpb[par],
                                      wp_all[sci, :, dt * 512:(dt + 1) * 512].rearrange(
                                          "(hc p) d2 -> p hc d2", p=128), WG,
                                      wait=[(CC, cc_w), (PE, wpb_free[par])])
                                wg_pj = WG.val
                                for t in range(4):
                                    for jc in range(8):
                                        pp = sci * 8 + jc
                                        P.op("tensor", lambda e, t=t, par=par, jc=jc,
                                             pp=pp, sci=sci: e.matmul(
                                                 pj[t], hfm[:, pp * 512 + t * 128:
                                                            pp * 512 + (t + 1) * 128],
                                                 wpb[par][:, jc * 512:(jc + 1) * 512],
                                                 start=(sci == 0 and jc == 0),
                                                 stop=(sci == 7 and jc == 7),
                                                 skip_group_check=True),
                                            wait=(([(WG, wg_pj), (DV, dv_hall)]
                                                   if t == 0 else [])
                                                  + [(DV, pj_free[t])]
                                                  if jc == 0 and sci == 0 else
                                                  ([(WG, wg_pj)] if jc == 0 and
                                                   t == 0 else [])),
                                            inc=([(PE, 1)] if (t == 3 and jc == 7)
                                                 else []))
                                wpb_free[par] = PE.val
                            pe_dt = PE.val
                            for t in range(4):
                                P.op("vector", lambda e, dt=dt, t=t:
                                     e.scalar_tensor_tensor(
                                         out=x1h[:, t * 2048 + dt * 512:
                                                 t * 2048 + (dt + 1) * 512],
                                         in0=pj[t], scalar=hinv[:, t:t + 1],
                                         in1=xh[:, t * 2048 + dt * 512:
                                                t * 2048 + (dt + 1) * 512],
                                         op0=ALU.mult, op1=ALU.add),
                                     wait=[(PE, pe_dt), (FE, fe_hinv)], inc=[(DV, 1)])
                                pj_free[t] = DV.val
                    dv_x1 = DV.val
                    for t in range(4):
                        P.dma("gpsimd", dst_dram[(half * 4 + t) * 128:
                                                 (half * 4 + t + 1) * 128, :],
                              x1h[:, t * 2048:(t + 1) * 2048], AC,
                              wait=[(DV, dv_x1)])
                    dram_fence("gpsimd", dst_dram[half * 512:half * 512 + 128, 0:8],
                               AC)
                    if DEBUG and tag == "f1" and half == 0:
                        P.dma("gpsimd", dbg_yfm, yfm, AC, wait=[(DV, dv_x1)])
                        P.dma("gpsimd", dbg_hfm, hfm, AC)
                        P.dma("gpsimd", dbg_stat, stat, AC)
                        P.dma("gpsimd", dbg_hinv, hinv, AC)

        ffn_pass("f1", xin, xdram)

        if STAGE in ("x2", "full"):
            P.label = "qkv"
            with ExitStack() as qes:
                qsb = lambda n, s, d: qes.enter_context(nc.sbuf_tensor(n, s, d))[:]
                qps = lambda n, s, d: psum_bank(qes, nc, n, s, d)
                y2fm = [qsb(f"y2fm{i}", [128, 16 * 512], BF16) for i in range(2)]
                wat = qsb("wat", [128, 16 * 2048], BF16)
                k2t = qsb("k2t", [128, 512], BF16)
                knr = qsb("knr", [1, 64], F32)
                qnr = qsb("qnr", [1, 64], F32)
                bv_bc = qsb("bv_bc", [128, D], F32)
                mrow = qsb("mrow", [1, 512], F32)
                P.dma("sync", bv_bc, bv_bc_i, LD)
                kvps = [qps(f"kvps{i}", [128, 512], F32) for i in range(2)]
                nps = qps("nps", [1, 512], F32)
                mps = qps("mps", [128, NH], F32)
                yps = [qps(f"qyps{i}", [128, 128], BF16) for i in range(2)]
                yps_free = [0, 0]

                lnes2 = ExitStack()
                with lnes2:
                    lsb = lambda n, s, d: lnes2.enter_context(
                        nc.sbuf_tensor(n, s, d))[:]
                    xh = lsb("qxh", [128, 4 * 2048], F32)
                    yb = lsb("qyb", [128, 2048], BF16)
                    qsqs = lsb("qsqs", [128, 2048], F32)
                    for half in range(2):
                        bar = P.barrier_vals()
                        for t in range(4):
                            P.dma("scalar", xh[:, t * 2048:(t + 1) * 2048],
                                  xdram[(half * 4 + t) * 128:
                                        (half * 4 + t + 1) * 128, :],
                                  XS, wait=bar if t == 0 else [])
                        ln_half(xh, lambda t, jc, h=half: y2fm[h][:, jc * 512 +
                                                              t * 128:jc * 512 +
                                                              (t + 1) * 128],
                                qsqs, yb, yps, yps_free)
                        if DEBUG and half == 1:
                            P.dma("gpsimd", dbg_qst, stat, AC,
                                  wait=[(DV, DV.val), (SC, SC.val)])
                            P.dma("gpsimd", dbg_yb, yb, AC)
                            P.dma("gpsimd", dbg_xh, xh[:, 0:2048], AC)
                dv_y2 = DV.val
                kqpan = qsb("kqpan", [128, 32 * 512], BF16)
                vpan = qsb("vpan", [128, 32 * 512], BF16)
                if DEBUG:
                    P.dma("gpsimd", dbg_y2[:, 0:16 * 512], y2fm[0], AC,
                          wait=[(DV, dv_y2)])
                    P.dma("gpsimd", dbg_y2[:, 16 * 512:], y2fm[1], AC)

                def load_w(widx, wv):
                    for r in range(8):
                        P.dma("sync", wat.rearrange("p (jc i) -> p jc i", i=2048)
                              [:, :, 256 * r:256 * (r + 1)],
                              wa_all[r, widx, :, :].rearrange(
                                  "(jc p) i -> p jc i", p=128),
                              WG, wait=wv if r == 0 else [])
                    return WG.val

                kvps_free = [[], []]
                nrm = [0]
                cc_k = [0]

                for widx, kind in [(1, "k"), (2, "v"), (0, "q")]:
                    wg_w = load_w(widx, [(CC, cc_w), (PE, PE.val)])
                    if kind in ("k", "q"):
                        for ic in range(16):
                            for th in range(2):
                                par = (ic * 2 + th) % 2
                                sl = kqpan[:, (ic * 2 + th) * 512:
                                           (ic * 2 + th + 1) * 512]
                                for jc in range(16):
                                    P.op("tensor", lambda e, par=par, ic=ic, th=th,
                                         jc=jc: e.matmul(
                                             kvps[par], wat[:, jc * 2048 + ic * 128:
                                                            jc * 2048 + (ic + 1) * 128],
                                             y2fm[th][:, jc * 512:(jc + 1) * 512],
                                             start=(jc == 0), stop=(jc == 15),
                                             skip_group_check=True),
                                        wait=([(WG, wg_w), (DV, dv_y2)]
                                              + kvps_free[par] if jc == 0 else []),
                                        inc=[(PE, 1)] if jc == 15 else [])
                                pe_kv = PE.val
                                bcol = (16 if kind == "k" else 0) + ic
                                P.op("scalar", lambda e, par=par, bcol=bcol,
                                     sl=sl: e.activation(
                                         out=sl, in_=kvps[par], func=AF.Identity,
                                         bias=bqk_tm[:, bcol:bcol + 1]),
                                    wait=[(PE, pe_kv)]
                                    + ([(CC, cc_k[0])] if kind == "q"
                                       and ic == 0 and th == 0 else []),
                                    inc=[(SC, 1)])
                                kvps_free[par] = [(SC, SC.val)]
                                sc_kv = SC.val
                                kq_dst = k_in if kind == "k" else qdram
                                P.dma("sync", kq_dst[ic * 128:(ic + 1) * 128,
                                                     th * 512:(th + 1) * 512],
                                      sl, AC, wait=[(SC, sc_kv)])
                                P.op("scalar", lambda e, sl=sl: e.activation(
                                    out=k2t, in_=sl, func=AF.Square),
                                    wait=[(SC, sc_kv)], inc=[(SC, 1)])
                                sc_k2 = SC.val
                                P.op("tensor", lambda e: e.matmul(
                                    nps, ones_col_b, k2t, start=True, stop=True,
                                    skip_group_check=True),
                                    wait=[(SC, sc_k2), (FE, nrm[0])], inc=[(PE, 1)])
                                pe_n = PE.val
                                tgt = knr if kind == "k" else qnr
                                P.op("vector", lambda e, t2=tgt, ic=ic, th=th:
                                     e.tensor_reduce(
                                         out=t2[0:1, ic * 2 + th:ic * 2 + th + 1],
                                         in_=nps, axis=AX.X, op=ALU.max),
                                     wait=[(PE, pe_n), (FE, FE.val)], inc=[(FE, 1)])
                                nrm[0] = FE.val
                    else:
                        for t in range(8):
                            hh, tt = t // 4, t % 4
                            for iv in range(4):
                                par = (t * 4 + iv) % 2
                                vsl = vpan[:, (t * 4 + iv) * 512:
                                           (t * 4 + iv + 1) * 512]
                                for jc in range(16):
                                    P.op("tensor", lambda e, par=par, hh=hh, tt=tt,
                                         iv=iv, jc=jc: e.matmul(
                                             kvps[par],
                                             y2fm[hh][:, jc * 512 + tt * 128:
                                                      jc * 512 + (tt + 1) * 128],
                                             wat[:, jc * 2048 + iv * 512:
                                                 jc * 2048 + (iv + 1) * 512],
                                             start=(jc == 0), stop=(jc == 15),
                                             skip_group_check=True),
                                        wait=([(WG, wg_w)] + kvps_free[par]
                                              if jc == 0 else []),
                                        inc=[(PE, 1)] if jc == 15 else [])
                                pe_v = PE.val
                                P.op("vector", lambda e, par=par, iv=iv,
                                     vsl=vsl: e.tensor_tensor(
                                         out=vsl, in0=kvps[par],
                                         in1=bv_bc[:, iv * 512:(iv + 1) * 512],
                                         op=ALU.add),
                                     wait=[(PE, pe_v), (FE, FE.val)],
                                     inc=[(DV, 1)])
                                kvps_free[par] = [(DV, DV.val)]
                                dv_v = DV.val
                                P.dma("sync", v_in[t * 128:(t + 1) * 128,
                                                   iv * 512:(iv + 1) * 512],
                                      vsl, AC, wait=[(DV, dv_v)])
                    if kind == "k":
                        P.op("vector", lambda e: e.tensor_reduce(
                            out=mrow[0:1, 224:240],
                            in_=knr[0:1, 0:32].rearrange("x (h t2) -> x h t2", t2=2),
                            axis=AX.X, op=ALU.max),
                            wait=[(FE, nrm[0]), (FE, FE.val)], inc=[(FE, 1)])
                        fe_kn = FE.val
                        P.dma("gpsimd", kn_in, mrow[0:1, 224:240], AC,
                              wait=[(FE, fe_kn)])
                        dram_fence("sync", k_in[0:128, 0:8], AC)
                        ac_k = AC.val
                        P.op("gpsimd", lambda e: e.collective_compute(
                            "AllGather", ALU.bypass, replica_groups=[list(range(8))],
                            ins=[k_in.opt()], outs=[k_all.opt()]),
                            wait=[(AC, ac_k)], inc=[(CC, 1)])
                        P.op("gpsimd", lambda e: e.collective_compute(
                            "AllGather", ALU.bypass, replica_groups=[list(range(8))],
                            ins=[kn_in.opt()], outs=[kn_all.opt()]),
                            inc=[(CC, 1)])
                        cc_k[0] = CC.val
                    if kind == "v":
                        dram_fence("sync", v_in[0:128, 0:8], AC)
                        ac_v = AC.val
                        P.op("gpsimd", lambda e: e.collective_compute(
                            "AllGather", ALU.bypass, replica_groups=[list(range(8))],
                            ins=[v_in.opt()], outs=[v_all.opt()]),
                            wait=[(AC, ac_v)], inc=[(CC, 1)])
                dram_fence("sync", qdram[0:128, 0:8], AC)
                cc_kv = CC.val
                ac_qkv = AC.val

                def vm(fn, wait=()):
                    P.op("vector", fn, wait=list(wait) + [(FE, FE.val)], inc=[(FE, 1)])
                P.dma("scalar", mrow[0:1, 0:128].rearrange(
                    "x (h r) -> x h r", r=8),
                    kn_all.rearrange("r x h -> x h r"), XS, wait=[(CC, cc_kv)],
                    allow_slow_non_contiguous=True)
                xs_kn = XS.val
                vm(lambda e: e.tensor_reduce(
                    out=mrow[0:1, 128:144],
                    in_=mrow[0:1, 0:128].rearrange("x (h r) -> x h r", r=8),
                    axis=AX.X, op=ALU.max), wait=[(XS, xs_kn)])
                vm(lambda e: e.tensor_reduce(
                    out=mrow[0:1, 144:160],
                    in_=qnr[0:1, 0:32].rearrange("x (h t2) -> x h t2", t2=2),
                    axis=AX.X, op=ALU.max), wait=[(FE, nrm[0])])
                vm(lambda e: e.tensor_tensor(
                    out=mrow[0:1, 160:176], in0=mrow[0:1, 128:144],
                    in1=mrow[0:1, 144:160], op=ALU.mult))
                vm(lambda e: e.tensor_scalar(
                    out=mrow[0:1, 176:192], in0=mrow[0:1, 160:176],
                    scalar1=1.0 / HD, scalar2=None, op0=ALU.mult))
                P.op("scalar", lambda e: e.activation(
                    out=mrow[0:1, 192:208], in_=mrow[0:1, 176:192], func=AF.Sqrt),
                    wait=[(FE, FE.val)], inc=[(FE, 1)])
                vm(lambda e: e.tensor_scalar(
                    out=mrow[0:1, 208:224], in0=mrow[0:1, 192:208],
                    scalar1=-1.0, scalar2=None, op0=ALU.mult))
                fe_m = FE.val
                P.op("tensor", lambda e: e.matmul(
                    mps, ones_row_f, mrow[0:1, 208:224], start=True, stop=True,
                    skip_group_check=True), wait=[(FE, fe_m)], inc=[(PE, 1)])
                pe_m = PE.val
                P.op("vector", lambda e: e.tensor_copy(out=negm, in_=mps),
                     wait=[(PE, pe_m), (FE, FE.val)], inc=[(FE, 1)])
                fe_negm = FE.val
                if DEBUG:
                    P.dma("gpsimd", dbg_negm, negm, AC, wait=[(FE, fe_negm)])
                    P.dma("gpsimd", dbg_mrow, mrow, AC)
                    P.dma("gpsimd", dbg_qn, qnr, AC)
                    P.dma("gpsimd", dbg_q, qdram, AC, wait=[(AC, ac_qkv)])
                    P.dma("gpsimd", dbg_k, k_all[0:2], AC, wait=[(CC, cc_kv)])
                    P.dma("gpsimd", dbg_v, v_all[:, 0:128, :], AC)

            P.label = "attn"
            avsb_es = ExitStack()
            av_sb = [avsb_es.enter_context(
                nc.sbuf_tensor(f"avsb{i}", [128, NH * 512], F32))[:]
                for i in range(2)]
            with ExitStack() as aes:
                asb = lambda n, s, d: aes.enter_context(nc.sbuf_tensor(n, s, d))[:]
                apsf = lambda n, s, d: psum_bank(aes, nc, n, s, d)
                ks = [asb(f"ks{i}", [128, 16 * 512], BF16) for i in range(2)]
                vs = [asb(f"vs{i}", [128, 4 * 2048], BF16) for i in range(2)]
                qs = [asb(f"qs{i}", [128, 16 * 512], BF16) for i in range(2)]
                pb = [asb(f"pb{i}", [128, 512], BF16) for i in range(4)]
                pm = [asb(f"pm{i}", [128, 512], BF16) for i in range(4)]
                fencb = asb("fencb", [128, 8], BF16)
                cms = asb("cms", [128, 4 * 512], BF16)
                P.dma("sync", cms, cmi, LD)
                ldc = LD.val
                sut = [asb(f"sut{i}", [1, 512], F32) for i in range(2)]
                sps = [apsf(f"sps{i}", [128, 512], F32) for i in range(2)]
                avp = [apsf(f"avp{i}", [128, 512], F32) for i in range(2)]
                sup = [apsf(f"sup{i}", [128, 512], F32) for i in range(2)]

                kq_free = [0, 0]
                v_free = [0, 0]
                sps_free = [0, 0]
                avp_free = [0, 0]
                sup_free = [0, 0]
                sut_free = [0, 0]
                pb_free = [0, 0, 0, 0]
                pm_free = [0, 0, 0, 0]

                KVK = P.sem("KVK")
                KVV = P.sem("KVV")
                KVQ = P.sem("KVQ")

                def dyn_load(eng, t, col, buf_ap, ap_dims, sem):
                    sh = sem.h

                    def fn(e, t=t, col=col, buf_ap=buf_ap, ap_dims=ap_dims, sh=sh):
                        with e.register(f"r{eng}{t}") as r:
                            e.reg_load(r, tabs[0:1, t * 8 + col:t * 8 + col + 1])
                            roff = e.snap(r, True, min_val=0,
                                          max_val=8 * 2048 * 1024)
                            src_t = k_all.tensor if col == 0 else (
                                v_all.tensor if col == 1 else qdram.tensor)
                            e.dma_start(out=buf_ap,
                                        in_=RawAP(src_t, roff, ap_dims)
                                        ).then_inc(sh, 16)
                            return e.dma_start(
                                out=fencb,
                                in_=RawAP(src_t, roff, [[ap_dims[0][0], 128],
                                                        [1, 8]]))
                    P.op(eng, fn, wait=[(CC, cc_kv), (AC, ac_qkv),
                                        (PE, kq_free[t % 2] if col != 1
                                         else v_free[t % 2])],
                         inc=[(sem, 16)], ghost_inc=[(sem, 16)])
                    return sem.val

                K_AP = [[1024, 128], [128 * 1024, 16], [1, 512]]
                V_AP = [[2048, 128], [128 * 2048, 4], [1, 2048]]
                Q_AP = [[1024, 128], [128 * 1024, 16], [1, 512]]

                def load_slot(t):
                    par = t % 2
                    a = dyn_load("sync", t, 0,
                                 ks[par].rearrange("p (ic t2) -> p ic t2", t2=512),
                                 K_AP, KVK)
                    b = dyn_load("gpsimd", t, 1,
                                 vs[par].rearrange("p (kc d2) -> p kc d2", d2=2048),
                                 V_AP, KVV)
                    c = dyn_load("scalar", t, 2,
                                 qs[par].rearrange("p (ic t2) -> p ic t2", t2=512),
                                 Q_AP, KVQ)
                    return [(KVK, a), (KVV, b), (KVQ, c)]

                kv_ld = [None] * (NSLOT + 1)
                kv_ld[0] = load_slot(0)
                for t in range(NSLOT):
                    par = t % 2
                    if t + 1 < NSLOT:
                        kv_ld[t + 1] = load_slot(t + 1)
                    diag = t < 2
                    carry = [None, None]
                    for h in range(NH):
                        hp = h % 2

                        def a_score(kc, par=par, h=h):
                            sp = kc % 2
                            P.op("tensor", lambda e, par=par, h=h, kc=kc, sp=sp:
                                 e.matmul(
                                     sps[sp], ks[par][:, h * 512 + kc * 128:
                                                      h * 512 + (kc + 1) * 128],
                                     qs[par][:, h * 512:(h + 1) * 512],
                                     start=True, stop=True,
                                     skip_group_check=True),
                                 wait=kv_ld[t] + [(SC, sps_free[sp]),
                                                  (FE, fe_negm)],
                                 inc=[(PE, 1)])
                            return PE.val

                        def a_exp(kc, pe_s, h=h):
                            sp = kc % 2
                            P.op("scalar", lambda e, sp=sp, kc=kc, h=h:
                                 e.activation(
                                     out=pb[kc], in_=sps[sp], func=AF.Exp,
                                     bias=negm[:, h:h + 1], scale=SM_SCALE),
                                 wait=[(PE, pe_s), (PE, pb_free[kc])],
                                 inc=[(SC, 1)])
                            sps_free[sp] = SC.val
                            sc_e = SC.val
                            if diag:
                                P.op("vector", lambda e, kc=kc: e.tensor_tensor(
                                    out=pm[kc], in0=pb[kc],
                                    in1=cms[:, kc * 512:(kc + 1) * 512],
                                    op=ALU.mult),
                                    wait=[(SC, sc_e), (LD, ldc),
                                          (PE, pm_free[kc])],
                                    inc=[(DV, 1)])
                                return pm, [(DV, DV.val)]
                            return pb, [(SC, sc_e)]

                        def a_avsu(kc, src2, wv, par=par, h=h, hp=hp):
                            P.op("tensor", lambda e, par=par, h=h, kc=kc,
                                 s2=src2, hp=hp: e.matmul(
                                     avp[hp], vs[par][:, kc * 2048 + h * 128:
                                                      kc * 2048 + (h + 1) * 128],
                                     s2[kc], start=(kc == 0), stop=(kc == 3),
                                     skip_group_check=True),
                                 wait=wv + ([(DV, avp_free[hp])] + kv_ld[t]
                                            if kc == 0 else []))
                            P.op("tensor", lambda e, kc=kc, s2=src2, hp=hp:
                                 e.matmul(
                                     sup[hp], cbs[:, 129:257], s2[kc],
                                     start=(kc == 0), stop=(kc == 3),
                                     skip_group_check=True),
                                 wait=([(DV, sup_free[hp])] if kc == 0 else []),
                                 inc=[(PE, 1)])
                            pb_free[kc] = PE.val
                            pm_free[kc] = PE.val

                        pe_s = [None] * 4
                        if carry[0] is not None:
                            pe_s[0], pe_s[1] = carry
                        else:
                            pe_s[0] = a_score(0)
                            pe_s[1] = a_score(1)
                        nxt = [None, None]
                        for kc in range(4):
                            src2, wv = a_exp(kc, pe_s[kc])
                            if kc + 2 < 4:
                                pe_s[kc + 2] = a_score(kc + 2)
                            elif h + 1 < NH:
                                nxt[kc - 2] = a_score(kc - 2, h=h + 1)
                            a_avsu(kc, src2, wv)
                        carry = nxt
                        pe_av = PE.val
                        for qb2 in range(2):
                            if t == 0:
                                P.op("vector", lambda e, hp=hp, qb2=qb2, t=t, h=h:
                                     e.tensor_scalar(
                                         out=av_sb[qb2][:, h * 512:(h + 1) * 512],
                                         in0=avp[hp],
                                         scalar1=wts[:, qb2 * NSLOT + t:
                                                     qb2 * NSLOT + t + 1],
                                         scalar2=None, op0=ALU.mult),
                                     wait=([(PE, pe_av), (LD, ldv)]
                                           if qb2 == 0 else []),
                                     inc=[(DV, 1)])
                            else:
                                P.op("vector", lambda e, hp=hp, qb2=qb2, t=t, h=h:
                                     e.scalar_tensor_tensor(
                                         out=av_sb[qb2][:, h * 512:(h + 1) * 512],
                                         in0=avp[hp],
                                         scalar=wts[:, qb2 * NSLOT + t:
                                                    qb2 * NSLOT + t + 1],
                                         in1=av_sb[qb2][:, h * 512:(h + 1) * 512],
                                         op0=ALU.mult, op1=ALU.add),
                                     wait=([(PE, pe_av), (LD, ldv)]
                                           if qb2 == 0 else []),
                                     inc=[(DV, 1)])
                        avp_free[hp] = DV.val
                        P.op("vector", lambda e, hp=hp: e.tensor_copy(
                            out=sut[hp], in_=sup[hp][0:1, :]),
                            wait=[(AC, sut_free[hp])], inc=[(DV, 1)])
                        sup_free[hp] = DV.val
                        dv_su = DV.val
                        P.dma("gpsimd", sudram[t, h, :].unsqueeze(0), sut[hp], AC,
                              wait=[(DV, dv_su)])
                        dram_fence("gpsimd", sudram[t, h, 0:8].unsqueeze(0), AC)
                        sut_free[hp] = AC.val
                    kq_free[par] = PE.val
                    v_free[par] = PE.val
                if DEBUG:
                    P.dma("gpsimd", dbg_av[0], av_sb[0], AC,
                          wait=[(DV, DV.val)])
                    P.dma("gpsimd", dbg_av[1], av_sb[1], AC)
                    P.dma("gpsimd", dbg_wts, wts, AC)
                    P.dma("gpsimd", dbg_tab, tabs, AC)

            P.label = "oproj"
            dram_fence("gpsimd", sudram[NSLOT - 1, NH - 1, 0:8].unsqueeze(0), AC)
            with ExitStack() as oes:
                osb = lambda n, s, d: oes.enter_context(nc.sbuf_tensor(n, s, d))[:]
                opsf = lambda n, s, d: psum_bank(oes, nc, n, s, d)
                wos = osb("wos", [128, 16 * 2048], BF16)
                sus2 = [osb(f"sus{i}", [NSLOT, 512], F32) for i in range(2)]
                avb = osb("avb", [128, 16 * 512], BF16)
                pout = osb("pout", [128, 8 * 2048], BF16)
                bo_bc = osb("bo_bc", [128, D], F32)
                P.dma("sync", bo_bc, bo_bc_i, LD)
                ops = [opsf(f"ops{i}", [128, 512], F32) for i in range(2)]
                ibp = opsf("ibp", [128, 512], F32)
                sps2 = opsf("sps2", [1, 512], F32)
                jnk = opsf("jnk", [128, 512], F32)

                def pe_pad():
                    P.op("tensor", lambda e: e.matmul(
                        jnk, ident, cbs[:, 0:512], start=True, stop=True,
                        skip_group_check=True), inc=[(PE, 1)])
                    return PE.val

                ac_att = AC.val
                bar_o = P.barrier_vals()
                for r in range(8):
                    P.dma("sync", wos.rearrange("p (jc i) -> p jc i", i=2048)
                          [:, :, 256 * r:256 * (r + 1)],
                          wa_all[r, 3, :, :].rearrange("(jc p) i -> p jc i", p=128),
                          WG, wait=bar_o if r == 0 else [])
                wg_o = WG.val
                if DEBUG:
                    P.dma("gpsimd", dbg_su, sudram, AC, wait=[(AC, ac_att)])
                sus_xs = [0, 0]
                sus_pe = [0, 0]

                def sus_load(pos, h):
                    sl = pos % 2
                    P.dma("scalar", sus2[sl], sudram[:, h, :], XS,
                          wait=[(AC, ac_att), (PE, sus_pe[sl])])
                    sus_xs[sl] = XS.val

                avb_free = 0
                ops_free = [0, 0]
                dbg_gate = []
                for qb in range(2):
                    dv_att = DV.val
                    ibp_free = 0
                    row_free = 0
                    sps2_free = 0
                    hseq = ([0] if qb == 0 else []) + list(range(NH))
                    sus_load(0, hseq[0])
                    for pos, h in enumerate(hseq):
                        sl = pos % 2
                        if pos + 1 < len(hseq):
                            sus_load(pos + 1, hseq[pos + 1])
                        P.op("tensor", lambda e, qb=qb, sl=sl: e.matmul(
                            sps2, wtv[:, qb:qb + 1], sus2[sl],
                            start=True, stop=True, skip_group_check=True),
                            wait=[(XS, sus_xs[sl]), (LD, LD.val),
                                  (FE, sps2_free)],
                            inc=[(PE, 1)])
                        sus_pe[sl] = PE.val
                        pe_su = pe_pad()
                        P.op("vector", lambda e: e.reciprocal(
                            out=row[0:1, 0:512], in_=sps2[0:1, :]),
                            wait=[(PE, pe_su), (FE, FE.val), (PE, row_free)],
                            inc=[(FE, 1)])
                        fe_i = FE.val
                        sps2_free = fe_i
                        P.op("tensor", lambda e: e.matmul(
                            ibp, ones_row_f, row[0:1, 0:512], start=True, stop=True,
                            skip_group_check=True),
                            wait=[(FE, fe_i), (DV, ibp_free)], inc=[(PE, 1)])
                        pe_i = pe_pad()
                        row_free = pe_i
                        P.op("vector", lambda e, qb=qb, h=h: e.tensor_tensor(
                            out=avb[:, h * 512:(h + 1) * 512],
                            in0=av_sb[qb][:, h * 512:(h + 1) * 512], in1=ibp,
                            op=ALU.mult),
                            wait=[(PE, pe_i), (DV, dv_att), (PE, avb_free)]
                            + dbg_gate,
                            inc=[(DV, 1)])
                        ibp_free = DV.val
                    dv_avb = DV.val
                    if DEBUG and qb == 0:
                        P.dma("gpsimd", dbg_avb, avb, AC, wait=[(DV, dv_avb)])
                        dbg_gate = [(AC, AC.val)]
                    for t in range(4):
                        gt = qb * 4 + t
                        psl = pout[:, gt * 2048:(gt + 1) * 2048]
                        for i in range(4):
                            par = i % 2
                            for h in range(NH):
                                P.op("tensor", lambda e, t=t, i=i, h=h, par=par:
                                     e.matmul(
                                         ops[par], avb[:, h * 512 + t * 128:
                                                       h * 512 + (t + 1) * 128],
                                         wos[:, h * 2048 + i * 512:
                                             h * 2048 + (i + 1) * 512],
                                         start=(h == 0), stop=(h == 15),
                                         skip_group_check=True),
                                     wait=([(WG, wg_o), (DV, dv_avb),
                                            (DV, ops_free[par])] if h == 0 else []),
                                     inc=[(PE, 1)] if h == 15 else [])
                            pe_o = PE.val
                            P.op("vector", lambda e, i=i, par=par, psl=psl:
                                 e.scalar_tensor_tensor(
                                     out=psl[:, i * 512:(i + 1) * 512],
                                     in0=ops[par], scalar=1.0,
                                     in1=bo_bc[:, i * 512:(i + 1) * 512],
                                     op0=ALU.mult, op1=ALU.add),
                                 wait=[(PE, pe_o), (LD, LD.val)],
                                 inc=[(DV, 1)])
                            ops_free[par] = DV.val
                        dv_p = DV.val
                        P.dma("sync", podram[gt * 128:(gt + 1) * 128, :], psl, AC,
                              wait=[(DV, dv_p)])
                    avb_free = PE.val
                dram_fence("sync", podram[0:128, 0:8], AC)
            avsb_es.close()

        if STAGE == "full":
            ffn_pass("f2", xdram, out, proj_dram=podram)
        else:
            bar = P.barrier_vals()
            with ExitStack() as x2es:
                x2b = x2es.enter_context(
                    nc.sbuf_tensor("x2asm", [128, 2048], F32))[:]
                p2b = x2es.enter_context(
                    nc.sbuf_tensor("p2asm", [128, 2048], BF16))[:]
                fe_a = FE.val
                for i in range(8):
                    P.dma("scalar", x2b, xdram[i * 128:(i + 1) * 128, :], XS,
                          wait=(bar if i == 0 else []) + [(AC, AC.val)])
                    P.dma("scalar", p2b, podram[i * 128:(i + 1) * 128, :], XS)
                    xs_a = XS.val
                    P.op("vector", lambda e: e.tensor_tensor(
                        out=x2b, in0=x2b, in1=p2b, op=ALU.add),
                        wait=[(XS, xs_a), (FE, fe_a)], inc=[(FE, 1)])
                    fe_a = FE.val
                    P.dma("gpsimd", out[i * 128:(i + 1) * 128, :], x2b, AC,
                          wait=[(FE, fe_a)])
        dram_fence("gpsimd", out[0:128, 0:8], AC)
        P.op("gpsimd", None, wait=P.barrier_vals())

        P.check_schedule()
        blk = es.enter_context(nc.Block())
        P.replay(blk)
    return nc


_CACHE = {}


def kernel(**inputs):
    x = np.asarray(inputs["L0_attn_hidden_states"]).reshape(S, D)
    wgu = np.asarray(inputs["L0_ffn_w_gate_up"])
    wp = np.asarray(inputs["L0_ffn_w_proj"])
    cbn, cfn, cmn = host_constants()
    bq = np.asarray(inputs["L0_attn_bq"]).astype(np.float32)
    bk = np.asarray(inputs["L0_attn_bk"]).astype(np.float32)
    bv = np.asarray(inputs["L0_attn_bv"]).astype(np.float32)
    bo = np.asarray(inputs["L0_attn_bo"]).astype(np.float32)
    bqk_tm = np.concatenate([bq.reshape(16, 128).T, bk.reshape(16, 128).T], 1)
    in_maps = []
    for c in range(NCORE):
        rows = core_rows(c)
        gate = wgu[1024 * c:1024 * (c + 1), :]
        up = wgu[8192 + 1024 * c:8192 + 1024 * (c + 1), :]
        m = {
            "xin": np.ascontiguousarray(x[rows]),
            "wgu_s": np.ascontiguousarray(np.concatenate([gate, up], 0)),
            "wp_s": np.ascontiguousarray(wp[:, 1024 * c:1024 * (c + 1)]),
            "wa_s": np.ascontiguousarray(np.stack(
                [np.asarray(inputs[f"L0_attn_W{w}"])[256 * c:256 * (c + 1), :]
                 for w in "qkvo"])),
            "bqk_tm_i": np.ascontiguousarray(bqk_tm.astype(np.float32)),
            "bv_bc_i": np.ascontiguousarray(np.tile(bv[None, :], (128, 1))),
            "bo_bc_i": np.ascontiguousarray(np.tile(bo[None, :], (128, 1))),
            "cbi": cbn, "cfi": cfn, "cmi": cmn,
            "tabi": slot_table(c)[0], "wti": slot_table(c)[1],
            "wtvi": np.ascontiguousarray(
                slot_table(c)[1][0, :].reshape(2, NSLOT).T.astype(np.float32)),
        }
        in_maps.append(m)
    if "nc" not in _CACHE:
        _CACHE["nc"] = build_program()
    res = run_bass_kernel_spmd(_CACHE["nc"], in_maps, core_ids=list(range(NCORE)))
    full = np.zeros((S, D), np.float32)
    for c in range(NCORE):
        full[core_rows(c)] = res.results[c]["out"]
    return full.reshape(1, S, D)


# version 05 (final)
# speedup vs baseline: 1.2103x; 1.0139x over previous
import numpy as np
import ml_dtypes
from contextlib import ExitStack

import concourse.bass as bass
import concourse.mybir as mybir
from concourse.bass_utils import run_bass_kernel_spmd
from concourse.ap import AP as RawAP

F32 = mybir.dt.float32
BF16 = mybir.dt.bfloat16
I32 = mybir.dt.int32
AF = mybir.ActivationFunctionType
ALU = mybir.AluOpType
AX = mybir.AxisListType
ds = bass.ds

NCORE = 8
S, D, T = 8192, 2048, 1024
NH, HD = 16, 128
FF = 8192
NJ = 16
NPAIR = 64
NSLOT = 17
C_THR = 0.5009765625
SM_SCALE = float(HD) ** -0.5
RMS_EPS = float(np.float32(np.finfo(np.float32).eps))
LN_EPS = 1e-5
BF = ml_dtypes.bfloat16

STAGE = "full"
DEBUG = False


def core_rows(c):
    return list(range(512 * c, 512 * c + 512)) + list(range(7680 - 512 * c, 8192 - 512 * c))


def slot_table(c):
    def sup(s):
        return (s, 0) if s <= 7 else (15 - s, 512)
    slots = [(0, c), (1, 15 - c)]
    slots += [(0, s) for s in range(c)]
    slots += [(1, s) for s in range(15 - c)]
    assert len(slots) == NSLOT
    rows = []
    for qb, s in slots:
        r, o = sup(s)
        koff = r * 2048 * 1024 + o
        voff = r * 1024 * 2048 + o * 2048
        rows.append([koff, voff, 512 * qb, qb, 0, 0, 0, 0])
    wt = np.zeros((128, 2 * NSLOT), np.float32)
    for t, (qb, s) in enumerate(slots):
        wt[:, qb * NSLOT + t] = 1.0
    return np.asarray(rows, np.int32).reshape(1, -1), wt


def host_constants():
    cbn = np.zeros((128, 640), dtype=BF)
    cbn[:, 0:128] = np.eye(128, dtype=BF)
    cbn[:, 128] = 1.0
    cbn[:, 129:257] = 1.0
    cfn = np.zeros((128, 160), np.float32)
    cfn[:, 0] = SM_SCALE
    cfn[:, 16:144] = 1.0
    q = np.arange(512)[None, :]
    k = np.arange(128)[:, None]
    cmn = np.zeros((128, 4 * 512), dtype=BF)
    for j in range(4):
        cmn[:, j * 512:(j + 1) * 512] = (q >= 128 * j + k).astype(BF)
    return cbn, cfn, cmn


class Sem:
    def __init__(self, es, nc, name):
        self.h = es.enter_context(nc.semaphore(name))
        self.name = name
        self.val = 0


class Prog:
    ENGINES = ("sync", "gpsimd", "vector", "scalar", "tensor")

    def __init__(self, nc, es):
        self.nc = nc
        self.es = es
        self.ops = {e: [] for e in self.ENGINES}
        self.sems = {}
        self.label = "init"

    def sem(self, name):
        if name not in self.sems:
            self.sems[name] = Sem(self.es, self.nc, name)
        return self.sems[name]

    def op(self, eng, fn=None, wait=(), inc=(), ghost_inc=()):
        wait = [(s, v) for (s, v) in wait if v > 0]
        inc2 = []
        for (s, n) in inc:
            s.val += n
            inc2.append((s, n, True))
        for (s, n) in ghost_inc:
            s.val += n
            inc2.append((s, n, False))
        assert fn is not None or not inc2
        self.ops[eng].append((fn, wait, inc2, self.label))

    def dma(self, eng, out, in_, sem, wait=(), **kw):
        self.op(eng, lambda e, out=out, in_=in_, kw=kw: e.dma_start(out=out, in_=in_, **kw),
                wait=wait, inc=[(sem, 16)])
        return sem.val

    def barrier_vals(self):
        return [(s, s.val) for s in self.sems.values()]

    def hard_barrier(self):
        bv = self.barrier_vals()
        for e in self.ENGINES:
            self.op(e, None, wait=bv)

    def check_schedule(self):
        idx = {e: 0 for e in self.ENGINES}
        vals = {s.name: 0 for s in self.sems.values()}
        total = sum(len(v) for v in self.ops.values())
        done = 0
        while done < total:
            progressed = False
            for e in self.ENGINES:
                while idx[e] < len(self.ops[e]):
                    fn, wait, incs, _lb = self.ops[e][idx[e]]
                    if any(vals[s.name] < v for (s, v) in wait):
                        break
                    for (s, n, _att) in incs:
                        vals[s.name] += n
                    idx[e] += 1
                    done += 1
                    progressed = True
            if not progressed:
                detail = []
                for e in self.ENGINES:
                    if idx[e] < len(self.ops[e]):
                        _, wait, _, lb = self.ops[e][idx[e]]
                        w = [(s.name, v, vals[s.name]) for (s, v) in wait
                             if vals[s.name] < v]
                        detail.append(f"{e}[op {idx[e]}/{len(self.ops[e])} @{lb}] blocked on {w}")
                        for (s, v) in wait:
                            if vals[s.name] < v:
                                acc = dict(vals)
                                for e2 in self.ENGINES:
                                    run = vals[s.name]
                                    for j in range(idx[e2], len(self.ops[e2])):
                                        _, _, incs2, lb2 = self.ops[e2][j]
                                        for (s2, n2) in incs2:
                                            if s2.name == s.name:
                                                run += n2
                                        if run >= v:
                                            detail.append(
                                                f"   {s.name}->{v} would come from "
                                                f"{e2}[{j} @{lb2}]")
                                            break
                                    else:
                                        continue
                                    break
                raise RuntimeError("DEADLOCK:\n" + "\n".join(detail))

    def replay(self, block):
        def mk(engname):
            def emit(eng):
                for fn, wait, incs, _lb in self.ops[engname]:
                    for s, v in wait:
                        eng.wait_ge(s.h, v)
                    if fn is not None:
                        ins = fn(eng)
                        for s, n, att in incs:
                            if att:
                                ins = ins.then_inc(s.h, n)
            return emit
        block.sync(mk("sync"))
        block.gpsimd(mk("gpsimd"))
        block.vector(mk("vector"))
        block.scalar(mk("scalar"))
        block.tensor(mk("tensor"))


def psum_bank(es, nc, name, shape, dtype):
    esz = 2 if dtype == BF16 else 4
    per = shape[-1] * esz
    assert per <= 2048
    pad_last = 2048 // esz
    full = list(shape[:-1]) + [pad_last]
    if len(shape) == 2 and shape[0] < 128:
        full[0] = 128
    t = es.enter_context(nc.psum_tensor(name, full, dtype))
    ap = t[:]
    sl = tuple([slice(0, shape[i]) for i in range(len(shape) - 1)]
               + [slice(0, shape[-1])])
    return ap[sl]


def build_program():
    nc = bass.Bass("TRN2", target_bir_lowering=False, detect_race_conditions=False)
    g = lambda n, s, d, k="ExternalInput": nc.dram_tensor(n, s, d, kind=k).ap()

    xin = g("xin", [T, D], F32)
    wgu_s = g("wgu_s", [2048, 2048], F32)
    wp_s = g("wp_s", [2048, 1024], F32)
    wa_s = g("wa_s", [4, 256, 2048], F32)
    bqk_tm_i = g("bqk_tm_i", [128, 32], F32)
    bv_bc_i = g("bv_bc_i", [128, D], F32)
    bo_bc_i = g("bo_bc_i", [128, D], F32)
    cbi = g("cbi", [128, 640], BF16)
    cfi = g("cfi", [128, 160], F32)
    cmi = g("cmi", [128, 4 * 512], BF16)
    tabi = g("tabi", [1, NSLOT * 8], I32)
    wti = g("wti", [128, 2 * NSLOT], F32)
    wtvi = g("wtvi", [NSLOT, 2], F32)
    out = g("out", [T, D], F32, "ExternalOutput")
    if DEBUG:
        dbg_yfm = g("dbg_yfm", [128, 16 * 512], BF16, "ExternalOutput")
        dbg_hfm = g("dbg_hfm", [128, 64 * 512], BF16, "ExternalOutput")
        dbg_stat = g("dbg_stat", [128, 64], F32, "ExternalOutput")
        dbg_hinv = g("dbg_hinv", [128, 4], F32, "ExternalOutput")
        dbg_negm = g("dbg_negm", [128, NH], F32, "ExternalOutput")
        dbg_mrow = g("dbg_mrow", [1, 512], F32, "ExternalOutput")
        dbg_su = g("dbg_su", [NSLOT, NH, 512], F32, "ExternalOutput")
        dbg_av = g("dbg_av", [2, 128, NH * 512], F32, "ExternalOutput")
        dbg_qn = g("dbg_qn", [1, 64], F32, "ExternalOutput")
        dbg_pm = g("dbg_pm", [4, 128, 512], BF16, "ExternalOutput")
        dbg_ks = g("dbg_ks", [128, 16 * 512], BF16, "ExternalOutput")
        dbg_vs = g("dbg_vs", [128, 4 * 2048], BF16, "ExternalOutput")
        dbg_qs = g("dbg_qs", [128, 16 * 512], BF16, "ExternalOutput")
        dbg_avt = g("dbg_avt", [128, 1024], F32, "ExternalOutput")
        dbg_sps = g("dbg_sps", [128, 512], F32, "ExternalOutput")
        dbg_sut = g("dbg_sut", [1, 1024], F32, "ExternalOutput")
        dbg_q = g("dbg_q", [2048, 1024], BF16, "ExternalOutput")
        dbg_y2 = g("dbg_y2", [128, 16 * 1024], BF16, "ExternalOutput")
        dbg_qst = g("dbg_qst", [128, 64], F32, "ExternalOutput")
        dbg_yb = g("dbg_yb", [128, 2048], BF16, "ExternalOutput")
        dbg_xh = g("dbg_xh", [128, 2048], F32, "ExternalOutput")
        dbg_wts = g("dbg_wts", [128, 2 * NSLOT], F32, "ExternalOutput")
        dbg_tab = g("dbg_tab", [1, NSLOT * 8], I32, "ExternalOutput")
        dbg_avb = g("dbg_avb", [128, NH * 512], BF16, "ExternalOutput")
        dbg_srow = g("dbg_srow", [1, NH * 512], F32, "ExternalOutput")
        dbg_k = g("dbg_k", [2, 2048, 1024], BF16, "ExternalOutput")
        dbg_v = g("dbg_v", [8, 128, 2048], BF16, "ExternalOutput")

    idr = lambda n, s, d: nc.dram_tensor(n, s, d).ap()
    sdr = lambda n, s, d: nc.dram_tensor(n, s, d, addr_space="Shared").ap()
    wgu_in = idr("wgu_in", [2048, 2048], BF16)
    wp_in = idr("wp_in", [1024, 2048], BF16)
    wa_in = idr("wa_in", [4, 2048, 256], BF16)
    wgu_all = sdr("wgu_all", [8, 2048, 2048], BF16)
    wp_all = sdr("wp_all", [8, 1024, 2048], BF16)
    wa_all = sdr("wa_all", [8, 4, 2048, 256], BF16)
    k_in = idr("k_in", [2048, 1024], BF16)
    v_in = idr("v_in", [1024, 2048], BF16)
    kn_in = idr("kn_in", [1, 16], F32)
    k_all = sdr("k_all", [8, 2048, 1024], BF16)
    v_all = sdr("v_all", [8, 1024, 2048], BF16)
    kn_all = sdr("kn_all", [8, 1, 16], F32)
    xdram = idr("xdram", [T, D], F32)
    qdram = idr("qdram", [2048, 1024], BF16)
    sudram = idr("sudram", [NSLOT, NH, 512], F32)
    podram = idr("podram", [T, D], BF16)

    es = ExitStack()
    with es:
        P = Prog(nc, es)
        sb = lambda n, s, d: es.enter_context(nc.sbuf_tensor(n, s, d))[:]

        LD = P.sem("LD")
        WG = P.sem("WG")
        XS = P.sem("XS")
        PP = P.sem("PP")
        CC = P.sem("CC")
        AC = P.sem("AC")
        KV = P.sem("KV")
        PE = P.sem("PE")
        DV = P.sem("DV")
        SC = P.sem("SC")
        FE = P.sem("FE")

        cbs = sb("cbs", [128, 640], BF16)
        cfs = sb("cfs", [128, 160], F32)
        tabs = sb("tabs", [1, NSLOT * 8], I32)
        wts = sb("wts", [128, 2 * NSLOT], F32)
        wtv = sb("wtv", [NSLOT, 2], F32)
        bqk_tm = sb("bqk_tm", [128, 32], F32)
        stat = sb("stat", [128, 64], F32)
        fsc = sb("fsc", [128, 16], F32)
        fsb = sb("fsb", [128, 16], BF16)

        def dram_fence(eng, dram_ap, sem):
            p, f = dram_ap.shape[0], dram_ap.shape[1]
            scr = fsb if dram_ap.tensor.dtype == BF16 else fsc
            P.dma(eng, scr[0:p, 0:f], dram_ap, sem)
            return sem.val
        row = sb("row", [1, 2048], F32)
        negm = sb("negm", [128, NH], F32)

        nc.const_aps.register(F32, SM_SCALE, cfs[:, 0:1])
        ident = cbs[:, 0:128]
        ones_col_b = cbs[:, 128:129]
        ones_row_f = cfs[0:1, 16:144]

        for src, dst in [(cbi, cbs), (cfi, cfs), (tabi, tabs),
                         (bqk_tm_i, bqk_tm), (wti, wts), (wtvi, wtv)]:
            P.dma("sync", dst, src, LD)
        ldv = LD.val

        P.label = "preproc"
        with ExitStack() as pes:
            psb = lambda n, s, d: pes.enter_context(nc.sbuf_tensor(n, s, d))[:]
            pps = lambda n, s, d: psum_bank(pes, nc, n, s, d)
            wfp = [psb(f"ppwfp{i}", [128, 2048], F32) for i in range(2)]
            wbb = [psb(f"ppwbb{i}", [128, 2048], BF16) for i in range(2)]
            wab = psb("ppwab", [128, 2048], BF16)
            wsg = psb("ppwsg", [128, 2048], BF16)
            gsc = psb("ppgsc", [128, 32], F32)
            scb = psb("ppscb", [128, 32], BF16)
            thr = psb("ppthr", [128, 32], F32)
            msk = psb("ppmsk", [128, 2048], BF16)
            wht = psb("ppwht", [128, 2048], BF16)
            dd_ = psb("ppdd", [128, 2048], BF16)
            fwd = [psb(f"ppfwd{i}", [128, 2048], BF16) for i in range(2)]
            asmb = [psb(f"ppasm{i}", [128, 2048], BF16) for i in range(2)]
            tp_ps = [pps(f"pptps{i}", [128, 128], BF16) for i in range(2)]

            wfp_free = [0, 0]
            fwd_free = [0, 0]
            asm_free = [0, 0]
            tp_free = [0, 0]

            def quant_tile(par, C, quant, wg_ld):
                G = C // 64
                P.op("vector", lambda e, p=par, C=C: e.tensor_copy(
                    out=wbb[p][:, :C], in_=wfp[p][:, :C]),
                    wait=[(WG, wg_ld)], inc=[(FE, 1)])
                wfp_free[par] = FE.val
                e0 = FE.val
                if not quant:
                    P.op("vector", lambda e, p=par, C=C: e.tensor_copy(
                        out=fwd[p][:, :C], in_=wbb[p][:, :C]),
                        wait=[(FE, e0), (PE, fwd_free[par])], inc=[(FE, 1)])
                    return
                P.op("scalar", lambda e, p=par, C=C: e.activation(
                    out=wab[:, :C], in_=wbb[p][:, :C], func=AF.Abs),
                    wait=[(FE, e0), (FE, FE.val)])
                P.op("scalar", lambda e, p=par, C=C: e.sign(
                    out=wsg[:, :C], in_=wbb[p][:, :C]), inc=[(SC, 1)])
                scv = SC.val
                P.op("vector", lambda e, p=par, C=C, G=G: e.tensor_reduce(
                    out=gsc[:, :G], in_=wbb[p][:, :C].rearrange("p (g k) -> p g k", k=64),
                    axis=AX.X, op=ALU.add, apply_absolute_value=True),
                    wait=[(FE, e0)], inc=[(FE, 1)])
                P.op("vector", lambda e, G=G: e.tensor_scalar(
                    out=scb[:, :G], in0=gsc[:, :G], scalar1=1.0 / 64, scalar2=1e-8,
                    op0=ALU.mult, op1=ALU.max), wait=[(FE, FE.val)], inc=[(FE, 1)])
                P.op("vector", lambda e, G=G: e.tensor_scalar(
                    out=thr[:, :G], in0=scb[:, :G], scalar1=C_THR, scalar2=None,
                    op0=ALU.mult), wait=[(FE, FE.val)], inc=[(FE, 1)])
                P.op("vector", lambda e, C=C, G=G: e.tensor_tensor(
                    out=msk[:, :C].rearrange("p (g k) -> p g k", k=64),
                    in0=wab[:, :C].rearrange("p (g k) -> p g k", k=64),
                    in1=thr[:, :G].unsqueeze(2).broadcast_to([128, G, 64]),
                    op=ALU.is_gt), wait=[(FE, FE.val), (SC, scv)], inc=[(FE, 1)])
                P.op("vector", lambda e, C=C: e.tensor_tensor(
                    out=wht[:, :C], in0=msk[:, :C], in1=wsg[:, :C], op=ALU.mult),
                    wait=[(FE, FE.val)], inc=[(FE, 1)])
                P.op("vector", lambda e, C=C, G=G: e.tensor_tensor(
                    out=wht[:, :C].rearrange("p (g k) -> p g k", k=64),
                    in0=wht[:, :C].rearrange("p (g k) -> p g k", k=64),
                    in1=scb[:, :G].unsqueeze(2).broadcast_to([128, G, 64]),
                    op=ALU.mult), wait=[(FE, FE.val)], inc=[(FE, 1)])
                P.op("vector", lambda e, p=par, C=C: e.tensor_tensor(
                    out=dd_[:, :C], in0=wht[:, :C], in1=wbb[p][:, :C], op=ALU.subtract),
                    wait=[(FE, FE.val)], inc=[(FE, 1)])
                P.op("vector", lambda e, p=par, C=C: e.tensor_tensor(
                    out=fwd[p][:, :C], in0=wbb[p][:, :C], in1=dd_[:, :C], op=ALU.add),
                    wait=[(FE, FE.val), (PE, fwd_free[par])], inc=[(FE, 1)])

            def transpose_out(par, C, dst):
                fv = FE.val
                for jc in range(C // 128):
                    tp = jc % 2
                    P.op("tensor", lambda e, p=par, jc=jc, tp=tp: e.transpose(
                        tp_ps[tp][:], fwd[p][:, jc * 128:(jc + 1) * 128], ident),
                        wait=[(FE, fv), (LD, ldv), (DV, tp_free[tp]),
                              (PP, asm_free[par])],
                        inc=[(PE, 1)])
                    pev = PE.val
                    P.op("vector", lambda e, p=par, jc=jc, tp=tp: e.tensor_copy(
                        out=asmb[p][:, jc * 128:(jc + 1) * 128], in_=tp_ps[tp][:]),
                        wait=[(PE, pev)], inc=[(DV, 1)])
                    tp_free[tp] = DV.val
                fwd_free[par] = PE.val
                P.dma("gpsimd", dst, asmb[par][:, :C], PP, wait=[(DV, DV.val)])
                asm_free[par] = PP.val

            pp_mark = {}
            ti = 0
            for i in range(16):
                par = ti % 2
                P.dma("sync", wfp[par], wgu_s[i * 128:(i + 1) * 128, :], WG,
                      wait=[(FE, wfp_free[par])])
                quant_tile(par, 2048, True, WG.val)
                col = i * 128 if i < 8 else 1024 + (i - 8) * 128
                transpose_out(par, 2048,
                              wgu_in[:, col:col + 128].rearrange("(jc p) h -> p jc h", p=128))
                ti += 1
            pp_mark["gu"] = PP.val
            for i in range(16):
                par = ti % 2
                P.dma("sync", wfp[par][:, 0:1024], wp_s[i * 128:(i + 1) * 128, :], WG,
                      wait=[(FE, wfp_free[par])])
                quant_tile(par, 1024, True, WG.val)
                transpose_out(par, 1024,
                              wp_in[:, i * 128:(i + 1) * 128].rearrange(
                                  "(hc p) d2 -> p hc d2", p=128))
                ti += 1
            pp_mark["wp"] = PP.val
            for w in range(4):
                for i in range(2):
                    par = ti % 2
                    P.dma("sync", wfp[par], wa_s[w, i * 128:(i + 1) * 128, :], WG,
                          wait=[(FE, wfp_free[par])])
                    quant_tile(par, 2048, False, WG.val)
                    transpose_out(par, 2048,
                                  wa_in[w, :, i * 128:(i + 1) * 128].rearrange(
                                      "(jc p) i2 -> p jc i2", p=128))
                    ti += 1
            pp_mark["wa"] = PP.val

            def ag(inp, outp, ppv):
                P.op("gpsimd", lambda e, i=inp, o=outp: e.collective_compute(
                    "AllGather", ALU.bypass, replica_groups=[list(range(8))],
                    ins=[i.opt()], outs=[o.opt()]),
                    wait=[(PP, ppv)], inc=[(CC, 1)])
                return CC.val
            cc_gu = ag(wgu_in, wgu_all, pp_mark["gu"])
            ag(wp_in, wp_all, pp_mark["wp"])
            cc_w = ag(wa_in.rearrange("w a b -> (w a) b"),
                      wa_all.rearrange("r w a b -> r (w a) b"), pp_mark["wa"])

        yb_free = [0]

        def ln_half(xh, ydst, sqs, yb, yps, yps_free, extra=()):
            xs_x = XS.val
            for t in range(4):
                P.op("vector", lambda e, xh=xh, t=t: e.tensor_reduce(
                    out=stat[:, t:t + 1], in_=xh[:, t * 2048:(t + 1) * 2048],
                    axis=AX.X, op=ALU.add), wait=[(XS, xs_x), (FE, FE.val)],
                    inc=[(FE, 1)])
                P.op("scalar", lambda e, xh=xh, sqs=sqs, t=t: e.activation(
                    out=sqs, in_=xh[:, t * 2048:(t + 1) * 2048], func=AF.Square,
                    accum_out=stat[:, 4 + t:5 + t]),
                    wait=[(XS, xs_x)] + list(extra), inc=[(SC, 1)])
            sc0 = SC.val

            def v(fn, wait=()):
                P.op("vector", fn, wait=list(wait) + [(FE, FE.val)], inc=[(FE, 1)])
            v(lambda e: e.tensor_scalar(out=stat[:, 8:12], in0=stat[:, 0:4],
                                        scalar1=1.0 / D, scalar2=None, op0=ALU.mult),
              wait=[(SC, sc0)])
            v(lambda e: e.tensor_scalar(out=stat[:, 12:16], in0=stat[:, 4:8],
                                        scalar1=1.0 / D, scalar2=None, op0=ALU.mult))
            v(lambda e: e.tensor_tensor(out=stat[:, 16:20], in0=stat[:, 8:12],
                                        in1=stat[:, 8:12], op=ALU.mult))
            v(lambda e: e.tensor_tensor(out=stat[:, 20:24], in0=stat[:, 12:16],
                                        in1=stat[:, 16:20], op=ALU.subtract))
            v(lambda e: e.tensor_scalar(out=stat[:, 24:28], in0=stat[:, 20:24],
                                        scalar1=LN_EPS, scalar2=None, op0=ALU.add))
            v(lambda e: e.reciprocal(out=stat[:, 28:32], in_=stat[:, 24:28]))
            P.op("scalar", lambda e: e.activation(out=stat[:, 32:36], in_=stat[:, 28:32],
                                                  func=AF.Sqrt),
                 wait=[(FE, FE.val)], inc=[(FE, 1)])
            v(lambda e: e.tensor_tensor(out=stat[:, 36:40], in0=stat[:, 32:36],
                                        in1=stat[:, 32:36], op=ALU.mult))
            v(lambda e: e.tensor_tensor(out=stat[:, 40:44], in0=stat[:, 36:40],
                                        in1=stat[:, 24:28], op=ALU.mult))
            v(lambda e: e.tensor_scalar(out=stat[:, 44:48], in0=stat[:, 40:44],
                                        scalar1=-0.5, scalar2=1.5,
                                        op0=ALU.mult, op1=ALU.add))
            v(lambda e: e.tensor_tensor(out=stat[:, 48:52], in0=stat[:, 32:36],
                                        in1=stat[:, 44:48], op=ALU.mult))
            v(lambda e: e.scalar_tensor_tensor(out=stat[:, 52:56], in0=stat[:, 8:12],
                                               scalar=-1.0, in1=stat[:, 48:52],
                                               op0=ALU.mult, op1=ALU.mult))
            fe_ln = FE.val
            for t in range(4):
                P.op("scalar", lambda e, xh=xh, yb=yb, t=t: e.activation(
                    out=yb, in_=xh[:, t * 2048:(t + 1) * 2048], func=AF.Identity,
                    bias=stat[:, 52 + t:53 + t], scale=stat[:, 48 + t:49 + t]),
                    wait=[(FE, fe_ln), (PE, yb_free[0])], inc=[(SC, 1)])
                sc_y = SC.val
                for jc in range(16):
                    tp = jc % 2
                    P.op("tensor", lambda e, yb=yb, yps=yps, jc=jc, tp=tp: e.transpose(
                        yps[tp][:], yb[:, jc * 128:(jc + 1) * 128], ident),
                        wait=[(SC, sc_y), (LD, ldv), (DV, yps_free[tp])],
                        inc=[(PE, 1)])
                    pev = PE.val
                    P.op("vector", lambda e, yps=yps, t=t, jc=jc, tp=tp,
                         ydst=ydst: e.tensor_copy(out=ydst(t, jc), in_=yps[tp][:]),
                        wait=[(PE, pev)], inc=[(DV, 1)])
                    yps_free[tp] = DV.val
                yb_free[0] = PE.val
            return DV.val

        def ffn_pass(tag, src_dram, dst_dram, proj_dram=None):
            P.label = tag
            with ExitStack() as fes:
                fsb = lambda n, s, d: fes.enter_context(nc.sbuf_tensor(tag + n, s, d))[:]
                xh = fsb("xh", [128, 4 * 2048], F32)
                x1h = fsb("x1h", [128, 4 * 2048], F32)
                yb = fsb("yb", [128, 2048], BF16)
                yfm = fsb("yfm", [128, 16 * 512], BF16)
                hfm = fsb("hfm", [128, 64 * 512], BF16)
                sg = [fsb(f"sg{i}", [128, 512], BF16) for i in range(2)]
                h2 = [fsb(f"h2{i}", [128, 512], BF16) for i in range(2)]
                wgu = [fsb(f"wgu{i}", [128, 16 * 256], BF16) for i in range(2)]
                wpb = [fsb(f"wpb{i}", [128, 8 * 512], BF16) for i in range(2)]
                sumr = fsb("sumr", [1, 512], F32)
                invr = fsb("invr", [1, 512], F32)
                hinv = fsb("hinv", [128, 4], F32)
                if proj_dram is not None:
                    pb2 = fsb("pb2", [128, 2048], BF16)

                for half in range(2):
                    hx = f"{half}"
                    P.label = f"{tag}h{half}"
                    bar = P.barrier_vals()
                    for t in range(4):
                        P.dma("scalar", xh[:, t * 2048:(t + 1) * 2048],
                              src_dram[(half * 4 + t) * 128:(half * 4 + t + 1) * 128, :],
                              XS, wait=bar if t == 0 else [])
                    extra = []
                    if proj_dram is not None:
                        fe_pr = FE.val
                        for t in range(4):
                            P.dma("scalar", pb2,
                                  proj_dram[(half * 4 + t) * 128:
                                            (half * 4 + t + 1) * 128, :],
                                  XS, wait=[(FE, fe_pr)])
                            xs_p = XS.val
                            P.op("vector", lambda e, t=t: e.tensor_tensor(
                                out=xh[:, t * 2048:(t + 1) * 2048],
                                in0=xh[:, t * 2048:(t + 1) * 2048], in1=pb2,
                                op=ALU.add),
                                wait=[(XS, xs_p), (FE, FE.val)], inc=[(FE, 1)])
                            fe_pr = FE.val
                        extra = [(FE, fe_pr)]
                    with ExitStack() as lnes:
                        yps = [psum_bank(lnes, nc, f"{tag}yps{hx}{i}",
                                         [128, 128], BF16) for i in range(2)]
                        yps_free = [0, 0]
                        dv_y = ln_half(xh, lambda t, jc: yfm[:, jc * 512 + t * 128:
                                                            jc * 512 + (t + 1) * 128],
                                       x1h[:, 0:2048], yb, yps, yps_free,
                                       extra=extra)
                    with ExitStack() as gues:
                        gps = [psum_bank(gues, nc, f"{tag}gps{hx}{i}",
                                         [128, 512], F32) for i in range(2)]
                        ups = [psum_bank(gues, nc, f"{tag}ups{hx}{i}",
                                         [128, 512], F32) for i in range(2)]
                        rps = psum_bank(gues, nc, f"{tag}rps{hx}", [1, 512], F32)
                        gps_free = [0, 0]
                        ups_free = [0, 0]
                        wgu_free = [0, 0]
                        for p in range(NPAIR):
                            par = p % 2
                            blk, loc = p // 8, (p % 8) * 128
                            P.dma("sync", wgu[par].rearrange(
                                "p (jc x) -> p jc x", x=256)[:, :, 0:128],
                                wgu_all[blk, :, loc:loc + 128].rearrange(
                                    "(jc p) h -> p jc h", p=128), WG,
                                wait=[(CC, cc_gu), (PE, wgu_free[par])])
                            P.dma("sync", wgu[par].rearrange(
                                "p (jc x) -> p jc x", x=256)[:, :, 128:256],
                                wgu_all[blk, :, 1024 + loc:1024 + loc + 128].rearrange(
                                    "(jc p) h -> p jc h", p=128), WG)
                            wg_p = WG.val
                            for jc in range(NJ):
                                P.op("tensor", lambda e, par=par, jc=jc: e.matmul(
                                    gps[par], wgu[par][:, jc * 256:jc * 256 + 128],
                                    yfm[:, jc * 512:(jc + 1) * 512],
                                    start=(jc == 0), stop=(jc == NJ - 1),
                                    skip_group_check=True),
                                    wait=([(WG, wg_p), (DV, dv_y), (SC, gps_free[par])]
                                          if jc == 0 else []))
                            for jc in range(NJ):
                                P.op("tensor", lambda e, par=par, jc=jc: e.matmul(
                                    ups[par], wgu[par][:, jc * 256 + 128:jc * 256 + 256],
                                    yfm[:, jc * 512:(jc + 1) * 512],
                                    start=(jc == 0), stop=(jc == NJ - 1),
                                    skip_group_check=True),
                                    wait=([(DV, ups_free[par])] if jc == 0 else []),
                                    inc=[(PE, 1)] if jc == NJ - 1 else [])
                            wgu_free[par] = PE.val
                            pe_p = PE.val
                            P.op("scalar", lambda e, par=par: e.activation(
                                out=sg[par], in_=gps[par], func=AF.Silu),
                                wait=[(PE, pe_p)], inc=[(SC, 1)])
                            gps_free[par] = SC.val
                            sc_sg = SC.val
                            P.op("vector", lambda e, par=par, p=p: e.tensor_tensor(
                                out=hfm[:, p * 512:(p + 1) * 512], in0=sg[par],
                                in1=ups[par], op=ALU.mult),
                                wait=[(SC, sc_sg)], inc=[(DV, 1)])
                            ups_free[par] = DV.val
                            dv_h = DV.val
                            P.op("scalar", lambda e, par=par, p=p: e.activation(
                                out=h2[par], in_=hfm[:, p * 512:(p + 1) * 512],
                                func=AF.Square),
                                wait=[(DV, dv_h), (PE, PE.val)], inc=[(SC, 1)])
                            sc_h2 = SC.val
                            P.op("tensor", lambda e, par=par, p=p: e.matmul(
                                rps, ones_col_b, h2[par], start=(p == 0),
                                stop=(p == NPAIR - 1), skip_group_check=True),
                                wait=[(SC, sc_h2)],
                                inc=[(PE, 1)])
                        dv_hall = DV.val
                        pe_rs = PE.val
                        def vr(fn, wait=()):
                            P.op("vector", fn, wait=list(wait) + [(FE, FE.val)],
                                 inc=[(FE, 1)])
                        vr(lambda e: e.tensor_copy(out=sumr, in_=rps),
                           wait=[(PE, pe_rs)])
                    vr2 = lambda fn, wait=(): P.op(
                        "vector", fn, wait=list(wait) + [(FE, FE.val)], inc=[(FE, 1)])
                    A, B, C, Dv = (row[0:1, 0:512], row[0:1, 512:1024],
                                   row[0:1, 1024:1536], row[0:1, 1536:2048])
                    vr2(lambda e: e.tensor_scalar(out=A, in0=sumr,
                                                  scalar1=1.0 / FF, scalar2=RMS_EPS,
                                                  op0=ALU.mult, op1=ALU.add))
                    vr2(lambda e: e.reciprocal(out=B, in_=A))
                    P.op("scalar", lambda e: e.activation(
                        out=C, in_=B, func=AF.Sqrt),
                        wait=[(FE, FE.val)], inc=[(FE, 1)])
                    vr2(lambda e: e.tensor_tensor(out=B, in0=C, in1=C, op=ALU.mult))
                    vr2(lambda e: e.tensor_tensor(out=Dv, in0=B, in1=A, op=ALU.mult))
                    vr2(lambda e: e.tensor_scalar(out=B, in0=Dv,
                                                  scalar1=-0.5, scalar2=1.5,
                                                  op0=ALU.mult, op1=ALU.add))
                    vr2(lambda e: e.tensor_tensor(out=invr, in0=C, in1=B, op=ALU.mult))
                    fe_inv = FE.val
                    with ExitStack() as pjes:
                        pj = [psum_bank(pjes, nc, f"{tag}pj{hx}{i}",
                                        [128, 512], F32) for i in range(4)]
                        tps2 = psum_bank(pjes, nc, f"{tag}tps2{hx}", [128, 4], F32)
                        for t in range(4):
                            P.op("tensor", lambda e, t=t: e.matmul(
                                tps2[:, t:t + 1], invr[0:1, t * 128:(t + 1) * 128],
                                ones_row_f[0:1, 0:1], start=(t == 0), stop=(t == 3),
                                skip_group_check=True),
                                wait=[(FE, fe_inv)] if t == 0 else [],
                                inc=[(PE, 1)] if t == 3 else [])
                        pe_ti = PE.val
                        P.op("vector", lambda e: e.tensor_scalar(
                            out=hinv, in0=tps2, scalar1=0.5, scalar2=None,
                            op0=ALU.mult),
                            wait=[(PE, pe_ti), (FE, FE.val)], inc=[(FE, 1)])
                        fe_hinv = FE.val
                        wpb_free = [0, 0]
                        pj_free = [0, 0, 0, 0]
                        for dt in range(4):
                            for sci in range(8):
                                par = (dt * 8 + sci) % 2
                                P.dma("sync", wpb[par],
                                      wp_all[sci, :, dt * 512:(dt + 1) * 512].rearrange(
                                          "(hc p) d2 -> p hc d2", p=128), WG,
                                      wait=[(CC, cc_w), (PE, wpb_free[par])])
                                wg_pj = WG.val
                                for t in range(4):
                                    for jc in range(8):
                                        pp = sci * 8 + jc
                                        P.op("tensor", lambda e, t=t, par=par, jc=jc,
                                             pp=pp, sci=sci: e.matmul(
                                                 pj[t], hfm[:, pp * 512 + t * 128:
                                                            pp * 512 + (t + 1) * 128],
                                                 wpb[par][:, jc * 512:(jc + 1) * 512],
                                                 start=(sci == 0 and jc == 0),
                                                 stop=(sci == 7 and jc == 7),
                                                 skip_group_check=True),
                                            wait=(([(WG, wg_pj), (DV, dv_hall)]
                                                   if t == 0 else [])
                                                  + [(DV, pj_free[t])]
                                                  if jc == 0 and sci == 0 else
                                                  ([(WG, wg_pj)] if jc == 0 and
                                                   t == 0 else [])),
                                            inc=([(PE, 1)] if (t == 3 and jc == 7)
                                                 else []))
                                wpb_free[par] = PE.val
                            pe_dt = PE.val
                            for t in range(4):
                                P.op("vector", lambda e, dt=dt, t=t:
                                     e.scalar_tensor_tensor(
                                         out=x1h[:, t * 2048 + dt * 512:
                                                 t * 2048 + (dt + 1) * 512],
                                         in0=pj[t], scalar=hinv[:, t:t + 1],
                                         in1=xh[:, t * 2048 + dt * 512:
                                                t * 2048 + (dt + 1) * 512],
                                         op0=ALU.mult, op1=ALU.add),
                                     wait=[(PE, pe_dt), (FE, fe_hinv)], inc=[(DV, 1)])
                                pj_free[t] = DV.val
                    dv_x1 = DV.val
                    for t in range(4):
                        P.dma("gpsimd", dst_dram[(half * 4 + t) * 128:
                                                 (half * 4 + t + 1) * 128, :],
                              x1h[:, t * 2048:(t + 1) * 2048], AC,
                              wait=[(DV, dv_x1)])
                    dram_fence("gpsimd", dst_dram[half * 512:half * 512 + 128, 0:8],
                               AC)
                    if DEBUG and tag == "f1" and half == 0:
                        P.dma("gpsimd", dbg_yfm, yfm, AC, wait=[(DV, dv_x1)])
                        P.dma("gpsimd", dbg_hfm, hfm, AC)
                        P.dma("gpsimd", dbg_stat, stat, AC)
                        P.dma("gpsimd", dbg_hinv, hinv, AC)

        ffn_pass("f1", xin, xdram)

        if STAGE in ("x2", "full"):
            P.label = "qkv"
            with ExitStack() as qes:
                qsb = lambda n, s, d: qes.enter_context(nc.sbuf_tensor(n, s, d))[:]
                qps = lambda n, s, d: psum_bank(qes, nc, n, s, d)
                y2fm = [qsb(f"y2fm{i}", [128, 16 * 512], BF16) for i in range(2)]
                wat = qsb("wat", [128, 16 * 2048], BF16)
                k2t = qsb("k2t", [128, 512], BF16)
                knr = qsb("knr", [1, 64], F32)
                qnr = qsb("qnr", [1, 64], F32)
                bv_bc = qsb("bv_bc", [128, D], F32)
                mrow = qsb("mrow", [1, 512], F32)
                P.dma("sync", bv_bc, bv_bc_i, LD)
                kvps = [qps(f"kvps{i}", [128, 512], F32) for i in range(2)]
                nps = qps("nps", [1, 512], F32)
                mps = qps("mps", [128, NH], F32)
                yps = [qps(f"qyps{i}", [128, 128], BF16) for i in range(2)]
                yps_free = [0, 0]

                lnes2 = ExitStack()
                with lnes2:
                    lsb = lambda n, s, d: lnes2.enter_context(
                        nc.sbuf_tensor(n, s, d))[:]
                    xh = lsb("qxh", [128, 4 * 2048], F32)
                    yb = lsb("qyb", [128, 2048], BF16)
                    qsqs = lsb("qsqs", [128, 2048], F32)
                    for half in range(2):
                        bar = P.barrier_vals()
                        for t in range(4):
                            P.dma("scalar", xh[:, t * 2048:(t + 1) * 2048],
                                  xdram[(half * 4 + t) * 128:
                                        (half * 4 + t + 1) * 128, :],
                                  XS, wait=bar if t == 0 else [])
                        ln_half(xh, lambda t, jc, h=half: y2fm[h][:, jc * 512 +
                                                              t * 128:jc * 512 +
                                                              (t + 1) * 128],
                                qsqs, yb, yps, yps_free)
                        if DEBUG and half == 1:
                            P.dma("gpsimd", dbg_qst, stat, AC,
                                  wait=[(DV, DV.val), (SC, SC.val)])
                            P.dma("gpsimd", dbg_yb, yb, AC)
                            P.dma("gpsimd", dbg_xh, xh[:, 0:2048], AC)
                dv_y2 = DV.val
                kqpan = qsb("kqpan", [128, 32 * 512], BF16)
                vpan = qsb("vpan", [128, 32 * 512], BF16)
                if DEBUG:
                    P.dma("gpsimd", dbg_y2[:, 0:16 * 512], y2fm[0], AC,
                          wait=[(DV, dv_y2)])
                    P.dma("gpsimd", dbg_y2[:, 16 * 512:], y2fm[1], AC)

                def load_w(widx, wv):
                    for r in range(8):
                        P.dma("sync", wat.rearrange("p (jc i) -> p jc i", i=2048)
                              [:, :, 256 * r:256 * (r + 1)],
                              wa_all[r, widx, :, :].rearrange(
                                  "(jc p) i -> p jc i", p=128),
                              WG, wait=wv if r == 0 else [])
                    return WG.val

                kvps_free = [[], []]
                nrm = [0]
                cc_k = [0]

                for widx, kind in [(1, "k"), (2, "v"), (0, "q")]:
                    wg_w = load_w(widx, [(CC, cc_w), (PE, PE.val)])
                    if kind in ("k", "q"):
                        for ic in range(16):
                            for th in range(2):
                                par = (ic * 2 + th) % 2
                                sl = kqpan[:, (ic * 2 + th) * 512:
                                           (ic * 2 + th + 1) * 512]
                                for jc in range(16):
                                    P.op("tensor", lambda e, par=par, ic=ic, th=th,
                                         jc=jc: e.matmul(
                                             kvps[par], wat[:, jc * 2048 + ic * 128:
                                                            jc * 2048 + (ic + 1) * 128],
                                             y2fm[th][:, jc * 512:(jc + 1) * 512],
                                             start=(jc == 0), stop=(jc == 15),
                                             skip_group_check=True),
                                        wait=([(WG, wg_w), (DV, dv_y2)]
                                              + kvps_free[par] if jc == 0 else []),
                                        inc=[(PE, 1)] if jc == 15 else [])
                                pe_kv = PE.val
                                bcol = (16 if kind == "k" else 0) + ic
                                P.op("scalar", lambda e, par=par, bcol=bcol,
                                     sl=sl: e.activation(
                                         out=sl, in_=kvps[par], func=AF.Identity,
                                         bias=bqk_tm[:, bcol:bcol + 1]),
                                    wait=[(PE, pe_kv)]
                                    + ([(CC, cc_k[0])] if kind == "q"
                                       and ic == 0 and th == 0 else []),
                                    inc=[(SC, 1)])
                                kvps_free[par] = [(SC, SC.val)]
                                sc_kv = SC.val
                                kq_dst = k_in if kind == "k" else qdram
                                P.dma("sync", kq_dst[ic * 128:(ic + 1) * 128,
                                                     th * 512:(th + 1) * 512],
                                      sl, AC, wait=[(SC, sc_kv)])
                                P.op("scalar", lambda e, sl=sl: e.activation(
                                    out=k2t, in_=sl, func=AF.Square),
                                    wait=[(SC, sc_kv)], inc=[(SC, 1)])
                                sc_k2 = SC.val
                                P.op("tensor", lambda e: e.matmul(
                                    nps, ones_col_b, k2t, start=True, stop=True,
                                    skip_group_check=True),
                                    wait=[(SC, sc_k2), (FE, nrm[0])], inc=[(PE, 1)])
                                pe_n = PE.val
                                tgt = knr if kind == "k" else qnr
                                P.op("vector", lambda e, t2=tgt, ic=ic, th=th:
                                     e.tensor_reduce(
                                         out=t2[0:1, ic * 2 + th:ic * 2 + th + 1],
                                         in_=nps, axis=AX.X, op=ALU.max),
                                     wait=[(PE, pe_n), (FE, FE.val)], inc=[(FE, 1)])
                                nrm[0] = FE.val
                    else:
                        for t in range(8):
                            hh, tt = t // 4, t % 4
                            for iv in range(4):
                                par = (t * 4 + iv) % 2
                                vsl = vpan[:, (t * 4 + iv) * 512:
                                           (t * 4 + iv + 1) * 512]
                                for jc in range(16):
                                    P.op("tensor", lambda e, par=par, hh=hh, tt=tt,
                                         iv=iv, jc=jc: e.matmul(
                                             kvps[par],
                                             y2fm[hh][:, jc * 512 + tt * 128:
                                                      jc * 512 + (tt + 1) * 128],
                                             wat[:, jc * 2048 + iv * 512:
                                                 jc * 2048 + (iv + 1) * 512],
                                             start=(jc == 0), stop=(jc == 15),
                                             skip_group_check=True),
                                        wait=([(WG, wg_w)] + kvps_free[par]
                                              if jc == 0 else []),
                                        inc=[(PE, 1)] if jc == 15 else [])
                                pe_v = PE.val
                                P.op("vector", lambda e, par=par, iv=iv,
                                     vsl=vsl: e.tensor_tensor(
                                         out=vsl, in0=kvps[par],
                                         in1=bv_bc[:, iv * 512:(iv + 1) * 512],
                                         op=ALU.add),
                                     wait=[(PE, pe_v), (FE, FE.val)],
                                     inc=[(DV, 1)])
                                kvps_free[par] = [(DV, DV.val)]
                                dv_v = DV.val
                                P.dma("sync", v_in[t * 128:(t + 1) * 128,
                                                   iv * 512:(iv + 1) * 512],
                                      vsl, AC, wait=[(DV, dv_v)])
                    if kind == "k":
                        P.op("vector", lambda e: e.tensor_reduce(
                            out=mrow[0:1, 224:240],
                            in_=knr[0:1, 0:32].rearrange("x (h t2) -> x h t2", t2=2),
                            axis=AX.X, op=ALU.max),
                            wait=[(FE, nrm[0]), (FE, FE.val)], inc=[(FE, 1)])
                        fe_kn = FE.val
                        P.dma("gpsimd", kn_in, mrow[0:1, 224:240], AC,
                              wait=[(FE, fe_kn)])
                        dram_fence("sync", k_in[0:128, 0:8], AC)
                        ac_k = AC.val
                        P.op("gpsimd", lambda e: e.collective_compute(
                            "AllGather", ALU.bypass, replica_groups=[list(range(8))],
                            ins=[k_in.opt()], outs=[k_all.opt()]),
                            wait=[(AC, ac_k)], inc=[(CC, 1)])
                        P.op("gpsimd", lambda e: e.collective_compute(
                            "AllGather", ALU.bypass, replica_groups=[list(range(8))],
                            ins=[kn_in.opt()], outs=[kn_all.opt()]),
                            inc=[(CC, 1)])
                        cc_k[0] = CC.val
                    if kind == "v":
                        dram_fence("sync", v_in[0:128, 0:8], AC)
                        ac_v = AC.val
                        P.op("gpsimd", lambda e: e.collective_compute(
                            "AllGather", ALU.bypass, replica_groups=[list(range(8))],
                            ins=[v_in.opt()], outs=[v_all.opt()]),
                            wait=[(AC, ac_v)], inc=[(CC, 1)])
                dram_fence("sync", qdram[0:128, 0:8], AC)
                cc_kv = CC.val
                ac_qkv = AC.val

                def vm(fn, wait=()):
                    P.op("vector", fn, wait=list(wait) + [(FE, FE.val)], inc=[(FE, 1)])
                P.dma("scalar", mrow[0:1, 0:128].rearrange(
                    "x (h r) -> x h r", r=8),
                    kn_all.rearrange("r x h -> x h r"), XS, wait=[(CC, cc_kv)],
                    allow_slow_non_contiguous=True)
                xs_kn = XS.val
                vm(lambda e: e.tensor_reduce(
                    out=mrow[0:1, 128:144],
                    in_=mrow[0:1, 0:128].rearrange("x (h r) -> x h r", r=8),
                    axis=AX.X, op=ALU.max), wait=[(XS, xs_kn)])
                vm(lambda e: e.tensor_reduce(
                    out=mrow[0:1, 144:160],
                    in_=qnr[0:1, 0:32].rearrange("x (h t2) -> x h t2", t2=2),
                    axis=AX.X, op=ALU.max), wait=[(FE, nrm[0])])
                vm(lambda e: e.tensor_tensor(
                    out=mrow[0:1, 160:176], in0=mrow[0:1, 128:144],
                    in1=mrow[0:1, 144:160], op=ALU.mult))
                vm(lambda e: e.tensor_scalar(
                    out=mrow[0:1, 176:192], in0=mrow[0:1, 160:176],
                    scalar1=1.0 / HD, scalar2=None, op0=ALU.mult))
                P.op("scalar", lambda e: e.activation(
                    out=mrow[0:1, 192:208], in_=mrow[0:1, 176:192], func=AF.Sqrt),
                    wait=[(FE, FE.val)], inc=[(FE, 1)])
                vm(lambda e: e.tensor_scalar(
                    out=mrow[0:1, 208:224], in0=mrow[0:1, 192:208],
                    scalar1=-1.0, scalar2=None, op0=ALU.mult))
                fe_m = FE.val
                P.op("tensor", lambda e: e.matmul(
                    mps, ones_row_f, mrow[0:1, 208:224], start=True, stop=True,
                    skip_group_check=True), wait=[(FE, fe_m)], inc=[(PE, 1)])
                pe_m = PE.val
                P.op("vector", lambda e: e.tensor_copy(out=negm, in_=mps),
                     wait=[(PE, pe_m), (FE, FE.val)], inc=[(FE, 1)])
                fe_negm = FE.val
                if DEBUG:
                    P.dma("gpsimd", dbg_negm, negm, AC, wait=[(FE, fe_negm)])
                    P.dma("gpsimd", dbg_mrow, mrow, AC)
                    P.dma("gpsimd", dbg_qn, qnr, AC)
                    P.dma("gpsimd", dbg_q, qdram, AC, wait=[(AC, ac_qkv)])
                    P.dma("gpsimd", dbg_k, k_all[0:2], AC, wait=[(CC, cc_kv)])
                    P.dma("gpsimd", dbg_v, v_all[:, 0:128, :], AC)

            P.label = "attn"
            avsb_es = ExitStack()
            av_sb = [avsb_es.enter_context(
                nc.sbuf_tensor(f"avsb{i}", [128, NH * 512], F32))[:]
                for i in range(2)]
            with ExitStack() as aes:
                asb = lambda n, s, d: aes.enter_context(nc.sbuf_tensor(n, s, d))[:]
                apsf = lambda n, s, d: psum_bank(aes, nc, n, s, d)
                ks = [asb(f"ks{i}", [128, 16 * 512], BF16) for i in range(2)]
                vs = [asb(f"vs{i}", [128, 4 * 2048], BF16) for i in range(2)]
                qs = [asb(f"qs{i}", [128, 16 * 512], BF16) for i in range(2)]
                pb = [asb(f"pb{i}", [128, 512], BF16) for i in range(4)]
                pm = [asb(f"pm{i}", [128, 512], BF16) for i in range(4)]
                fencb = asb("fencb", [128, 8], BF16)
                cms = asb("cms", [128, 4 * 512], BF16)
                P.dma("sync", cms, cmi, LD)
                ldc = LD.val
                sut = [asb(f"sut{i}", [1, 512], F32) for i in range(2)]
                sps = [apsf(f"sps{i}", [128, 512], F32) for i in range(2)]
                avp = [apsf(f"avp{i}", [128, 512], F32) for i in range(2)]
                sup = [apsf(f"sup{i}", [128, 512], F32) for i in range(2)]

                kq_free = [0, 0]
                v_free = [0, 0]
                sps_free = [0, 0]
                avp_free = [0, 0]
                sup_free = [0, 0]
                sut_free = [0, 0]
                pb_free = [0, 0, 0, 0]
                pm_free = [0, 0, 0, 0]

                KVK = P.sem("KVK")
                KVV = P.sem("KVV")
                KVQ = P.sem("KVQ")

                def dyn_load(eng, t, col, buf_ap, ap_dims, sem):
                    sh = sem.h

                    def fn(e, t=t, col=col, buf_ap=buf_ap, ap_dims=ap_dims, sh=sh):
                        with e.register(f"r{eng}{t}") as r:
                            e.reg_load(r, tabs[0:1, t * 8 + col:t * 8 + col + 1])
                            roff = e.snap(r, True, min_val=0,
                                          max_val=8 * 2048 * 1024)
                            src_t = k_all.tensor if col == 0 else (
                                v_all.tensor if col == 1 else qdram.tensor)
                            e.dma_start(out=buf_ap,
                                        in_=RawAP(src_t, roff, ap_dims)
                                        ).then_inc(sh, 16)
                            return e.dma_start(
                                out=fencb,
                                in_=RawAP(src_t, roff, [[ap_dims[0][0], 128],
                                                        [1, 8]]))
                    P.op(eng, fn, wait=[(CC, cc_kv), (AC, ac_qkv),
                                        (PE, kq_free[t % 2] if col != 1
                                         else v_free[t % 2])],
                         inc=[(sem, 16)], ghost_inc=[(sem, 16)])
                    return sem.val

                K_AP = [[1024, 128], [128 * 1024, 16], [1, 512]]
                V_AP = [[2048, 128], [128 * 2048, 4], [1, 2048]]
                Q_AP = [[1024, 128], [128 * 1024, 16], [1, 512]]

                def load_slot(t):
                    par = t % 2
                    a = dyn_load("sync", t, 0,
                                 ks[par].rearrange("p (ic t2) -> p ic t2", t2=512),
                                 K_AP, KVK)
                    b = dyn_load("gpsimd", t, 1,
                                 vs[par].rearrange("p (kc d2) -> p kc d2", d2=2048),
                                 V_AP, KVV)
                    c = dyn_load("scalar", t, 2,
                                 qs[par].rearrange("p (ic t2) -> p ic t2", t2=512),
                                 Q_AP, KVQ)
                    return [(KVK, a), (KVV, b), (KVQ, c)]

                kv_ld = [None] * (NSLOT + 1)
                kv_ld[0] = load_slot(0)
                carry = [None, None]
                for t in range(NSLOT):
                    par = t % 2
                    if t + 1 < NSLOT:
                        kv_ld[t + 1] = load_slot(t + 1)
                    diag = t < 2
                    for h in range(NH):
                        hp = h % 2

                        def a_score(kc, par=par, h=h, kvw=None):
                            sp = kc % 2
                            P.op("tensor", lambda e, par=par, h=h, kc=kc, sp=sp:
                                 e.matmul(
                                     sps[sp], ks[par][:, h * 512 + kc * 128:
                                                      h * 512 + (kc + 1) * 128],
                                     qs[par][:, h * 512:(h + 1) * 512],
                                     start=True, stop=True,
                                     skip_group_check=True),
                                 wait=(kvw if kvw is not None else kv_ld[t])
                                 + [(SC, sps_free[sp]), (FE, fe_negm)],
                                 inc=[(PE, 1)])
                            return PE.val

                        def a_exp(kc, pe_s, h=h):
                            sp = kc % 2
                            P.op("scalar", lambda e, sp=sp, kc=kc, h=h:
                                 e.activation(
                                     out=pb[kc], in_=sps[sp], func=AF.Exp,
                                     bias=negm[:, h:h + 1], scale=SM_SCALE),
                                 wait=[(PE, pe_s), (PE, pb_free[kc])],
                                 inc=[(SC, 1)])
                            sps_free[sp] = SC.val
                            sc_e = SC.val
                            if diag:
                                P.op("vector", lambda e, kc=kc: e.tensor_tensor(
                                    out=pm[kc], in0=pb[kc],
                                    in1=cms[:, kc * 512:(kc + 1) * 512],
                                    op=ALU.mult),
                                    wait=[(SC, sc_e), (LD, ldc),
                                          (PE, pm_free[kc])],
                                    inc=[(DV, 1)])
                                return pm, [(DV, DV.val)]
                            return pb, [(SC, sc_e)]

                        def a_avsu(kc, src2, wv, par=par, h=h, hp=hp):
                            P.op("tensor", lambda e, par=par, h=h, kc=kc,
                                 s2=src2, hp=hp: e.matmul(
                                     avp[hp], vs[par][:, kc * 2048 + h * 128:
                                                      kc * 2048 + (h + 1) * 128],
                                     s2[kc], start=(kc == 0), stop=(kc == 3),
                                     skip_group_check=True),
                                 wait=wv + ([(DV, avp_free[hp])] + kv_ld[t]
                                            if kc == 0 else []))
                            P.op("tensor", lambda e, kc=kc, s2=src2, hp=hp:
                                 e.matmul(
                                     sup[hp], cbs[:, 129:257], s2[kc],
                                     start=(kc == 0), stop=(kc == 3),
                                     skip_group_check=True),
                                 wait=([(DV, sup_free[hp])] if kc == 0 else []),
                                 inc=[(PE, 1)])
                            pb_free[kc] = PE.val
                            pm_free[kc] = PE.val

                        pe_s = [None] * 4
                        if carry[0] is not None:
                            pe_s[0], pe_s[1] = carry
                        else:
                            pe_s[0] = a_score(0)
                            pe_s[1] = a_score(1)
                        nxt = [None, None]
                        for kc in range(4):
                            src2, wv = a_exp(kc, pe_s[kc])
                            if kc + 2 < 4:
                                pe_s[kc + 2] = a_score(kc + 2)
                            elif h + 1 < NH:
                                nxt[kc - 2] = a_score(kc - 2, h=h + 1)
                            elif t + 1 < NSLOT:
                                nxt[kc - 2] = a_score(
                                    kc - 2, par=(t + 1) % 2, h=0,
                                    kvw=kv_ld[t + 1])
                            a_avsu(kc, src2, wv)
                        carry = nxt
                        pe_av = PE.val
                        for qb2 in range(2):
                            if t == 0:
                                P.op("vector", lambda e, hp=hp, qb2=qb2, t=t, h=h:
                                     e.tensor_scalar(
                                         out=av_sb[qb2][:, h * 512:(h + 1) * 512],
                                         in0=avp[hp],
                                         scalar1=wts[:, qb2 * NSLOT + t:
                                                     qb2 * NSLOT + t + 1],
                                         scalar2=None, op0=ALU.mult),
                                     wait=([(PE, pe_av), (LD, ldv)]
                                           if qb2 == 0 else []),
                                     inc=[(DV, 1)])
                            else:
                                P.op("vector", lambda e, hp=hp, qb2=qb2, t=t, h=h:
                                     e.scalar_tensor_tensor(
                                         out=av_sb[qb2][:, h * 512:(h + 1) * 512],
                                         in0=avp[hp],
                                         scalar=wts[:, qb2 * NSLOT + t:
                                                    qb2 * NSLOT + t + 1],
                                         in1=av_sb[qb2][:, h * 512:(h + 1) * 512],
                                         op0=ALU.mult, op1=ALU.add),
                                     wait=([(PE, pe_av), (LD, ldv)]
                                           if qb2 == 0 else []),
                                     inc=[(DV, 1)])
                        avp_free[hp] = DV.val
                        P.op("vector", lambda e, hp=hp: e.tensor_copy(
                            out=sut[hp], in_=sup[hp][0:1, :]),
                            wait=[(AC, sut_free[hp])], inc=[(DV, 1)])
                        sup_free[hp] = DV.val
                        dv_su = DV.val
                        P.dma("gpsimd", sudram[t, h, :].unsqueeze(0), sut[hp], AC,
                              wait=[(DV, dv_su)])
                        dram_fence("gpsimd", sudram[t, h, 0:8].unsqueeze(0), AC)
                        sut_free[hp] = AC.val
                    kq_free[par] = PE.val
                    v_free[par] = PE.val
                if DEBUG:
                    P.dma("gpsimd", dbg_av[0], av_sb[0], AC,
                          wait=[(DV, DV.val)])
                    P.dma("gpsimd", dbg_av[1], av_sb[1], AC)
                    P.dma("gpsimd", dbg_wts, wts, AC)
                    P.dma("gpsimd", dbg_tab, tabs, AC)

            P.label = "oproj"
            dram_fence("gpsimd", sudram[NSLOT - 1, NH - 1, 0:8].unsqueeze(0), AC)
            with ExitStack() as oes:
                osb = lambda n, s, d: oes.enter_context(nc.sbuf_tensor(n, s, d))[:]
                opsf = lambda n, s, d: psum_bank(oes, nc, n, s, d)
                wos = osb("wos", [128, 16 * 2048], BF16)
                sus2 = [osb(f"sus{i}", [NSLOT, 512], F32) for i in range(2)]
                avb = osb("avb", [128, 16 * 512], BF16)
                pout = osb("pout", [128, 8 * 2048], BF16)
                bo_bc = osb("bo_bc", [128, D], F32)
                P.dma("sync", bo_bc, bo_bc_i, LD)
                ops = [opsf(f"ops{i}", [128, 512], F32) for i in range(2)]
                ibp = opsf("ibp", [128, 512], F32)
                sps2 = opsf("sps2", [1, 512], F32)
                jnk = opsf("jnk", [128, 512], F32)

                def pe_pad():
                    P.op("tensor", lambda e: e.matmul(
                        jnk, ident, cbs[:, 0:512], start=True, stop=True,
                        skip_group_check=True), inc=[(PE, 1)])
                    return PE.val

                ac_att = AC.val
                bar_o = P.barrier_vals()
                for r in range(8):
                    P.dma("sync", wos.rearrange("p (jc i) -> p jc i", i=2048)
                          [:, :, 256 * r:256 * (r + 1)],
                          wa_all[r, 3, :, :].rearrange("(jc p) i -> p jc i", p=128),
                          WG, wait=bar_o if r == 0 else [])
                wg_o = WG.val
                if DEBUG:
                    P.dma("gpsimd", dbg_su, sudram, AC, wait=[(AC, ac_att)])
                sus_xs = [0, 0]
                sus_pe = [0, 0]

                def sus_load(pos, h):
                    sl = pos % 2
                    P.dma("scalar", sus2[sl], sudram[:, h, :], XS,
                          wait=[(AC, ac_att), (PE, sus_pe[sl])])
                    sus_xs[sl] = XS.val

                avb_free = 0
                ops_free = [0, 0]
                dbg_gate = []
                for qb in range(2):
                    dv_att = DV.val
                    ibp_free = 0
                    row_free = 0
                    sps2_free = 0
                    hseq = ([0] if qb == 0 else []) + list(range(NH))
                    sus_load(0, hseq[0])
                    for pos, h in enumerate(hseq):
                        sl = pos % 2
                        if pos + 1 < len(hseq):
                            sus_load(pos + 1, hseq[pos + 1])
                        P.op("tensor", lambda e, qb=qb, sl=sl: e.matmul(
                            sps2, wtv[:, qb:qb + 1], sus2[sl],
                            start=True, stop=True, skip_group_check=True),
                            wait=[(XS, sus_xs[sl]), (LD, LD.val),
                                  (FE, sps2_free)],
                            inc=[(PE, 1)])
                        sus_pe[sl] = PE.val
                        pe_su = pe_pad()
                        P.op("vector", lambda e: e.reciprocal(
                            out=row[0:1, 0:512], in_=sps2[0:1, :]),
                            wait=[(PE, pe_su), (FE, FE.val), (PE, row_free)],
                            inc=[(FE, 1)])
                        fe_i = FE.val
                        sps2_free = fe_i
                        P.op("tensor", lambda e: e.matmul(
                            ibp, ones_row_f, row[0:1, 0:512], start=True, stop=True,
                            skip_group_check=True),
                            wait=[(FE, fe_i), (DV, ibp_free)], inc=[(PE, 1)])
                        pe_i = pe_pad()
                        row_free = pe_i
                        P.op("vector", lambda e, qb=qb, h=h: e.tensor_tensor(
                            out=avb[:, h * 512:(h + 1) * 512],
                            in0=av_sb[qb][:, h * 512:(h + 1) * 512], in1=ibp,
                            op=ALU.mult),
                            wait=[(PE, pe_i), (DV, dv_att), (PE, avb_free)]
                            + dbg_gate,
                            inc=[(DV, 1)])
                        ibp_free = DV.val
                    dv_avb = DV.val
                    if DEBUG and qb == 0:
                        P.dma("gpsimd", dbg_avb, avb, AC, wait=[(DV, dv_avb)])
                        dbg_gate = [(AC, AC.val)]
                    for t in range(4):
                        gt = qb * 4 + t
                        psl = pout[:, gt * 2048:(gt + 1) * 2048]
                        for i in range(4):
                            par = i % 2
                            for h in range(NH):
                                P.op("tensor", lambda e, t=t, i=i, h=h, par=par:
                                     e.matmul(
                                         ops[par], avb[:, h * 512 + t * 128:
                                                       h * 512 + (t + 1) * 128],
                                         wos[:, h * 2048 + i * 512:
                                             h * 2048 + (i + 1) * 512],
                                         start=(h == 0), stop=(h == 15),
                                         skip_group_check=True),
                                     wait=([(WG, wg_o), (DV, dv_avb),
                                            (DV, ops_free[par])] if h == 0 else []),
                                     inc=[(PE, 1)] if h == 15 else [])
                            pe_o = PE.val
                            P.op("vector", lambda e, i=i, par=par, psl=psl:
                                 e.scalar_tensor_tensor(
                                     out=psl[:, i * 512:(i + 1) * 512],
                                     in0=ops[par], scalar=1.0,
                                     in1=bo_bc[:, i * 512:(i + 1) * 512],
                                     op0=ALU.mult, op1=ALU.add),
                                 wait=[(PE, pe_o), (LD, LD.val)],
                                 inc=[(DV, 1)])
                            ops_free[par] = DV.val
                        dv_p = DV.val
                        P.dma("sync", podram[gt * 128:(gt + 1) * 128, :], psl, AC,
                              wait=[(DV, dv_p)])
                    avb_free = PE.val
                dram_fence("sync", podram[0:128, 0:8], AC)
            avsb_es.close()

        if STAGE == "full":
            ffn_pass("f2", xdram, out, proj_dram=podram)
        else:
            bar = P.barrier_vals()
            with ExitStack() as x2es:
                x2b = x2es.enter_context(
                    nc.sbuf_tensor("x2asm", [128, 2048], F32))[:]
                p2b = x2es.enter_context(
                    nc.sbuf_tensor("p2asm", [128, 2048], BF16))[:]
                fe_a = FE.val
                for i in range(8):
                    P.dma("scalar", x2b, xdram[i * 128:(i + 1) * 128, :], XS,
                          wait=(bar if i == 0 else []) + [(AC, AC.val)])
                    P.dma("scalar", p2b, podram[i * 128:(i + 1) * 128, :], XS)
                    xs_a = XS.val
                    P.op("vector", lambda e: e.tensor_tensor(
                        out=x2b, in0=x2b, in1=p2b, op=ALU.add),
                        wait=[(XS, xs_a), (FE, fe_a)], inc=[(FE, 1)])
                    fe_a = FE.val
                    P.dma("gpsimd", out[i * 128:(i + 1) * 128, :], x2b, AC,
                          wait=[(FE, fe_a)])
        dram_fence("gpsimd", out[0:128, 0:8], AC)
        P.op("gpsimd", None, wait=P.barrier_vals())

        P.check_schedule()
        blk = es.enter_context(nc.Block())
        P.replay(blk)
    return nc


_CACHE = {}


def kernel(**inputs):
    x = np.asarray(inputs["L0_attn_hidden_states"]).reshape(S, D)
    wgu = np.asarray(inputs["L0_ffn_w_gate_up"])
    wp = np.asarray(inputs["L0_ffn_w_proj"])
    cbn, cfn, cmn = host_constants()
    bq = np.asarray(inputs["L0_attn_bq"]).astype(np.float32)
    bk = np.asarray(inputs["L0_attn_bk"]).astype(np.float32)
    bv = np.asarray(inputs["L0_attn_bv"]).astype(np.float32)
    bo = np.asarray(inputs["L0_attn_bo"]).astype(np.float32)
    bqk_tm = np.concatenate([bq.reshape(16, 128).T, bk.reshape(16, 128).T], 1)
    in_maps = []
    for c in range(NCORE):
        rows = core_rows(c)
        gate = wgu[1024 * c:1024 * (c + 1), :]
        up = wgu[8192 + 1024 * c:8192 + 1024 * (c + 1), :]
        m = {
            "xin": np.ascontiguousarray(x[rows]),
            "wgu_s": np.ascontiguousarray(np.concatenate([gate, up], 0)),
            "wp_s": np.ascontiguousarray(wp[:, 1024 * c:1024 * (c + 1)]),
            "wa_s": np.ascontiguousarray(np.stack(
                [np.asarray(inputs[f"L0_attn_W{w}"])[256 * c:256 * (c + 1), :]
                 for w in "qkvo"])),
            "bqk_tm_i": np.ascontiguousarray(bqk_tm.astype(np.float32)),
            "bv_bc_i": np.ascontiguousarray(np.tile(bv[None, :], (128, 1))),
            "bo_bc_i": np.ascontiguousarray(np.tile(bo[None, :], (128, 1))),
            "cbi": cbn, "cfi": cfn, "cmi": cmn,
            "tabi": slot_table(c)[0], "wti": slot_table(c)[1],
            "wtvi": np.ascontiguousarray(
                slot_table(c)[1][0, :].reshape(2, NSLOT).T.astype(np.float32)),
        }
        in_maps.append(m)
    if "nc" not in _CACHE:
        _CACHE["nc"] = build_program()
    res = run_bass_kernel_spmd(_CACHE["nc"], in_maps, core_ids=list(range(NCORE)))
    full = np.zeros((S, D), np.float32)
    for c in range(NCORE):
        full[core_rows(c)] = res.results[c]["out"]
    return full.reshape(1, S, D)
